# version 04
# baseline (speedup 1.0000x reference)
import contextlib
import numpy as np
import concourse.bass as bass
import concourse.mybir as mybir
from concourse.bass_utils import run_bass_kernel_spmd

F32 = mybir.dt.float32
BF16 = mybir.dt.bfloat16
I32 = mybir.dt.int32
AF = mybir.ActivationFunctionType
ALU = mybir.AluOpType
AX = mybir.AxisListType
PI = float(np.pi)

S = 2048
D = 1024
NCH = 8
FF = 4096
NFB = 32
NSEQ = 4
NB = S // 512
EPS = 1e-5
H0 = 16
H1 = 8
NE = 8

_lin = np.linspace(np.log(1.0 / 512.0), np.log(1.0 / 32.0), H1)
GAMMAS = (1.0 - np.exp(_lin)).astype(np.float64)
LGAM = [float(np.log(g)) for g in GAMMAS]

COMPUTE = ("pe", "act", "dve", "pool")
DMACH = ["cx", "cs0", "cs1", "cw0", "cw1", "cw2", "cw3", "cw4", "cw5", "cv", "ct"]
_uid = [0]


_buf_seq = [0]
_buf_reg = []


class Buf:
    __slots__ = ("w", "r", "fresh", "psum", "small", "bid")

    def __init__(self, psum=False, small=False):
        self.w = None
        self.r = {}
        self.fresh = True
        self.psum = psum
        self.small = small
        self.bid = _buf_seq[0]
        _buf_seq[0] += 1
        _buf_reg.append(self)


class K:
    ENAME2ENG = {"pe": "tensor", "act": "scalar", "dve": "vector",
                 "pool": "gpsimd", "sp": "sync"}

    def __init__(self, nc, es):
        self.nc = nc
        self.eng = {"pe": nc.tensor, "act": nc.scalar, "dve": nc.vector,
                    "pool": nc.gpsimd, "sp": nc.sync}
        self.h = {}
        self.cnt = {}
        for name in list(COMPUTE) + DMACH:
            self.h[name] = es.enter_context(nc.semaphore(name))
            self.cnt[name] = 0
        self.waited = {}
        self.legacy = {}
        self.epoch = None
        self.cnt0 = None
        self.delta = None
        self.lwaited = {}
        self.bregs = {}
        self.tmpreg = {}
        self.wait_pairs = set()
        self.chan_buf = {n: Buf() for n in DMACH}

    def begin_loop(self, stats=None):
        self.cnt0 = dict(self.cnt)
        self.epoch = 0
        self.lwaited = {}
        if stats is not None:
            self.delta = stats["delta"]
            for sem in self.cnt:
                if self.cnt0[sem] > 0 and self.delta.get(sem, 0) > 0:
                    self.eng["sp"].wait_ge(self.h[sem], self.cnt0[sem])
            for sem in COMPUTE:
                if self.cnt0[sem] > 0:
                    self.eng["sp"].wait_ge(self.h[sem], self.cnt0[sem])
            for sem, d in self.delta.items():
                if d > 0:
                    self.eng["sp"].sem_inc(self.h[sem], d)
            for (ename, sem) in sorted(stats["wait_pairs"]):
                if self.delta.get(sem, 0) == 0:
                    continue
                r = self.eng[ename].alloc_register(f"B_{ename}_{sem}")
                self.eng[ename].reg_mov(r, self.cnt0[sem] + self.delta[sem])
                self.bregs[(ename, sem)] = r
            for ename in ("pe", "act", "dve", "pool", "sp"):
                self.tmpreg[ename] = self.eng[ename].alloc_register(f"T_{ename}")
            by_id = stats["bufs_by_id"]
            for b in _buf_reg:
                st = by_id.get(b.bid)
                if st is None:
                    continue
                w, r_ = st
                if w is not None:
                    b.w = w
                if r_:
                    b.r = dict(r_)

    def end_body(self):
        for (ename, sem), r in self.bregs.items():
            self.eng[ename].reg_alu(r, r, self.delta[sem], mybir.AluOpType.add)

    def collect_stats(self, all_bufs):
        delta = {sem: self.cnt[sem] - self.cnt0[sem] for sem in self.cnt}
        bufs_by_id = {}
        def shift(tok):
            sem, cnt, ep = tok
            if ep is None:
                return tok
            return (sem, cnt, ep - 1)
        for b in all_bufs:
            w = shift(b.w) if b.w is not None else None
            r_ = {sem: (cnt, (ep - 1) if ep is not None else None)
                  for sem, (cnt, ep) in b.r.items()}
            bufs_by_id[b.bid] = (w, r_)
        return {"delta": delta, "wait_pairs": set(self.wait_pairs),
                "bufs_by_id": bufs_by_id}

    def _wait_tok(self, ename, tok, force=False):
        sem, cnt, ep = tok
        if cnt <= 0:
            return
        if sem == ename and sem in COMPUTE and not force:
            return
        key = (ename, sem)
        if self.epoch is None or ep is None:
            if self.waited.get(key, 0) >= cnt:
                return
            self.eng[ename].wait_ge(self.h[sem], cnt)
            self.waited[key] = cnt
            return
        self.wait_pairs.add(key)
        x = (cnt - self.cnt0[sem]) + self.delta_of(sem) * ep
        if self.lwaited.get(key, -10**9) >= x:
            return
        self.lwaited[key] = x
        if self.delta is None:
            return
        breg = self.bregs.get(key)
        if breg is None:
            return
        t = self.tmpreg[ename]
        self.eng[ename].reg_alu(t, breg, x, mybir.AluOpType.add)
        self.eng[ename].wait_ge(self.h[sem], t)

    def delta_of(self, sem):
        if self.delta is not None:
            return self.delta.get(sem, 0)
        return 0

    def _wait(self, ename, sem, cnt, force=False):
        self._wait_tok(ename, (sem, cnt, None), force)

    def _pre(self, ename, reads, writes, sreads=()):
        for b in sreads:
            if b.w is not None:
                self._wait_tok(ename, b.w, force=True)
        for b in reads:
            if b.w is not None:
                self._wait_tok(ename, b.w, force=b.small)
            if b.psum:
                for sem, (cnt, ep) in b.r.items():
                    if sem != ename:
                        self._wait_tok(ename, (sem, cnt, ep))
        for b in writes:
            if b.fresh:
                for sem, (cnt, ep) in self.legacy.items():
                    self._wait_tok(ename, (sem, cnt, ep))
                b.fresh = False
            if b.w is not None:
                self._wait_tok(ename, b.w, force=b.small)
            for sem, (cnt, ep) in b.r.items():
                self._wait_tok(ename, (sem, cnt, ep), force=b.small)

    def _post(self, tok, reads, writes):
        sem, cnt, ep = tok
        for b in reads:
            cur = b.r.get(sem)
            if cur is None or (cur[1] or 0, cur[0]) < ((ep or 0), cnt):
                b.r[sem] = (cnt, ep)
        for b in writes:
            b.w = tok
            b.r = {}

    def emit(self, ename, fn, reads=(), writes=(), sreads=(), low_prec=False):
        self._pre(ename, reads, writes, sreads)
        if low_prec:
            with self.nc.allow_low_precision(reason="rounded normalization factor"):
                instr = fn(self.eng[ename])
        else:
            instr = fn(self.eng[ename])
        self.cnt[ename] += 1
        instr.then_inc(self.h[ename], 1)
        tok = (ename, self.cnt[ename], self.epoch)
        self._post(tok, list(reads) + list(sreads), writes)
        return tok

    def dma(self, chan, ename, out, in_, reads=(), writes=()):
        cb = self.chan_buf[chan]
        self._pre(ename, reads, list(writes) + [cb])
        instr = self.eng[ename].dma_start(out=out, in_=in_)
        self.cnt[chan] += 16
        instr.then_inc(self.h[chan], 16)
        tok = (chan, self.cnt[chan], self.epoch)
        self._post(tok, reads, list(writes) + [cb])
        return tok

    def retire(self, *bufs):
        def fold(sem, cnt, ep):
            cur = self.legacy.get(sem)
            if cur is None or ((ep or 0), cnt) > ((cur[1] or 0), cur[0]):
                self.legacy[sem] = (cnt, ep)
        for b in bufs:
            if b.w is not None:
                fold(*b.w)
            for sem, (cnt, ep) in b.r.items():
                fold(sem, cnt, ep)


def build_program(layers=(0, 1), nseq=NSEQ, taps=(),
                  phases=("ln", "attn", "ffn", "gate", "l1attn", "moe"),
                  loop=False, loop_stats=None):
    _buf_seq[0] = 0
    _buf_reg.clear()
    _uid[0] = 0
    nc = bass.Bass("TRN2", target_bir_lowering=False)
    dt = nc.dram_tensor
    xin = dt("x", [nseq, D, S], F32, kind="ExternalInput").ap()
    out = dt("out", [nseq, D, S], F32, kind="ExternalOutput").ap()
    ins = {}

    def din(name, shape):
        ins[name] = dt(name, shape, F32, kind="ExternalInput").ap()
        return ins[name]

    if 0 in layers:
        din("wqp", [8, 128, 1024]); din("wkp", [8, 128, 1024]); din("wvp", [8, 128, 1024])
        din("wo", [8, 128, 1024])
        din("w1f", [32, 128, 1024]); din("w2f", [8, 128, 4096])
        din("b1c", [128, 32]); din("b2c", [128, 8])
        din("thc", [128, 32]); din("wrc", [128, 32]); din("wic", [128, 32])
    if 1 in layers:
        din("wq1", [8, 128, 128]); din("wk1", [8, 128, 128]); din("wv1", [8, 128, 128])
        din("w11", [8, 128, 1024]); din("w21", [8, 128, 1024])
        din("gnw", [128, 8]); din("gnb", [128, 8])
        din("gw", [8, 128, 8]); din("gb", [1, 8])
        din("mw1", [8, 32, 128, 1024]); din("mw2", [8, 8, 128, 4096])
        din("mb1", [8, 128, 32]); din("mb2", [8, 128, 8])
    tapd = {}
    for tname in taps:
        tapd[tname] = dt("tap_" + tname, [NCH, 128, S], F32, kind="ExternalOutput").ap()

    es = contextlib.ExitStack()
    with es:
        k = K(nc, es)
        es.enter_context(nc.Block())

        def sbg(shape, dtype=F32, st=es):
            _uid[0] += 1
            return st.enter_context(nc.sbuf_tensor(f"t{_uid[0]}", shape, dtype))

        def psg(shape, st, dtype=F32):
            _uid[0] += 1
            return st.enter_context(nc.psum_tensor(f"p{_uid[0]}", shape, dtype))

        B = Buf

        X = [sbg([128, S]) for _ in range(NCH)]
        Xb = [B() for _ in range(NCH)]
        XN = [sbg([128, S], BF16) for _ in range(NCH)]
        XNb = [B() for _ in range(NCH)]
        ones_f = sbg([128, 1]); ones_f_b = B()
        ones1_f = sbg([1, 128]); ones1_f_b = B()
        onesH0 = sbg([128, 1], BF16); onesH0_b = B()
        onesH1 = sbg([128, 1], BF16); onesH1_b = B()
        ones1_b16 = sbg([1, 128], BF16); ones1_b16_b = B()
        c01 = sbg([128, 1]); c01_b = B()
        c125 = sbg([128, 1]); c125_b = B()
        ceps = sbg([128, 1]); ceps_b = B()

        k.emit("dve", lambda e: e.memset(ones_f[:], 1.0), writes=[ones_f_b])
        k.emit("dve", lambda e: e.memset(ones1_f[:], 1.0), writes=[ones1_f_b])
        k.emit("dve", lambda e: e.memset(onesH0[0:64, :], 1.0), writes=[onesH0_b])
        k.emit("dve", lambda e: e.memset(onesH0[64:128, :], 0.0), writes=[onesH0_b])
        k.emit("dve", lambda e: e.memset(onesH1[0:64, :], 0.0), writes=[onesH1_b])
        k.emit("dve", lambda e: e.memset(onesH1[64:128, :], 1.0), writes=[onesH1_b])
        k.emit("dve", lambda e: e.memset(ones1_b16[:], 1.0), writes=[ones1_b16_b])
        k.emit("dve", lambda e: e.memset(c01[:], 0.1), writes=[c01_b])
        k.emit("dve", lambda e: e.memset(c125[:], 0.125), writes=[c125_b])
        k.emit("dve", lambda e: e.memset(ceps[:], EPS), writes=[ceps_b])

        def explode(srct, srcb, K, st=es):
            tiles = [sbg([128, 1], st=st) for _ in range(K)]
            bufs = [B(small=True) for _ in range(K)]
            for j in range(K):
                k.emit("dve", lambda e, j=j: e.tensor_copy(tiles[j][:], srct[:, j:j + 1]),
                       reads=[srcb], writes=[bufs[j]])
            return tiles, bufs

        if 0 in layers:
            b1c_s = sbg([128, 32]); b1c_b = B()
            b2c_s = sbg([128, 8]); b2c_b = B()
            Acol = sbg([128, 32]); Acol_b = B(small=True)
            Bcol = sbg([128, 32]); Bcol_b = B(small=True)
            BSIN = sbg([128, 32]); BSIN_b = B(small=True)
            BCOS = sbg([128, 32]); BCOS_b = B(small=True)
            with contextlib.ExitStack() as qs:
                thc_s = sbg([128, 32], st=qs); thc_b = B()
                wrc_s = sbg([128, 32], st=qs); wrc_b = B()
                wic_s = sbg([128, 32], st=qs); wic_b = B()
                t0 = sbg([128, 32], st=qs); t0b = B(small=True)
                t1 = sbg([128, 32], st=qs); t1b = B(small=True)
                t2 = sbg([128, 32], st=qs); t2b = B(small=True)
                ti = sbg([128, 32], I32, st=qs); tib = B(small=True)
                sinth = sbg([128, 32], st=qs); sinth_b = B(small=True)
                costh = sbg([128, 32], st=qs); costh_b = B(small=True)
                cpi2 = sbg([128, 1], st=qs); cpi2_b = B(small=True)
                cm1 = sbg([128, 1], st=qs); cm1_b = B(small=True)
                k.dma("cv", "sp", b1c_s[:], ins["b1c"], writes=[b1c_b])
                k.dma("cv", "sp", thc_s[:], ins["thc"], writes=[thc_b])
                k.dma("cv", "sp", wrc_s[:], ins["wrc"], writes=[wrc_b])
                k.dma("cv", "sp", wic_s[:], ins["wic"], writes=[wic_b])
                k.dma("cv", "sp", b2c_s[:], ins["b2c"], writes=[b2c_b])
                k.emit("dve", lambda e: e.tensor_scalar_mul(t0[:], thc_s[:], float(1.0 / (2 * PI))),
                       reads=[thc_b], writes=[t0b])
                k.emit("dve", lambda e: e.tensor_copy(ti[:], t0[:]), reads=[t0b], writes=[tib])
                k.emit("dve", lambda e: e.tensor_copy(t0[:], ti[:]), reads=[tib], writes=[t0b])
                k.emit("dve", lambda e: e.scalar_tensor_tensor(
                    t1[:], t0[:], float(-2 * PI), thc_s[:], ALU.mult, ALU.add),
                    reads=[t0b, thc_b], writes=[t1b])
                k.emit("dve", lambda e: e.tensor_scalar(t0[:], t1[:], PI, None, ALU.is_gt),
                       reads=[t1b], writes=[t0b])
                k.emit("dve", lambda e: e.scalar_tensor_tensor(
                    t1[:], t0[:], float(-2 * PI), t1[:], ALU.mult, ALU.add),
                    reads=[t0b, t1b], writes=[t1b])
                k.emit("dve", lambda e: e.tensor_scalar(t0[:], t1[:], -PI, None, ALU.is_lt),
                       reads=[t1b], writes=[t0b])
                k.emit("dve", lambda e: e.scalar_tensor_tensor(
                    t1[:], t0[:], float(2 * PI), t1[:], ALU.mult, ALU.add),
                    reads=[t0b, t1b], writes=[t1b])
                k.emit("act", lambda e: e.activation(sinth[:], t1[:], AF.Sin),
                       reads=[t1b], writes=[sinth_b])
                k.emit("act", lambda e: e.activation(t2[:], t1[:], AF.Abs),
                       reads=[t1b], writes=[t2b])
                k.emit("dve", lambda e: e.memset(cpi2[:], PI / 2), writes=[cpi2_b])
                k.emit("dve", lambda e: e.memset(cm1[:], -1.0), writes=[cm1_b])
                k.emit("act", lambda e: e.activation(costh[:], t2[:], AF.Sin, bias=cpi2[:], scale=cm1[:]),
                       reads=[t2b], sreads=[cpi2_b, cm1_b], writes=[costh_b])
                k.emit("dve", lambda e: e.tensor_mul(t0[:], costh[:], wrc_s[:]),
                       reads=[costh_b, wrc_b], writes=[t0b])
                k.emit("dve", lambda e: e.tensor_mul(t2[:], sinth[:], wic_s[:]),
                       reads=[sinth_b, wic_b], writes=[t2b])
                k.emit("dve", lambda e: e.tensor_add(t0[:], t0[:], t2[:]),
                       reads=[t0b, t2b], writes=[t0b])
                k.emit("dve", lambda e: e.tensor_mul(t0[:], t0[:], wic_s[:]),
                       reads=[t0b, wic_b], writes=[t0b])
                k.emit("dve", lambda e: e.tensor_scalar_mul(t0[:], t0[:], 0.1),
                       reads=[t0b], writes=[t0b])
                k.emit("dve", lambda e: e.tensor_mul(Acol[:], t0[:], sinth[:]),
                       reads=[t0b, sinth_b], writes=[Acol_b])
                k.emit("dve", lambda e: e.tensor_mul(Bcol[:], t0[:], costh[:]),
                       reads=[t0b, costh_b], writes=[Bcol_b])
                k.emit("dve", lambda e: e.tensor_scalar_mul(BSIN[:], b1c_s[:], 0.1),
                       reads=[b1c_b], writes=[BSIN_b])
                k.emit("dve", lambda e: e.tensor_scalar(BCOS[:], b1c_s[:], 0.1, PI / 2, ALU.mult, ALU.add),
                       reads=[b1c_b], writes=[BCOS_b])
                k.retire(thc_b, wrc_b, wic_b, t0b, t1b, t2b, tib, sinth_b, costh_b, cpi2_b, cm1_b)
            B1Cc, B1Cc_b = explode(b1c_s, b1c_b, 32)
            if "qc2" in tapd:
                k.dma("ct", "pool", tapd["qc2"][0][:, 0:32], Acol[:], reads=[Acol_b])
            B2Cc, B2Cc_b = explode(b2c_s, b2c_b, 8)
            Ac, Ac_b = explode(Acol, Acol_b, 32)
            Bc, Bc_b = explode(Bcol, Bcol_b, 32)
            BSINc, BSINc_b = explode(BSIN, BSIN_b, 32)
            BCOSc, BCOSc_b = explode(BCOS, BCOS_b, 32)
            if "qc2" in tapd:
                with nc.allow_non_contiguous_dma(reason="debug tap"):
                    for j in range(8):
                        k.dma("ct", "pool", tapd["qc2"][1][:, j:j+1], Ac[j][:], reads=[Ac_b[j]])
                        k.dma("ct", "pool", tapd["qc2"][2][:, j:j+1], BSINc[j][:], reads=[BSINc_b[j]])

        if 1 in layers:
            gnw_s = sbg([128, 8]); gnw_b = B()
            gnb_s = sbg([128, 8]); gnb_b = B()
            mb1_s = [sbg([128, 32]) for _ in range(NE)]
            mb1_b = [B() for _ in range(NE)]
            mb2_s = [sbg([128, 8]) for _ in range(NE)]
            mb2_b = [B() for _ in range(NE)]
            IDF = sbg([128, 128]); IDF_b = B()
            TRIB = sbg([128, 128], BF16); TRIB_b = B()
            SEL = sbg([8, 1024], BF16); SEL_b = B()
            GNEG = [sbg([128, 1]) for _ in range(H1)]
            GNEG_b = [B(small=True) for _ in range(H1)]
            GBS = sbg([128, 8]); GBS_b = B()
            combT = sbg([8, S], BF16); combTb = B()
            k.dma("cv", "sp", gnw_s[:], ins["gnw"], writes=[gnw_b])
            k.dma("cv", "sp", gnb_s[:], ins["gnb"], writes=[gnb_b])
            for e_ in range(NE):
                k.dma("cv", "sp", mb1_s[e_][:], ins["mb1"][e_], writes=[mb1_b[e_]])
                k.dma("cv", "sp", mb2_s[e_][:], ins["mb2"][e_], writes=[mb2_b[e_]])
            GNWc, GNWc_b = explode(gnw_s, gnw_b, 8)
            GNBc, GNBc_b = explode(gnb_s, gnb_b, 8)
            with contextlib.ExitStack() as qs:
                gbrow = sbg([1, 8], st=qs); gbrow_b = B()
                IOTC = sbg([128, 1], st=qs); IOTC_b = B()
                tI = sbg([128, 128], st=qs); tIb = B()
                tT = sbg([128, 128], st=qs); tTb = B()
                tmpc = sbg([128, 1], st=qs); tmpc_b = B(small=True)
                k.dma("cv", "sp", gbrow[:], ins["gb"], writes=[gbrow_b])
                k.emit("pool", lambda e: e.iota(tI[:], pattern=[[1, 128]], base=0,
                                                channel_multiplier=-1,
                                                allow_small_or_imprecise_dtypes=True), writes=[tIb])
                k.emit("dve", lambda e: e.tensor_scalar(IDF[:], tI[:], 0.0, None, ALU.is_equal),
                       reads=[tIb], writes=[IDF_b])
                k.emit("dve", lambda e: e.tensor_scalar(tT[:], tI[:], 0.0, None, ALU.is_ge),
                       reads=[tIb], writes=[tTb])
                k.emit("dve", lambda e: e.tensor_copy(TRIB[:], tT[:]), reads=[tTb], writes=[TRIB_b])
                k.emit("pool", lambda e: e.iota(IOTC[:], pattern=[[1, 1]], base=0,
                                                channel_multiplier=1,
                                                allow_small_or_imprecise_dtypes=True), writes=[IOTC_b])
                SELF = sbg([8, 1024], st=qs); SELF_b = B()
                selap = SELF[:].rearrange("k (e p) -> k e p", p=128)
                k.emit("pool", lambda e: e.iota(selap, pattern=[[1, 8], [0, 128]], base=0,
                                                channel_multiplier=-1,
                                                allow_small_or_imprecise_dtypes=True), writes=[SELF_b])
                k.emit("dve", lambda e: e.tensor_scalar(SELF[:], SELF[:], 0.0, None, ALU.is_equal),
                       reads=[SELF_b], writes=[SELF_b])
                k.emit("dve", lambda e: e.tensor_copy(SEL[:], SELF[:]),
                       reads=[SELF_b], writes=[SEL_b])
                for h in range(H1):
                    lg = LGAM[h]
                    k.emit("dve", lambda e: e.memset(tmpc[:], -lg), writes=[tmpc_b])
                    k.emit("act", lambda e: e.activation(GNEG[h][:], IOTC[:], AF.Exp, scale=tmpc[:]),
                           reads=[IOTC_b], sreads=[tmpc_b], writes=[GNEG_b[h]])
                with contextlib.ExitStack() as ps:
                    P_gb = psg([128, 8], ps); P_gb_b = B(psum=True)
                    k.emit("pe", lambda e: e.matmul(P_gb[:], ones1_f[:], gbrow[:], start=True, stop=True),
                           reads=[ones1_f_b, gbrow_b], writes=[P_gb_b])
                    k.emit("act", lambda e: e.copy(GBS[:], P_gb[:]), reads=[P_gb_b], writes=[GBS_b])
                    k.retire(P_gb_b)
                k.retire(gbrow_b, IOTC_b, tIb, tTb, tmpc_b, SELF_b)

        def tap(name, src_list, src_bufs):
            if name not in tapd:
                return
            for c in range(NCH):
                k.dma("ct", "pool", tapd[name][c], src_list[c][:], reads=[src_bufs[c]])

        def emit_load_x(s):
            for c in range(NCH):
                k.dma("cx", "sp", X[c][:], xin[s][c * 128:(c + 1) * 128, :], writes=[Xb[c]])

        def emit_store_x(s):
            for c in range(NCH):
                k.dma("cs0" if c % 2 == 0 else "cs1", "sp",
                      out[s][c * 128:(c + 1) * 128, :], X[c][:], reads=[Xb[c]])

        def emit_layernorm():
            with contextlib.ExitStack() as fs:
                SQ = [sbg([128, S], st=fs) for _ in range(2)]
                SQb = [B(), B()]
                R_mu = sbg([1, S], st=fs); R_mu_b = B()
                R_m2 = sbg([1, S], st=fs); R_m2_b = B()
                R_rs = sbg([1, S], st=fs); R_rs_b = B()
                T1 = [sbg([128, S], st=fs) for _ in range(2)]
                T1b = [B(), B()]
                with contextlib.ExitStack() as ps:
                    P_s1 = psg([1, S], ps); P_s1_b = B(psum=True)
                    P_s2 = psg([1, S], ps); P_s2_b = B(psum=True)
                    for c in range(NCH):
                        q = SQ[c % 2]; qb = SQb[c % 2]
                        k.emit("act", lambda e: e.activation(q[:], X[c][:], AF.Square),
                               reads=[Xb[c]], writes=[qb])
                        for nb in range(NB):
                            k.emit("pe", lambda e: e.matmul(
                                P_s1[:, nb * 512:(nb + 1) * 512], ones_f[:],
                                X[c][:, nb * 512:(nb + 1) * 512],
                                start=(c == 0), stop=(c == NCH - 1)),
                                reads=[ones_f_b, Xb[c]], writes=[P_s1_b])
                            k.emit("pe", lambda e: e.matmul(
                                P_s2[:, nb * 512:(nb + 1) * 512], ones_f[:],
                                q[:, nb * 512:(nb + 1) * 512],
                                start=(c == 0), stop=(c == NCH - 1)),
                                reads=[ones_f_b, qb], writes=[P_s2_b])
                    k.emit("dve", lambda e: e.tensor_scalar_mul(R_mu[:], P_s1[:], 1.0 / D),
                           reads=[P_s1_b], writes=[R_mu_b])
                    k.emit("dve", lambda e: e.tensor_scalar_mul(R_m2[:], P_s2[:], 1.0 / D),
                           reads=[P_s2_b], writes=[R_m2_b])
                    k.emit("dve", lambda e: e.tensor_mul(R_rs[:], R_mu[:], R_mu[:]),
                           reads=[R_mu_b], writes=[R_rs_b])
                    k.emit("dve", lambda e: e.tensor_sub(R_m2[:], R_m2[:], R_rs[:]),
                           reads=[R_m2_b, R_rs_b], writes=[R_m2_b])
                    k.emit("act", lambda e: e.activation(R_m2[:], R_m2[:], AF.Sqrt, bias=ceps[0:1, :]),
                           reads=[R_m2_b], sreads=[ceps_b], writes=[R_m2_b])
                    k.emit("dve", lambda e: e.reciprocal(R_rs[:], R_m2[:]),
                           reads=[R_m2_b], writes=[R_rs_b])
                    k.retire(P_s1_b, P_s2_b)
                with contextlib.ExitStack() as ps:
                    P_bm = psg([128, S], ps); P_bm_b = B(psum=True)
                    P_br = psg([128, S], ps); P_br_b = B(psum=True)
                    for nb in range(NB):
                        k.emit("pe", lambda e: e.matmul(
                            P_bm[:, nb * 512:(nb + 1) * 512], ones1_f[:],
                            R_mu[0:1, nb * 512:(nb + 1) * 512], start=True, stop=True),
                            reads=[ones1_f_b, R_mu_b], writes=[P_bm_b])
                        k.emit("pe", lambda e: e.matmul(
                            P_br[:, nb * 512:(nb + 1) * 512], ones1_f[:],
                            R_rs[0:1, nb * 512:(nb + 1) * 512], start=True, stop=True),
                            reads=[ones1_f_b, R_rs_b], writes=[P_br_b])
                    for c in range(NCH):
                        t = T1[c % 2]; tb = T1b[c % 2]
                        k.emit("dve", lambda e: e.tensor_sub(t[:], X[c][:], P_bm[:]),
                               reads=[Xb[c], P_bm_b], writes=[tb])
                        k.emit("dve", lambda e: e.tensor_mul(XN[c][:], t[:], P_br[:]),
                               reads=[tb, P_br_b], writes=[XNb[c]])
                    k.retire(P_bm_b, P_br_b)
                k.retire(SQb[0], SQb[1], T1b[0], T1b[1], R_mu_b, R_m2_b, R_rs_b)

        def emit_l0_attn():
            with contextlib.ExitStack() as fs:
                WOD = [sbg([128, 1024], BF16, st=fs) for _ in range(2)]
                WODb = [B(), B()]
                O_all = [sbg([128, S], BF16, st=fs) for _ in range(NCH)]
                O_allb = [B() for _ in range(NCH)]
                WP = [[sbg([128, 1024], BF16, st=fs) for _ in range(3)] for _ in range(2)]
                WPb = [[B() for _ in range(3)] for _ in range(2)]
                QA = [sbg([65, S], BF16, st=fs) for _ in range(2)]
                QAb = [B(), B()]
                KA = [sbg([65, S], BF16, st=fs) for _ in range(2)]
                KAb = [B(), B()]
                VA = [sbg([128, 16 * 65], BF16, st=fs) for _ in range(2)]
                VAb = [B(), B()]
                SQH = sbg([128, S], BF16, st=fs); SQHb = B()
                KM = [sbg([1, 1], st=fs) for _ in range(2)]
                KMb = [B(small=True), B(small=True)]
                PTS = [sbg([128, 512], BF16, st=fs) for _ in range(2)]
                PTSb = [B(), B()]
                R_inv = sbg([1, S], BF16, st=fs); R_invb = B()
                BCS = sbg([64, S], BF16, st=fs); BCSb = B()

                def load_pair(hp):
                    i = hp % 2
                    ch = "cw0" if i == 0 else "cw1"
                    for w, nm in enumerate(["wqp", "wkp", "wvp"]):
                        k.dma(ch, "pool", WP[i][w][:], ins[nm][hp], writes=[WPb[i][w]])

                load_pair(0)
                for hp in range(H0 // 2):
                    i = hp % 2
                    if hp + 1 < H0 // 2:
                        load_pair(hp + 1)
                    with contextlib.ExitStack() as ps:
                        P_A = psg([128, S], ps); P_A_b = B(psum=True)
                        P_NR = psg([1, S], ps); P_NR_b = B(psum=True)
                        for kc in range(NCH):
                            for nb in range(NB):
                                k.emit("pe", lambda e: e.matmul(
                                    P_A[:, nb * 512:(nb + 1) * 512],
                                    WP[i][1][:, kc * 128:(kc + 1) * 128],
                                    XN[kc][:, nb * 512:(nb + 1) * 512],
                                    start=(kc == 0), stop=(kc == NCH - 1)),
                                    reads=[WPb[i][1], XNb[kc]], writes=[P_A_b])
                        k.emit("act", lambda e: e.copy(KA[0][0:64, :], P_A[0:64, :]),
                               reads=[P_A_b], writes=[KAb[0]])
                        k.emit("act", lambda e: e.copy(KA[1][0:64, :], P_A[64:128, :]),
                               reads=[P_A_b], writes=[KAb[1]])
                        k.emit("dve", lambda e: e.memset(KA[0][64:65, :], -1.0), writes=[KAb[0]])
                        k.emit("dve", lambda e: e.memset(KA[1][64:65, :], -1.0), writes=[KAb[1]])
                        k.emit("act", lambda e: e.activation(SQH[:], P_A[:], AF.Square),
                               reads=[P_A_b], writes=[SQHb])
                        for j in range(2):
                            oh, ohb = (onesH0, onesH0_b) if j == 0 else (onesH1, onesH1_b)
                            for nb in range(NB):
                                k.emit("pe", lambda e: e.matmul(
                                    P_NR[:, nb * 512:(nb + 1) * 512], oh[:],
                                    SQH[:, nb * 512:(nb + 1) * 512], start=True, stop=True),
                                    reads=[ohb, SQHb], writes=[P_NR_b])
                            k.emit("dve", lambda e: e.reduce_max(KM[j][:], P_NR[:], axis=AX.X),
                                   reads=[P_NR_b], writes=[KMb[j]])
                        for kc in range(NCH):
                            for nb in range(NB):
                                k.emit("pe", lambda e: e.matmul(
                                    P_A[:, nb * 512:(nb + 1) * 512],
                                    WP[i][0][:, kc * 128:(kc + 1) * 128],
                                    XN[kc][:, nb * 512:(nb + 1) * 512],
                                    start=(kc == 0), stop=(kc == NCH - 1)),
                                    reads=[WPb[i][0], XNb[kc]], writes=[P_A_b])
                        k.emit("act", lambda e: e.activation(QA[0][0:64, :], P_A[0:64, :],
                                                             AF.Identity, scale=c125[0:64, :]),
                               reads=[P_A_b], sreads=[c125_b], writes=[QAb[0]])
                        k.emit("act", lambda e: e.activation(QA[1][0:64, :], P_A[64:128, :],
                                                             AF.Identity, scale=c125[64:128, :]),
                               reads=[P_A_b], sreads=[c125_b], writes=[QAb[1]])
                        k.emit("act", lambda e: e.activation(SQH[:], P_A[:], AF.Square, scale=c125[:]),
                               reads=[P_A_b], sreads=[c125_b], writes=[SQHb])
                        for j in range(2):
                            oh, ohb = (onesH0, onesH0_b) if j == 0 else (onesH1, onesH1_b)
                            for nb in range(NB):
                                k.emit("pe", lambda e: e.matmul(
                                    P_NR[:, nb * 512:(nb + 1) * 512], oh[:],
                                    SQH[:, nb * 512:(nb + 1) * 512], start=True, stop=True),
                                    reads=[ohb, SQHb], writes=[P_NR_b])
                            k.emit("act", lambda e: e.activation(QA[j][64:65, :], P_NR[:],
                                                                 AF.Sqrt, scale=KM[j][0:1, :]),
                                   reads=[P_NR_b], sreads=[KMb[j]], writes=[QAb[j]])
                        k.retire(P_A_b, P_NR_b)
                    with contextlib.ExitStack() as ps:
                        P_V = psg([128, 128], ps); P_V_b = B(psum=True)
                        k.emit("dve", lambda e: e.memset(VA[0][:, 64:16 * 65:65], 1.0), writes=[VAb[0]])
                        k.emit("dve", lambda e: e.memset(VA[1][:, 64:16 * 65:65], 1.0), writes=[VAb[1]])
                        for mc in range(16):
                            for kc in range(NCH):
                                k.emit("pe", lambda e: e.matmul(
                                    P_V[:], XN[kc][:, mc * 128:(mc + 1) * 128],
                                    WP[i][2][:, kc * 128:(kc + 1) * 128],
                                    start=(kc == 0), stop=(kc == NCH - 1)),
                                    reads=[XNb[kc], WPb[i][2]], writes=[P_V_b])
                            k.emit("act", lambda e: e.copy(VA[0][:, mc * 65:mc * 65 + 64], P_V[:, 0:64]),
                                   reads=[P_V_b], writes=[VAb[0]])
                            k.emit("act", lambda e: e.copy(VA[1][:, mc * 65:mc * 65 + 64], P_V[:, 64:128]),
                                   reads=[P_V_b], writes=[VAb[1]])
                        k.retire(P_V_b)
                    for j in range(2):
                        with contextlib.ExitStack() as hs:
                            P_O = psg([65, S], hs); P_O_b = B(psum=True)
                            with contextlib.ExitStack() as ts:
                                P_T = psg([128, S], ts); P_T_b = B(psum=True)
                                for mc in range(16):
                                    for nb in range(NB):
                                        k.emit("pe", lambda e: e.matmul(
                                            P_T[:, nb * 512:(nb + 1) * 512],
                                            KA[j][:, mc * 128:(mc + 1) * 128],
                                            QA[j][:, nb * 512:(nb + 1) * 512],
                                            start=True, stop=True),
                                            reads=[KAb[j], QAb[j]], writes=[P_T_b])
                                    for nb in range(NB):
                                        pt = PTS[nb % 2]; ptb = PTSb[nb % 2]
                                        k.emit("act", lambda e: e.activation(
                                            pt[:], P_T[:, nb * 512:(nb + 1) * 512], AF.Exp),
                                            reads=[P_T_b], writes=[ptb])
                                        k.emit("pe", lambda e: e.matmul(
                                            P_O[:, nb * 512:(nb + 1) * 512],
                                            VA[j][:, mc * 65:(mc + 1) * 65], pt[:],
                                            start=(mc == 0), stop=(mc == 15)),
                                            reads=[VAb[j], ptb], writes=[P_O_b])
                                k.retire(P_T_b)
                            k.emit("dve", lambda e: e.reciprocal(R_inv[:], P_O[64:65, :]),
                                   reads=[P_O_b], writes=[R_invb], low_prec=True)
                            with contextlib.ExitStack() as ts:
                                P_Bc = psg([64, S], ts); P_B_b = B(psum=True)
                                for nb in range(NB):
                                    k.emit("pe", lambda e: e.matmul(
                                        P_Bc[:, nb * 512:(nb + 1) * 512], ones1_b16[0:1, 0:64],
                                        R_inv[0:1, nb * 512:(nb + 1) * 512], start=True, stop=True),
                                        reads=[ones1_b16_b, R_invb], writes=[P_B_b])
                                k.emit("act", lambda e: e.copy(BCS[:], P_Bc[:]),
                                       reads=[P_B_b], writes=[BCSb])
                                ro = j * 64
                                k.emit("dve", lambda e: e.tensor_mul(
                                    O_all[hp][ro:ro + 64, :], P_O[0:64, :], BCS[:]),
                                    reads=[P_O_b, BCSb], writes=[O_allb[hp]])
                                k.retire(P_B_b)
                            k.retire(P_O_b)
                with contextlib.ExitStack() as ps:
                    P_P = [psg([128, S], ps) for _ in range(2)]
                    P_P_b = [B(psum=True), B(psum=True)]
                    k.dma("cw2", "pool", WOD[0][:], ins["wo"][0], writes=[WODb[0]])
                    for dc in range(NCH):
                        i2 = dc % 2
                        if dc + 1 < NCH:
                            k.dma("cw2" if (dc + 1) % 2 == 0 else "cw3", "pool",
                                  WOD[(dc + 1) % 2][:], ins["wo"][dc + 1],
                                  writes=[WODb[(dc + 1) % 2]])
                        pp = P_P[dc % 2]; ppb = P_P_b[dc % 2]
                        for oc in range(NCH):
                            for nb in range(NB):
                                k.emit("pe", lambda e: e.matmul(
                                    pp[:, nb * 512:(nb + 1) * 512],
                                    WOD[i2][:, oc * 128:(oc + 1) * 128],
                                    O_all[oc][:, nb * 512:(nb + 1) * 512],
                                    start=(oc == 0), stop=(oc == NCH - 1)),
                                    reads=[WODb[i2], O_allb[oc]], writes=[ppb])
                        k.emit("dve", lambda e: e.tensor_add(X[dc][:], X[dc][:], pp[:]),
                               reads=[Xb[dc], ppb], writes=[Xb[dc]])
                    k.retire(P_P_b[0], P_P_b[1])
                k.retire(*WODb, *O_allb, *QAb, *KAb, *VAb, SQHb, *KMb,
                         *PTSb, R_invb, BCSb, *WPb[0], *WPb[1])

        def emit_l0_ffn(h_only=False, quantum=True, nfb=NFB, nth=2):
            with contextlib.ExitStack() as fs:
                Hsb = [sbg([128, 1024], BF16, st=fs) for _ in range(NFB)]
                Hb = [B() for _ in range(NFB)]
                WF = [sbg([128, 1024], BF16, st=fs) for _ in range(2)]
                WFb = [B(), B()]
                W2D = [sbg([128, 4096], BF16, st=fs)]
                W2Db = [B()]
                SC = sbg([128, 1024], BF16, st=fs); SCb = B()
                SS = sbg([128, 1024], BF16, st=fs); SSb = B()
                UU = sbg([128, 1024], st=fs); UUb = B()
                with contextlib.ExitStack() as ps:
                    P_H = [psg([128, 1024], ps) for _ in range(2)]
                    P_H_b = [B(psum=True), B(psum=True)]
                    P_Y = [psg([128, 1024], ps) for _ in range(2)]
                    P_Y_b = [B(psum=True), B(psum=True)]
                    for th in range(nth):
                        t0 = th * 1024
                        k.dma("cw0", "pool", WF[0][:], ins["w1f"][0], writes=[WFb[0]])
                        for fb in range(nfb):
                            i = fb % 2
                            if fb + 1 < nfb:
                                k.dma("cw0" if (fb + 1) % 2 == 0 else "cw1", "pool",
                                      WF[(fb + 1) % 2][:], ins["w1f"][fb + 1],
                                      writes=[WFb[(fb + 1) % 2]])
                            ph = P_H[i]; phb = P_H_b[i]
                            for kc in range(NCH):
                                for nb in range(2):
                                    k.emit("pe", lambda e: e.matmul(
                                        ph[:, nb * 512:(nb + 1) * 512],
                                        WF[i][:, kc * 128:(kc + 1) * 128],
                                        XN[kc][:, t0 + nb * 512:t0 + (nb + 1) * 512],
                                        start=(kc == 0), stop=(kc == NCH - 1)),
                                        reads=[WFb[i], XNb[kc]], writes=[phb])
                            if quantum:
                                k.emit("act", lambda e: e.activation(
                                    SC[:], ph[:], AF.Sin, bias=BCOSc[fb][:], scale=c01[:]),
                                    reads=[phb], sreads=[BCOSc_b[fb], c01_b], writes=[SCb])
                                k.emit("act", lambda e: e.activation(
                                    SS[:], ph[:], AF.Sin, bias=BSINc[fb][:], scale=c01[:]),
                                    reads=[phb], sreads=[BSINc_b[fb], c01_b], writes=[SSb])
                                k.emit("dve", lambda e: e.scalar_tensor_tensor(
                                    UU[:], SC[:], Ac[fb][:], ph[:], ALU.mult, ALU.add),
                                    reads=[SCb, phb], sreads=[Ac_b[fb]], writes=[UUb])
                                k.emit("dve", lambda e: e.scalar_tensor_tensor(
                                    UU[:], SS[:], Bc[fb][:], UU[:], ALU.mult, ALU.add),
                                    reads=[SSb, UUb], sreads=[Bc_b[fb]], writes=[UUb])
                                k.emit("act", lambda e: e.activation(
                                    Hsb[fb][:], UU[:], AF.Relu, bias=B1Cc[fb][:]),
                                    reads=[UUb], sreads=[B1Cc_b[fb]], writes=[Hb[fb]])
                            else:
                                k.emit("act", lambda e: e.activation(
                                    Hsb[fb][:], ph[:], AF.Relu, bias=B1Cc[fb][:]),
                                    reads=[phb], sreads=[B1Cc_b[fb]], writes=[Hb[fb]])
                        if h_only:
                            continue
                        if th == 0 and "hsb" in tapd:
                            for c8 in range(NCH):
                                k.dma("ct", "pool", tapd["hsb"][c8][:, 0:1024], Hsb[c8][:],
                                      reads=[Hb[c8]])
                        for dc in range(NCH):
                            i = dc % 2
                            k.dma("cw2", "pool", W2D[0][:], ins["w2f"][dc], writes=[W2Db[0]])
                            py = P_Y[i]; pyb = P_Y_b[i]
                            for fb in range(NFB):
                                for nb in range(2):
                                    k.emit("pe", lambda e: e.matmul(
                                        py[:, nb * 512:(nb + 1) * 512],
                                        W2D[0][:, fb * 128:(fb + 1) * 128],
                                        Hsb[fb][:, nb * 512:(nb + 1) * 512],
                                        start=(fb == 0), stop=(fb == NFB - 1)),
                                        reads=[W2Db[0], Hb[fb]], writes=[pyb])
                            k.emit("dve", lambda e: e.scalar_tensor_tensor(
                                X[dc][:, t0:t0 + 1024], py[:], B2Cc[dc][:],
                                X[dc][:, t0:t0 + 1024], ALU.add, ALU.add),
                                reads=[pyb, Xb[dc]], sreads=[B2Cc_b[dc]], writes=[Xb[dc]])
                    k.retire(P_H_b[0], P_H_b[1], P_Y_b[0], P_Y_b[1])
                k.retire(*Hb, *WFb, *W2Db, SCb, SSb, UUb)

        def emit_gate():
            with contextlib.ExitStack() as fs:
                GWS = [sbg([128, 8], BF16, st=fs) for _ in range(NCH)]
                GWSb = [B() for _ in range(NCH)]
                for kc in range(NCH):
                    k.dma("cw4", "pool", GWS[kc][:], ins["gw"][kc], writes=[GWSb[kc]])
                GL = sbg([128, 8], st=fs); GLb = B(small=True)
                GL2 = sbg([128, 8], st=fs); GL2b = B(small=True)
                M1 = sbg([128, 8], st=fs); M1b = B(small=True)
                M2 = sbg([128, 8], st=fs); M2b = B(small=True)
                V1 = sbg([128, 1], st=fs); V1b = B(small=True)
                V2 = sbg([128, 1], st=fs); V2b = B(small=True)
                W1g = sbg([128, 1], st=fs); W1gb = B(small=True)
                W2g = sbg([128, 1], st=fs); W2gb = B(small=True)
                CMB = sbg([128, 8], st=fs); CMBb = B(small=True)
                with contextlib.ExitStack() as ps:
                    P_G = psg([128, 8], ps); P_G_b = B(psum=True)
                    P_TC = psg([8, 128], ps); P_TC_b = B(psum=True)
                    for tc in range(16):
                        for kc in range(NCH):
                            k.emit("pe", lambda e: e.matmul(
                                P_G[:], XN[kc][:, tc * 128:(tc + 1) * 128], GWS[kc][:],
                                start=(kc == 0), stop=(kc == NCH - 1)),
                                reads=[XNb[kc], GWSb[kc]], writes=[P_G_b])
                        k.emit("dve", lambda e: e.tensor_add(GL[:], P_G[:], GBS[:]),
                               reads=[P_G_b, GBS_b], writes=[GLb])
                        k.emit("dve", lambda e: e.reduce_max(V1[:], GL[:], axis=AX.X),
                               reads=[GLb], writes=[V1b])
                        k.emit("dve", lambda e: e.tensor_scalar(M1[:], GL[:], V1[:], None, ALU.is_ge),
                               reads=[GLb], sreads=[V1b], writes=[M1b])
                        k.emit("dve", lambda e: e.scalar_tensor_tensor(
                            GL2[:], M1[:], -1e30, GL[:], ALU.mult, ALU.add),
                            reads=[M1b, GLb], writes=[GL2b])
                        k.emit("dve", lambda e: e.reduce_max(V2[:], GL2[:], axis=AX.X),
                               reads=[GL2b], writes=[V2b])
                        k.emit("dve", lambda e: e.tensor_scalar(M2[:], GL2[:], V2[:], None, ALU.is_ge),
                               reads=[GL2b], sreads=[V2b], writes=[M2b])
                        k.emit("dve", lambda e: e.tensor_sub(V1[:], V1[:], V2[:]),
                               reads=[V1b, V2b], writes=[V1b])
                        k.emit("act", lambda e: e.activation(W1g[:], V1[:], AF.Sigmoid),
                               reads=[V1b], writes=[W1gb])
                        k.emit("dve", lambda e: e.tensor_scalar(W2g[:], W1g[:], -1.0, 1.0,
                                                                ALU.mult, ALU.add),
                               reads=[W1gb], writes=[W2gb])
                        k.emit("dve", lambda e: e.tensor_scalar(CMB[:], M1[:], W1g[:], None, ALU.mult),
                               reads=[M1b], sreads=[W1gb], writes=[CMBb])
                        k.emit("dve", lambda e: e.scalar_tensor_tensor(
                            CMB[:], M2[:], W2g[:], CMB[:], ALU.mult, ALU.add),
                            reads=[M2b, CMBb], sreads=[W2gb], writes=[CMBb])
                        k.emit("pe", lambda e: e.transpose(P_TC[:], CMB[:], IDF[:]),
                               reads=[CMBb, IDF_b], writes=[P_TC_b])
                        k.emit("dve", lambda e: e.tensor_copy(combT[:, tc * 128:(tc + 1) * 128], P_TC[:]),
                               reads=[P_TC_b], writes=[combTb])
                    k.retire(P_G_b, P_TC_b)
                k.retire(*GWSb, GLb, GL2b, M1b, M2b, V1b, V2b, W1gb, W2gb, CMBb)

        def emit_l1_attn():
            with contextlib.ExitStack() as fs:
                RETN = [sbg([128, S], BF16, st=fs) for _ in range(NCH)]
                RETNb = [B() for _ in range(NCH)]
                with contextlib.ExitStack() as fa:
                    WQ1 = [sbg([128, 128], BF16, st=fa) for _ in range(2)]
                    WQ1b = [B(), B()]
                    WK1 = [sbg([128, 128], BF16, st=fa) for _ in range(2)]
                    WK1b = [B(), B()]
                    WV1 = [sbg([128, 128], BF16, st=fa) for _ in range(2)]
                    WV1b = [B(), B()]
                    QH = sbg([128, S], BF16, st=fa); QHb = B()
                    KH = sbg([128, S], BF16, st=fa); KHb = B()
                    VH = sbg([128, S], BF16, st=fa); VHb = B()
                    SDT = sbg([128, S], BF16, st=fa); SDTb = B()
                    GBR = sbg([128, S], BF16, st=fa); GBRb = B()
                    NROW = sbg([128, S], st=fa); NROWb = B()
                    GAMC = sbg([128, 1], st=fa); GAMCb = B(small=True)
                    RS = sbg([128, 1024], st=fa); RSb = B()
                    RS2 = sbg([128, 1024], st=fa); RS2b = B()
                    R_m = sbg([1, 1024], st=fa); R_mb = B()
                    R_rs = sbg([1, 1024], st=fa); R_rsb = B()
                    R_nm = sbg([1, 1024], st=fa); R_nmb = B()
                    k.emit("pool", lambda e: e.iota(NROW[:], pattern=[[1, S]], base=0,
                                                    channel_multiplier=0,
                                                    allow_small_or_imprecise_dtypes=True),
                           writes=[NROWb])

                    def load_h(h):
                        i = h % 2
                        ch = "cw0" if i == 0 else "cw1"
                        k.dma(ch, "pool", WQ1[i][:], ins["wq1"][h], writes=[WQ1b[i]])
                        k.dma(ch, "pool", WK1[i][:], ins["wk1"][h], writes=[WK1b[i]])
                        k.dma(ch, "pool", WV1[i][:], ins["wv1"][h], writes=[WV1b[i]])

                    load_h(0)
                    for h in range(H1):
                        i = h % 2
                        if h + 1 < H1:
                            load_h(h + 1)
                        lg = LGAM[h]
                        k.emit("dve", lambda e: e.memset(GAMC[:], lg), writes=[GAMCb])
                        k.emit("act", lambda e: e.activation(GBR[:], NROW[:], AF.Exp, scale=GAMC[:]),
                               reads=[NROWb], sreads=[GAMCb], writes=[GBRb])
                        with contextlib.ExitStack() as qs:
                            P_A = psg([128, S], qs); P_A_b = B(psum=True)
                            for nb in range(NB):
                                k.emit("pe", lambda e: e.matmul(
                                    P_A[:, nb * 512:(nb + 1) * 512], WQ1[i][:],
                                    XN[h][:, nb * 512:(nb + 1) * 512], start=True, stop=True),
                                    reads=[WQ1b[i], XNb[h]], writes=[P_A_b])
                            k.emit("dve", lambda e: e.tensor_mul(QH[:], P_A[:], GBR[:]),
                                   reads=[P_A_b, GBRb], writes=[QHb])
                            for nb in range(NB):
                                k.emit("pe", lambda e: e.matmul(
                                    P_A[:, nb * 512:(nb + 1) * 512], WK1[i][:],
                                    XN[h][:, nb * 512:(nb + 1) * 512], start=True, stop=True),
                                    reads=[WK1b[i], XNb[h]], writes=[P_A_b])
                            k.emit("act", lambda e: e.copy(KH[:], P_A[:]), reads=[P_A_b], writes=[KHb])
                            k.retire(P_A_b)
                        with contextlib.ExitStack() as vs:
                            P_V = psg([128, 128], vs); P_V_b = B(psum=True)
                            for mc in range(16):
                                k.emit("pe", lambda e: e.matmul(
                                    P_V[:], XN[h][:, mc * 128:(mc + 1) * 128], WV1[i][:],
                                    start=True, stop=True),
                                    reads=[XNb[h], WV1b[i]], writes=[P_V_b])
                                gm0 = float(np.exp(-lg * (mc * 128)))
                                k.emit("dve", lambda e: e.tensor_scalar(
                                    VH[:, mc * 128:(mc + 1) * 128], P_V[:], GNEG[h][:], gm0,
                                    ALU.mult, ALU.mult),
                                    reads=[P_V_b, GNEG_b[h]], writes=[VHb])
                            k.retire(P_V_b)
                        with contextlib.ExitStack() as rs_:
                            P_R = psg([128, S], rs_); P_R_b = B(psum=True)
                            with contextlib.ExitStack() as ts:
                                P_T = [psg([128, 512], ts) for _ in range(2)]
                                P_T_b = [B(psum=True), B(psum=True)]
                                for mc in range(16):
                                    m0 = mc * 128
                                    blocks = []
                                    n0 = m0
                                    while n0 < S:
                                        n1 = min(S, (n0 // 512 + 1) * 512)
                                        blocks.append((n0, n1))
                                        n0 = n1
                                    for bi, (n0, n1) in enumerate(blocks):
                                        pt = P_T[bi % 2]; ptb = P_T_b[bi % 2]
                                        k.emit("pe", lambda e: e.matmul(
                                            pt[:, 0:n1 - n0], KH[:, mc * 128:(mc + 1) * 128],
                                            QH[:, n0:n1], start=True, stop=True),
                                            reads=[KHb, QHb], writes=[ptb])
                                        k.emit("act", lambda e: e.copy(SDT[:, n0:n1], pt[:, 0:n1 - n0]),
                                               reads=[ptb], writes=[SDTb])
                                        if bi == 0:
                                            k.emit("dve", lambda e: e.tensor_mul(
                                                SDT[:, m0:m0 + 128], SDT[:, m0:m0 + 128], TRIB[:]),
                                                reads=[SDTb, TRIB_b], writes=[SDTb])
                                        for nq in range(n0, n1, 512):
                                            nq1 = min(n1, nq + 512)
                                            k.emit("pe", lambda e: e.matmul(
                                                P_R[:, nq:nq1], VH[:, mc * 128:(mc + 1) * 128],
                                                SDT[:, nq:nq1], start=(mc == 0),
                                                stop=(mc == 15)),
                                                reads=[VHb, SDTb], writes=[P_R_b])
                                k.retire(P_T_b[0], P_T_b[1])
                            for nh in range(2):
                                o0 = nh * 1024
                                k.emit("act", lambda e: e.copy(RS[:], P_R[:, o0:o0 + 1024]),
                                       reads=[P_R_b], writes=[RSb])
                                k.emit("act", lambda e: e.activation(RS2[:], P_R[:, o0:o0 + 1024],
                                                                     AF.Square),
                                       reads=[P_R_b], writes=[RS2b])
                                with contextlib.ExitStack() as gs:
                                    P_r1 = psg([1, 1024], gs); P_r1_b = B(psum=True)
                                    P_r2 = psg([1, 1024], gs); P_r2_b = B(psum=True)
                                    for nb in range(2):
                                        k.emit("pe", lambda e: e.matmul(
                                            P_r1[:, nb * 512:(nb + 1) * 512], ones_f[:],
                                            RS[:, nb * 512:(nb + 1) * 512], start=True, stop=True),
                                            reads=[ones_f_b, RSb], writes=[P_r1_b])
                                        k.emit("pe", lambda e: e.matmul(
                                            P_r2[:, nb * 512:(nb + 1) * 512], ones_f[:],
                                            RS2[:, nb * 512:(nb + 1) * 512], start=True, stop=True),
                                            reads=[ones_f_b, RS2b], writes=[P_r2_b])
                                    k.emit("dve", lambda e: e.tensor_scalar_mul(R_m[:], P_r1[:],
                                                                                1.0 / 128.0),
                                           reads=[P_r1_b], writes=[R_mb])
                                    k.emit("dve", lambda e: e.tensor_scalar_mul(R_rs[:], P_r2[:],
                                                                                1.0 / 128.0),
                                           reads=[P_r2_b], writes=[R_rsb])
                                    k.emit("dve", lambda e: e.tensor_mul(R_nm[:], R_m[:], R_m[:]),
                                           reads=[R_mb], writes=[R_nmb])
                                    k.emit("dve", lambda e: e.tensor_sub(R_rs[:], R_rs[:], R_nm[:]),
                                           reads=[R_rsb, R_nmb], writes=[R_rsb])
                                    k.emit("act", lambda e: e.activation(R_rs[:], R_rs[:], AF.Sqrt,
                                                                         bias=ceps[0:1, :]),
                                           reads=[R_rsb], sreads=[ceps_b], writes=[R_rsb])
                                    k.emit("dve", lambda e: e.reciprocal(R_rs[:], R_rs[:]),
                                           reads=[R_rsb], writes=[R_rsb])
                                    k.emit("dve", lambda e: e.tensor_mul(R_nm[:], R_m[:], R_rs[:]),
                                           reads=[R_mb, R_rsb], writes=[R_nmb])
                                    k.emit("dve", lambda e: e.tensor_scalar_mul(R_nm[:], R_nm[:], -1.0),
                                           reads=[R_nmb], writes=[R_nmb])
                                    k.retire(P_r1_b, P_r2_b)
                                with contextlib.ExitStack() as gs:
                                    P_b1 = psg([128, 1024], gs); P_b1_b = B(psum=True)
                                    P_b2g = psg([128, 1024], gs); P_b2_b = B(psum=True)
                                    for nb in range(2):
                                        k.emit("pe", lambda e: e.matmul(
                                            P_b1[:, nb * 512:(nb + 1) * 512], ones1_f[:],
                                            R_rs[0:1, nb * 512:(nb + 1) * 512], start=True, stop=True),
                                            reads=[ones1_f_b, R_rsb], writes=[P_b1_b])
                                        k.emit("pe", lambda e: e.matmul(
                                            P_b2g[:, nb * 512:(nb + 1) * 512], ones1_f[:],
                                            R_nm[0:1, nb * 512:(nb + 1) * 512], start=True, stop=True),
                                            reads=[ones1_f_b, R_nmb], writes=[P_b2_b])
                                    k.emit("dve", lambda e: e.tensor_mul(RS[:], RS[:], P_b1[:]),
                                           reads=[RSb, P_b1_b], writes=[RSb])
                                    k.emit("dve", lambda e: e.tensor_add(RS[:], RS[:], P_b2g[:]),
                                           reads=[RSb, P_b2_b], writes=[RSb])
                                    k.emit("dve", lambda e: e.tensor_scalar(
                                        RETN[h][:, o0:o0 + 1024], RS[:], GNWc[h][:],
                                        GNBc[h][:], ALU.mult, ALU.add),
                                        reads=[RSb], sreads=[GNWc_b[h], GNBc_b[h]], writes=[RETNb[h]])
                                    k.retire(P_b1_b, P_b2_b)
                            k.retire(P_R_b)
                    k.retire(*WQ1b, *WK1b, *WV1b, QHb, KHb, VHb, SDTb, GBRb, NROWb, GAMCb,
                             RSb, RS2b, R_mb, R_rsb, R_nmb)
                with contextlib.ExitStack() as fb_:
                    W11S = [sbg([128, 1024], BF16, st=fb_) for _ in range(NCH)]
                    W11b = [B() for _ in range(NCH)]
                    W21S = [sbg([128, 1024], BF16, st=fb_) for _ in range(NCH)]
                    W21b = [B() for _ in range(NCH)]
                    GS = sbg([128, S], BF16, st=fb_); GSb = B()
                    for kc in range(NCH):
                        k.dma("cw2", "pool", W11S[kc][:], ins["w11"][kc], writes=[W11b[kc]])
                        k.dma("cw3", "pool", W21S[kc][:], ins["w21"][kc], writes=[W21b[kc]])
                    with contextlib.ExitStack() as ps:
                        P_G = [psg([128, S], ps) for _ in range(2)]
                        P_G_b = [B(psum=True), B(psum=True)]
                        for dc in range(NCH):
                            pg = P_G[dc % 2]; pgb = P_G_b[dc % 2]
                            for kc in range(NCH):
                                for nb in range(NB):
                                    k.emit("pe", lambda e: e.matmul(
                                        pg[:, nb * 512:(nb + 1) * 512],
                                        W11S[kc][:, dc * 128:(dc + 1) * 128],
                                        XN[kc][:, nb * 512:(nb + 1) * 512],
                                        start=(kc == 0), stop=(kc == NCH - 1)),
                                        reads=[W11b[kc], XNb[kc]], writes=[pgb])
                            k.emit("act", lambda e: e.activation(GS[:], pg[:], AF.Silu),
                                   reads=[pgb], writes=[GSb])
                            k.emit("dve", lambda e: e.tensor_add(RETN[dc][:], GS[:], RETN[dc][:]),
                                   reads=[GSb, RETNb[dc]], writes=[RETNb[dc]])
                        k.retire(P_G_b[0], P_G_b[1])
                    with contextlib.ExitStack() as ps:
                        P_Y2 = [psg([128, S], ps) for _ in range(2)]
                        P_Y2_b = [B(psum=True), B(psum=True)]
                        for dc in range(NCH):
                            py = P_Y2[dc % 2]; pyb = P_Y2_b[dc % 2]
                            for oc in range(NCH):
                                for nb in range(NB):
                                    k.emit("pe", lambda e: e.matmul(
                                        py[:, nb * 512:(nb + 1) * 512],
                                        W21S[oc][:, dc * 128:(dc + 1) * 128],
                                        RETN[oc][:, nb * 512:(nb + 1) * 512],
                                        start=(oc == 0), stop=(oc == NCH - 1)),
                                        reads=[W21b[oc], RETNb[oc]], writes=[pyb])
                            k.emit("dve", lambda e: e.tensor_add(X[dc][:], X[dc][:], py[:]),
                                   reads=[Xb[dc], pyb], writes=[Xb[dc]])
                        k.retire(P_Y2_b[0], P_Y2_b[1])
                    k.retire(*W11b, *W21b, GSb)
                k.retire(*RETNb)

        def emit_moe():
            with contextlib.ExitStack() as fs:
                MB1c = []; MB1c_b = []
                MB2c = []; MB2c_b = []
                for e_ in range(NE):
                    t_, b_ = explode(mb1_s[e_], mb1_b[e_], 32, st=fs)
                    MB1c.append(t_); MB1c_b.append(b_)
                    t_, b_ = explode(mb2_s[e_], mb2_b[e_], 8, st=fs)
                    MB2c.append(t_); MB2c_b.append(b_)
                Hsb = [sbg([128, 1024], BF16, st=fs) for _ in range(NFB)]
                Hb = [B() for _ in range(NFB)]
                WF1 = [sbg([128, 1024], BF16, st=fs) for _ in range(2)]
                WF1b = [B(), B()]
                MW2 = [sbg([128, 4096], BF16, st=fs)]
                MW2b = [B()]
                CBS = [sbg([128, 1024], BF16, st=fs) for _ in range(2)]
                CBSb = [B(), B()]
                T1 = sbg([128, 1024], st=fs); T1b = B()
                for e_ in range(NE):
                    with contextlib.ExitStack() as cs:
                        P_CB = psg([128, 1024], cs); P_CB_b = B(psum=True)
                        for th in range(2):
                            for nb in range(2):
                                k.emit("pe", lambda e: e.matmul(
                                    P_CB[:, nb * 512:(nb + 1) * 512],
                                    SEL[:, e_ * 128:(e_ + 1) * 128],
                                    combT[:, th * 1024 + nb * 512:th * 1024 + (nb + 1) * 512],
                                    start=True, stop=True),
                                    reads=[SEL_b, combTb], writes=[P_CB_b])
                            k.emit("act", lambda e: e.copy(CBS[th][:], P_CB[:]),
                                   reads=[P_CB_b], writes=[CBSb[th]])
                        k.retire(P_CB_b)
                    with contextlib.ExitStack() as ps:
                        P_H = [psg([128, 1024], ps) for _ in range(2)]
                        P_H_b = [B(psum=True), B(psum=True)]
                        P_Y = [psg([128, 1024], ps) for _ in range(2)]
                        P_Y_b = [B(psum=True), B(psum=True)]
                        for th in range(2):
                            t0 = th * 1024
                            k.dma("cw4", "pool", WF1[0][:], ins["mw1"][e_, 0], writes=[WF1b[0]])
                            for fb in range(NFB):
                                i = fb % 2
                                if fb + 1 < NFB:
                                    k.dma("cw4" if (fb + 1) % 2 == 0 else "cw5", "pool",
                                          WF1[(fb + 1) % 2][:], ins["mw1"][e_, fb + 1],
                                          writes=[WF1b[(fb + 1) % 2]])
                                ph = P_H[i]; phb = P_H_b[i]
                                for kc in range(NCH):
                                    for nb in range(2):
                                        k.emit("pe", lambda e: e.matmul(
                                            ph[:, nb * 512:(nb + 1) * 512],
                                            WF1[i][:, kc * 128:(kc + 1) * 128],
                                            XN[kc][:, t0 + nb * 512:t0 + (nb + 1) * 512],
                                            start=(kc == 0), stop=(kc == NCH - 1)),
                                            reads=[WF1b[i], XNb[kc]], writes=[phb])
                                k.emit("act", lambda e: e.activation(
                                    Hsb[fb][:], ph[:], AF.Gelu, bias=MB1c[e_][fb][:]),
                                    reads=[phb], sreads=[MB1c_b[e_][fb]], writes=[Hb[fb]])
                            for dc in range(NCH):
                                k.dma("cw2", "pool", MW2[0][:], ins["mw2"][e_, dc], writes=[MW2b[0]])
                                py = P_Y[dc % 2]; pyb = P_Y_b[dc % 2]
                                for fb in range(NFB):
                                    for nb in range(2):
                                        k.emit("pe", lambda e: e.matmul(
                                            py[:, nb * 512:(nb + 1) * 512],
                                            MW2[0][:, fb * 128:(fb + 1) * 128],
                                            Hsb[fb][:, nb * 512:(nb + 1) * 512],
                                            start=(fb == 0), stop=(fb == NFB - 1)),
                                            reads=[MW2b[0], Hb[fb]], writes=[pyb])
                                k.emit("dve", lambda e: e.scalar_tensor_tensor(
                                    T1[:], py[:], MB2c[e_][dc][:], CBS[th][:],
                                    ALU.add, ALU.mult),
                                    reads=[pyb, CBSb[th]], sreads=[MB2c_b[e_][dc]], writes=[T1b])
                                k.emit("dve", lambda e: e.tensor_add(
                                    X[dc][:, t0:t0 + 1024], X[dc][:, t0:t0 + 1024], T1[:]),
                                    reads=[T1b, Xb[dc]], writes=[Xb[dc]])
                        k.retire(P_H_b[0], P_H_b[1], P_Y_b[0], P_Y_b[1])
                k.retire(*Hb, *WF1b, *MW2b, *CBSb, T1b)

        def emit_body(s):
            want_taps = (not loop) and s == 0
            emit_load_x(s)
            if 0 in layers:
                if "ln" in phases:
                    emit_layernorm()
                    if want_taps:
                        tap("xn0", XN, XNb)
                if "attn" in phases:
                    emit_l0_attn()
                    if want_taps:
                        tap("att0", X, Xb)
                if "ffnH" in phases:
                    emit_l0_ffn(h_only=True)
                if "ffnH4" in phases:
                    emit_l0_ffn(h_only=True, nfb=4, nth=1)
                if "ffnH4p" in phases:
                    emit_l0_ffn(h_only=True, quantum=False, nfb=4, nth=1)
                if "ffnH1p" in phases:
                    emit_l0_ffn(h_only=True, quantum=False, nfb=1, nth=1)
                if "ffnP" in phases:
                    emit_l0_ffn(quantum=False)
                if "ffnH1q" in phases:
                    emit_l0_ffn(h_only=True, quantum=True, nfb=1, nth=1)
                if "ffnH2q" in phases:
                    emit_l0_ffn(h_only=True, quantum=True, nfb=2, nth=1)
                if "ffn" in phases:
                    emit_l0_ffn()
                    if want_taps:
                        tap("x1", X, Xb)
            if 1 in layers:
                if "ln" in phases:
                    emit_layernorm()
                    if want_taps:
                        tap("xn1", XN, XNb)
                if "gate" in phases:
                    emit_gate()
                if "l1attn" in phases:
                    emit_l1_attn()
                    if want_taps:
                        tap("att1", X, Xb)
                if "moe" in phases:
                    emit_moe()
            emit_store_x(s)

        stats_out = None
        if not loop:
            for s in range(nseq):
                emit_body(s)
            for ch in ("cs0", "cs1", "cx", "cv", "ct"):
                if k.cnt[ch]:
                    nc.sync.wait_ge(k.h[ch], k.cnt[ch])
        else:
            k.begin_loop(loop_stats)
            if loop_stats is None:
                emit_body(0)
                stats_out = k.collect_stats(_buf_reg)
            else:
                with nc.Fori(0, nseq) as i_sym:
                    emit_body(i_sym)
                    k.end_body()
                stats_out = k.collect_stats(_buf_reg)
                for ch in DMACH:
                    d = k.delta.get(ch, 0)
                    final = k.cnt0[ch] + d * (nseq + 1)
                    if final:
                        nc.sync.wait_ge(k.h[ch], final)
    return (nc, stats_out) if loop else nc


def _prep_inputs(inp):
    f32 = np.float32
    A = lambda x: np.ascontiguousarray(np.asarray(x, f32))
    d = {}
    d["x"] = np.ascontiguousarray(A(inp["L0_attn_x"]).transpose(0, 2, 1))
    pr = lambda w: np.ascontiguousarray(
        w.reshape(8, 128, 8, 128).transpose(2, 1, 0, 3)).reshape(8, 128, 1024)
    d["wqp"] = pr(A(inp["L0_attn_w_q"]).T)
    d["wkp"] = pr(A(inp["L0_attn_w_k"]).T)
    d["wvp"] = pr(A(inp["L0_attn_w_v"]).T)
    d["wo"] = pr(A(inp["L0_attn_w_o"]).T)
    W1 = A(inp["L0_ffn_W1"]); W2 = A(inp["L0_ffn_W2"])
    d["w1f"] = np.ascontiguousarray(
        W1.reshape(8, 128, 32, 128).transpose(2, 1, 0, 3)).reshape(32, 128, 1024)
    d["w2f"] = np.ascontiguousarray(
        W2.reshape(32, 128, 8, 128).transpose(2, 1, 0, 3)).reshape(8, 128, 4096)
    d["b1c"] = np.ascontiguousarray(A(inp["L0_ffn_b1"]).reshape(32, 128).T)
    d["b2c"] = np.ascontiguousarray(A(inp["L0_ffn_b2"]).reshape(8, 128).T)
    d["thc"] = np.ascontiguousarray(A(inp["L0_ffn_quantum_phases"]).reshape(32, 128).T)
    d["wrc"] = np.ascontiguousarray(A(inp["L0_ffn_quantum_weights_real"]).reshape(32, 128).T)
    d["wic"] = np.ascontiguousarray(A(inp["L0_ffn_quantum_weights_imag"]).reshape(32, 128).T)
    d["wq1"] = A(inp["L1_attn_Wq"]); d["wk1"] = A(inp["L1_attn_Wk"]); d["wv1"] = A(inp["L1_attn_Wv"])
    d["w11"] = np.ascontiguousarray(A(inp["L1_attn_W1"]).reshape(8, 128, 1024))
    d["w21"] = np.ascontiguousarray(A(inp["L1_attn_W2"]).reshape(8, 128, 1024))
    d["gnw"] = np.ascontiguousarray(A(inp["L1_attn_gn_w"]).reshape(8, 128).T)
    d["gnb"] = np.ascontiguousarray(A(inp["L1_attn_gn_b"]).reshape(8, 128).T)
    d["gw"] = np.ascontiguousarray(A(inp["L1_ffn_gate_w"]).reshape(8, 128, 8))
    d["gb"] = np.ascontiguousarray(A(inp["L1_ffn_gate_b"]).reshape(1, 8))
    d["mw1"] = np.ascontiguousarray(
        A(inp["L1_ffn_w1"]).reshape(8, 8, 128, 32, 128).transpose(0, 3, 2, 1, 4)).reshape(8, 32, 128, 1024)
    d["mw2"] = np.ascontiguousarray(
        A(inp["L1_ffn_w2"]).reshape(8, 32, 128, 8, 128).transpose(0, 3, 2, 1, 4)).reshape(8, 8, 128, 4096)
    d["mb1"] = np.ascontiguousarray(A(inp["L1_ffn_b1"]).reshape(8, 32, 128).transpose(0, 2, 1))
    d["mb2"] = np.ascontiguousarray(A(inp["L1_ffn_b2"]).reshape(8, 8, 128).transpose(0, 2, 1))
    return d


def kernel(**inputs):
    d = _prep_inputs(inputs)
    nc1, st1 = build_program(loop=True, loop_stats=None)
    nc, st2 = build_program(loop=True, loop_stats=st1)
    if st2["wait_pairs"] - st1["wait_pairs"]:
        nc, _ = build_program(loop=True, loop_stats=st2)
    wkeys = [k_ for k_ in d if k_ != "x"]
    in_maps = []
    for c in range(8):
        m = {"x": np.ascontiguousarray(d["x"][c * NSEQ:(c + 1) * NSEQ])}
        for k_ in wkeys:
            m[k_] = d[k_]
        in_maps.append(m)
    res = run_bass_kernel_spmd(nc, in_maps, core_ids=list(range(8)))
    outs = [np.asarray(r["out"]) for r in res.results]
    full = np.concatenate(outs, axis=0)
    return np.ascontiguousarray(full.transpose(0, 2, 1))


# version 05
# speedup vs baseline: 1.0035x; 1.0035x over previous
import contextlib
import numpy as np
import concourse.bass as bass
import concourse.mybir as mybir
from concourse.bass_utils import run_bass_kernel_spmd

F32 = mybir.dt.float32
BF16 = mybir.dt.bfloat16
I32 = mybir.dt.int32
AF = mybir.ActivationFunctionType
ALU = mybir.AluOpType
AX = mybir.AxisListType
PI = float(np.pi)

S = 2048
D = 1024
NCH = 8
FF = 4096
NFB = 32
NSEQ = 4
NB = S // 512
EPS = 1e-5
H0 = 16
H1 = 8
NE = 8

_lin = np.linspace(np.log(1.0 / 512.0), np.log(1.0 / 32.0), H1)
GAMMAS = (1.0 - np.exp(_lin)).astype(np.float64)
LGAM = [float(np.log(g)) for g in GAMMAS]

COMPUTE = ("pe", "act", "dve", "pool")
DMACH = ["cx", "cs0", "cs1", "cw0", "cw1", "cw2", "cw3", "cw4", "cw5", "cv", "ct"]
_uid = [0]


_buf_seq = [0]
_buf_reg = []


class Buf:
    __slots__ = ("w", "r", "fresh", "psum", "small", "bid")

    def __init__(self, psum=False, small=False):
        self.w = None
        self.r = {}
        self.fresh = True
        self.psum = psum
        self.small = small
        self.bid = _buf_seq[0]
        _buf_seq[0] += 1
        _buf_reg.append(self)


class K:
    ENAME2ENG = {"pe": "tensor", "act": "scalar", "dve": "vector",
                 "pool": "gpsimd", "sp": "sync"}

    def __init__(self, nc, es):
        self.nc = nc
        self.eng = {"pe": nc.tensor, "act": nc.scalar, "dve": nc.vector,
                    "pool": nc.gpsimd, "sp": nc.sync}
        self.h = {}
        self.cnt = {}
        for name in list(COMPUTE) + DMACH:
            self.h[name] = es.enter_context(nc.semaphore(name))
            self.cnt[name] = 0
        self.waited = {}
        self.legacy = {}
        self.epoch = None
        self.cnt0 = None
        self.delta = None
        self.lwaited = {}
        self.bregs = {}
        self.tmpreg = {}
        self.wait_pairs = set()
        self.chan_buf = {n: Buf() for n in DMACH}

    def begin_loop(self, stats=None):
        self.cnt0 = dict(self.cnt)
        self.epoch = 0
        self.lwaited = {}
        if stats is not None:
            self.delta = stats["delta"]
            for sem in self.cnt:
                if self.cnt0[sem] > 0 and self.delta.get(sem, 0) > 0:
                    self.eng["sp"].wait_ge(self.h[sem], self.cnt0[sem])
            for sem in COMPUTE:
                if self.cnt0[sem] > 0:
                    self.eng["sp"].wait_ge(self.h[sem], self.cnt0[sem])
            for sem, d in self.delta.items():
                if d > 0:
                    self.eng["sp"].sem_inc(self.h[sem], d)
            for (ename, sem) in sorted(stats["wait_pairs"]):
                if self.delta.get(sem, 0) == 0:
                    continue
                r = self.eng[ename].alloc_register(f"B_{ename}_{sem}")
                self.eng[ename].reg_mov(r, self.cnt0[sem] + self.delta[sem])
                self.bregs[(ename, sem)] = r
            for ename in ("pe", "act", "dve", "pool", "sp"):
                self.tmpreg[ename] = self.eng[ename].alloc_register(f"T_{ename}")
            by_id = stats["bufs_by_id"]
            for b in _buf_reg:
                st = by_id.get(b.bid)
                if st is None:
                    continue
                w, r_ = st
                if w is not None:
                    b.w = w
                if r_:
                    b.r = dict(r_)

    def end_body(self):
        for (ename, sem), r in self.bregs.items():
            self.eng[ename].reg_alu(r, r, self.delta[sem], mybir.AluOpType.add)

    def collect_stats(self, all_bufs):
        delta = {sem: self.cnt[sem] - self.cnt0[sem] for sem in self.cnt}
        bufs_by_id = {}
        def shift(tok):
            sem, cnt, ep = tok
            if ep is None:
                return tok
            return (sem, cnt, ep - 1)
        for b in all_bufs:
            w = shift(b.w) if b.w is not None else None
            r_ = {sem: (cnt, (ep - 1) if ep is not None else None)
                  for sem, (cnt, ep) in b.r.items()}
            bufs_by_id[b.bid] = (w, r_)
        return {"delta": delta, "wait_pairs": set(self.wait_pairs),
                "bufs_by_id": bufs_by_id}

    def _wait_tok(self, ename, tok, force=False):
        sem, cnt, ep = tok
        if cnt <= 0:
            return
        if sem == ename and sem in COMPUTE and not force:
            return
        key = (ename, sem)
        if self.epoch is None or ep is None:
            if self.waited.get(key, 0) >= cnt:
                return
            self.eng[ename].wait_ge(self.h[sem], cnt)
            self.waited[key] = cnt
            return
        self.wait_pairs.add(key)
        x = (cnt - self.cnt0[sem]) + self.delta_of(sem) * ep
        if self.lwaited.get(key, -10**9) >= x:
            return
        self.lwaited[key] = x
        if self.delta is None:
            return
        breg = self.bregs.get(key)
        if breg is None:
            return
        t = self.tmpreg[ename]
        self.eng[ename].reg_alu(t, breg, x, mybir.AluOpType.add)
        self.eng[ename].wait_ge(self.h[sem], t)

    def delta_of(self, sem):
        if self.delta is not None:
            return self.delta.get(sem, 0)
        return 0

    def _wait(self, ename, sem, cnt, force=False):
        self._wait_tok(ename, (sem, cnt, None), force)

    def _pre(self, ename, reads, writes, sreads=()):
        for b in sreads:
            if b.w is not None:
                self._wait_tok(ename, b.w, force=True)
        for b in reads:
            if b.w is not None:
                self._wait_tok(ename, b.w, force=b.small)
            if b.psum:
                for sem, (cnt, ep) in b.r.items():
                    if sem != ename:
                        self._wait_tok(ename, (sem, cnt, ep))
        for b in writes:
            if b.fresh:
                for sem, (cnt, ep) in self.legacy.items():
                    self._wait_tok(ename, (sem, cnt, ep))
                b.fresh = False
            if b.w is not None:
                self._wait_tok(ename, b.w, force=b.small)
            for sem, (cnt, ep) in b.r.items():
                self._wait_tok(ename, (sem, cnt, ep), force=b.small)

    def _post(self, tok, reads, writes):
        sem, cnt, ep = tok
        for b in reads:
            cur = b.r.get(sem)
            if cur is None or (cur[1] or 0, cur[0]) < ((ep or 0), cnt):
                b.r[sem] = (cnt, ep)
        for b in writes:
            b.w = tok
            b.r = {}

    def emit(self, ename, fn, reads=(), writes=(), sreads=(), low_prec=False):
        self._pre(ename, reads, writes, sreads)
        if low_prec:
            with self.nc.allow_low_precision(reason="rounded normalization factor"):
                instr = fn(self.eng[ename])
        else:
            instr = fn(self.eng[ename])
        self.cnt[ename] += 1
        instr.then_inc(self.h[ename], 1)
        tok = (ename, self.cnt[ename], self.epoch)
        self._post(tok, list(reads) + list(sreads), writes)
        return tok

    def dma(self, chan, ename, out, in_, reads=(), writes=()):
        cb = self.chan_buf[chan]
        self._pre(ename, reads, list(writes) + [cb])
        instr = self.eng[ename].dma_start(out=out, in_=in_)
        self.cnt[chan] += 16
        instr.then_inc(self.h[chan], 16)
        tok = (chan, self.cnt[chan], self.epoch)
        self._post(tok, reads, list(writes) + [cb])
        return tok

    def retire(self, *bufs):
        def fold(sem, cnt, ep):
            cur = self.legacy.get(sem)
            if cur is None or ((ep or 0), cnt) > ((cur[1] or 0), cur[0]):
                self.legacy[sem] = (cnt, ep)
        for b in bufs:
            if b.w is not None:
                fold(*b.w)
            for sem, (cnt, ep) in b.r.items():
                fold(sem, cnt, ep)


def build_program(layers=(0, 1), nseq=NSEQ, taps=(),
                  phases=("ln", "attn", "ffn", "gate", "l1attn", "moe"),
                  loop=False, loop_stats=None):
    _buf_seq[0] = 0
    _buf_reg.clear()
    _uid[0] = 0
    nc = bass.Bass("TRN2", target_bir_lowering=False)
    dt = nc.dram_tensor
    xin = dt("x", [nseq, D, S], F32, kind="ExternalInput").ap()
    out = dt("out", [nseq, D, S], F32, kind="ExternalOutput").ap()
    ins = {}

    def din(name, shape):
        ins[name] = dt(name, shape, F32, kind="ExternalInput").ap()
        return ins[name]

    if 0 in layers:
        din("wqp", [8, 128, 1024]); din("wkp", [8, 128, 1024]); din("wvp", [8, 128, 1024])
        din("wo", [8, 128, 1024])
        din("w1f", [32, 128, 1024]); din("w2f", [8, 128, 4096])
        din("b1c", [128, 32]); din("b2c", [128, 8])
        din("thc", [128, 32]); din("wrc", [128, 32]); din("wic", [128, 32])
    if 1 in layers:
        din("wq1", [8, 128, 128]); din("wk1", [8, 128, 128]); din("wv1", [8, 128, 128])
        din("w11", [8, 128, 1024]); din("w21", [8, 128, 1024])
        din("gnw", [128, 8]); din("gnb", [128, 8])
        din("gw", [8, 128, 8]); din("gb", [1, 8])
        din("mw1", [8, 32, 128, 1024]); din("mw2", [8, 8, 128, 4096])
        din("mb1", [8, 128, 32]); din("mb2", [8, 128, 8])
    tapd = {}
    for tname in taps:
        tapd[tname] = dt("tap_" + tname, [NCH, 128, S], F32, kind="ExternalOutput").ap()

    es = contextlib.ExitStack()
    with es:
        k = K(nc, es)
        es.enter_context(nc.Block())

        def sbg(shape, dtype=F32, st=es):
            _uid[0] += 1
            return st.enter_context(nc.sbuf_tensor(f"t{_uid[0]}", shape, dtype))

        def psg(shape, st, dtype=F32):
            _uid[0] += 1
            return st.enter_context(nc.psum_tensor(f"p{_uid[0]}", shape, dtype))

        B = Buf

        X = [sbg([128, S]) for _ in range(NCH)]
        Xb = [B() for _ in range(NCH)]
        XN = [sbg([128, S], BF16) for _ in range(NCH)]
        XNb = [B() for _ in range(NCH)]
        ones_f = sbg([128, 1]); ones_f_b = B()
        ones1_f = sbg([1, 128]); ones1_f_b = B()
        onesH0 = sbg([128, 1], BF16); onesH0_b = B()
        onesH1 = sbg([128, 1], BF16); onesH1_b = B()
        ones1_b16 = sbg([1, 128], BF16); ones1_b16_b = B()
        c01 = sbg([128, 1]); c01_b = B()
        c125 = sbg([128, 1]); c125_b = B()
        ceps = sbg([128, 1]); ceps_b = B()

        k.emit("dve", lambda e: e.memset(ones_f[:], 1.0), writes=[ones_f_b])
        k.emit("dve", lambda e: e.memset(ones1_f[:], 1.0), writes=[ones1_f_b])
        k.emit("dve", lambda e: e.memset(onesH0[0:64, :], 1.0), writes=[onesH0_b])
        k.emit("dve", lambda e: e.memset(onesH0[64:128, :], 0.0), writes=[onesH0_b])
        k.emit("dve", lambda e: e.memset(onesH1[0:64, :], 0.0), writes=[onesH1_b])
        k.emit("dve", lambda e: e.memset(onesH1[64:128, :], 1.0), writes=[onesH1_b])
        k.emit("dve", lambda e: e.memset(ones1_b16[:], 1.0), writes=[ones1_b16_b])
        k.emit("dve", lambda e: e.memset(c01[:], 0.1), writes=[c01_b])
        k.emit("dve", lambda e: e.memset(c125[:], 0.125), writes=[c125_b])
        k.emit("dve", lambda e: e.memset(ceps[:], EPS), writes=[ceps_b])

        def explode(srct, srcb, K, st=es):
            tiles = [sbg([128, 1], st=st) for _ in range(K)]
            bufs = [B(small=True) for _ in range(K)]
            for j in range(K):
                k.emit("dve", lambda e, j=j: e.tensor_copy(tiles[j][:], srct[:, j:j + 1]),
                       reads=[srcb], writes=[bufs[j]])
            return tiles, bufs

        if 0 in layers:
            b1c_s = sbg([128, 32]); b1c_b = B()
            b2c_s = sbg([128, 8]); b2c_b = B()
            Acol = sbg([128, 32]); Acol_b = B(small=True)
            Bcol = sbg([128, 32]); Bcol_b = B(small=True)
            BSIN = sbg([128, 32]); BSIN_b = B(small=True)
            BCOS = sbg([128, 32]); BCOS_b = B(small=True)
            with contextlib.ExitStack() as qs:
                thc_s = sbg([128, 32], st=qs); thc_b = B()
                wrc_s = sbg([128, 32], st=qs); wrc_b = B()
                wic_s = sbg([128, 32], st=qs); wic_b = B()
                t0 = sbg([128, 32], st=qs); t0b = B(small=True)
                t1 = sbg([128, 32], st=qs); t1b = B(small=True)
                t2 = sbg([128, 32], st=qs); t2b = B(small=True)
                ti = sbg([128, 32], I32, st=qs); tib = B(small=True)
                sinth = sbg([128, 32], st=qs); sinth_b = B(small=True)
                costh = sbg([128, 32], st=qs); costh_b = B(small=True)
                cpi2 = sbg([128, 1], st=qs); cpi2_b = B(small=True)
                cm1 = sbg([128, 1], st=qs); cm1_b = B(small=True)
                k.dma("cv", "sp", b1c_s[:], ins["b1c"], writes=[b1c_b])
                k.dma("cv", "sp", thc_s[:], ins["thc"], writes=[thc_b])
                k.dma("cv", "sp", wrc_s[:], ins["wrc"], writes=[wrc_b])
                k.dma("cv", "sp", wic_s[:], ins["wic"], writes=[wic_b])
                k.dma("cv", "sp", b2c_s[:], ins["b2c"], writes=[b2c_b])
                k.emit("dve", lambda e: e.tensor_scalar_mul(t0[:], thc_s[:], float(1.0 / (2 * PI))),
                       reads=[thc_b], writes=[t0b])
                k.emit("dve", lambda e: e.tensor_copy(ti[:], t0[:]), reads=[t0b], writes=[tib])
                k.emit("dve", lambda e: e.tensor_copy(t0[:], ti[:]), reads=[tib], writes=[t0b])
                k.emit("dve", lambda e: e.scalar_tensor_tensor(
                    t1[:], t0[:], float(-2 * PI), thc_s[:], ALU.mult, ALU.add),
                    reads=[t0b, thc_b], writes=[t1b])
                k.emit("dve", lambda e: e.tensor_scalar(t0[:], t1[:], PI, None, ALU.is_gt),
                       reads=[t1b], writes=[t0b])
                k.emit("dve", lambda e: e.scalar_tensor_tensor(
                    t1[:], t0[:], float(-2 * PI), t1[:], ALU.mult, ALU.add),
                    reads=[t0b, t1b], writes=[t1b])
                k.emit("dve", lambda e: e.tensor_scalar(t0[:], t1[:], -PI, None, ALU.is_lt),
                       reads=[t1b], writes=[t0b])
                k.emit("dve", lambda e: e.scalar_tensor_tensor(
                    t1[:], t0[:], float(2 * PI), t1[:], ALU.mult, ALU.add),
                    reads=[t0b, t1b], writes=[t1b])
                k.emit("act", lambda e: e.activation(sinth[:], t1[:], AF.Sin),
                       reads=[t1b], writes=[sinth_b])
                k.emit("act", lambda e: e.activation(t2[:], t1[:], AF.Abs),
                       reads=[t1b], writes=[t2b])
                k.emit("dve", lambda e: e.memset(cpi2[:], PI / 2), writes=[cpi2_b])
                k.emit("dve", lambda e: e.memset(cm1[:], -1.0), writes=[cm1_b])
                k.emit("act", lambda e: e.activation(costh[:], t2[:], AF.Sin, bias=cpi2[:], scale=cm1[:]),
                       reads=[t2b], sreads=[cpi2_b, cm1_b], writes=[costh_b])
                k.emit("dve", lambda e: e.tensor_mul(t0[:], costh[:], wrc_s[:]),
                       reads=[costh_b, wrc_b], writes=[t0b])
                k.emit("dve", lambda e: e.tensor_mul(t2[:], sinth[:], wic_s[:]),
                       reads=[sinth_b, wic_b], writes=[t2b])
                k.emit("dve", lambda e: e.tensor_add(t0[:], t0[:], t2[:]),
                       reads=[t0b, t2b], writes=[t0b])
                k.emit("dve", lambda e: e.tensor_mul(t0[:], t0[:], wic_s[:]),
                       reads=[t0b, wic_b], writes=[t0b])
                k.emit("dve", lambda e: e.tensor_scalar_mul(t0[:], t0[:], 0.1),
                       reads=[t0b], writes=[t0b])
                k.emit("dve", lambda e: e.tensor_mul(Acol[:], t0[:], sinth[:]),
                       reads=[t0b, sinth_b], writes=[Acol_b])
                k.emit("dve", lambda e: e.tensor_mul(Bcol[:], t0[:], costh[:]),
                       reads=[t0b, costh_b], writes=[Bcol_b])
                k.emit("dve", lambda e: e.tensor_scalar_mul(BSIN[:], b1c_s[:], 0.1),
                       reads=[b1c_b], writes=[BSIN_b])
                k.emit("dve", lambda e: e.tensor_scalar(BCOS[:], b1c_s[:], 0.1, PI / 2, ALU.mult, ALU.add),
                       reads=[b1c_b], writes=[BCOS_b])
                k.retire(thc_b, wrc_b, wic_b, t0b, t1b, t2b, tib, sinth_b, costh_b, cpi2_b, cm1_b)
            B1Cc, B1Cc_b = explode(b1c_s, b1c_b, 32)
            if "qc2" in tapd:
                k.dma("ct", "pool", tapd["qc2"][0][:, 0:32], Acol[:], reads=[Acol_b])
            B2Cc, B2Cc_b = explode(b2c_s, b2c_b, 8)
            Ac, Ac_b = explode(Acol, Acol_b, 32)
            Bc, Bc_b = explode(Bcol, Bcol_b, 32)
            BSINc, BSINc_b = explode(BSIN, BSIN_b, 32)
            BCOSc, BCOSc_b = explode(BCOS, BCOS_b, 32)
            if "qc2" in tapd:
                with nc.allow_non_contiguous_dma(reason="debug tap"):
                    for j in range(8):
                        k.dma("ct", "pool", tapd["qc2"][1][:, j:j+1], Ac[j][:], reads=[Ac_b[j]])
                        k.dma("ct", "pool", tapd["qc2"][2][:, j:j+1], BSINc[j][:], reads=[BSINc_b[j]])

        if 1 in layers:
            gnw_s = sbg([128, 8]); gnw_b = B()
            gnb_s = sbg([128, 8]); gnb_b = B()
            mb1_s = [sbg([128, 32]) for _ in range(NE)]
            mb1_b = [B() for _ in range(NE)]
            mb2_s = [sbg([128, 8]) for _ in range(NE)]
            mb2_b = [B() for _ in range(NE)]
            IDF = sbg([128, 128]); IDF_b = B()
            TRIB = sbg([128, 128], BF16); TRIB_b = B()
            SEL = sbg([8, 1024], BF16); SEL_b = B()
            GNEG = [sbg([128, 1]) for _ in range(H1)]
            GNEG_b = [B(small=True) for _ in range(H1)]
            GBS = sbg([128, 8]); GBS_b = B()
            combT = sbg([8, S], BF16); combTb = B()
            k.dma("cv", "sp", gnw_s[:], ins["gnw"], writes=[gnw_b])
            k.dma("cv", "sp", gnb_s[:], ins["gnb"], writes=[gnb_b])
            for e_ in range(NE):
                k.dma("cv", "sp", mb1_s[e_][:], ins["mb1"][e_], writes=[mb1_b[e_]])
                k.dma("cv", "sp", mb2_s[e_][:], ins["mb2"][e_], writes=[mb2_b[e_]])
            GNWc, GNWc_b = explode(gnw_s, gnw_b, 8)
            GNBc, GNBc_b = explode(gnb_s, gnb_b, 8)
            with contextlib.ExitStack() as qs:
                gbrow = sbg([1, 8], st=qs); gbrow_b = B()
                IOTC = sbg([128, 1], st=qs); IOTC_b = B()
                tI = sbg([128, 128], st=qs); tIb = B()
                tT = sbg([128, 128], st=qs); tTb = B()
                tmpc = sbg([128, 1], st=qs); tmpc_b = B(small=True)
                k.dma("cv", "sp", gbrow[:], ins["gb"], writes=[gbrow_b])
                k.emit("pool", lambda e: e.iota(tI[:], pattern=[[1, 128]], base=0,
                                                channel_multiplier=-1,
                                                allow_small_or_imprecise_dtypes=True), writes=[tIb])
                k.emit("dve", lambda e: e.tensor_scalar(IDF[:], tI[:], 0.0, None, ALU.is_equal),
                       reads=[tIb], writes=[IDF_b])
                k.emit("dve", lambda e: e.tensor_scalar(tT[:], tI[:], 0.0, None, ALU.is_ge),
                       reads=[tIb], writes=[tTb])
                k.emit("dve", lambda e: e.tensor_copy(TRIB[:], tT[:]), reads=[tTb], writes=[TRIB_b])
                k.emit("pool", lambda e: e.iota(IOTC[:], pattern=[[1, 1]], base=0,
                                                channel_multiplier=1,
                                                allow_small_or_imprecise_dtypes=True), writes=[IOTC_b])
                SELF = sbg([8, 1024], st=qs); SELF_b = B()
                selap = SELF[:].rearrange("k (e p) -> k e p", p=128)
                k.emit("pool", lambda e: e.iota(selap, pattern=[[1, 8], [0, 128]], base=0,
                                                channel_multiplier=-1,
                                                allow_small_or_imprecise_dtypes=True), writes=[SELF_b])
                k.emit("dve", lambda e: e.tensor_scalar(SELF[:], SELF[:], 0.0, None, ALU.is_equal),
                       reads=[SELF_b], writes=[SELF_b])
                k.emit("dve", lambda e: e.tensor_copy(SEL[:], SELF[:]),
                       reads=[SELF_b], writes=[SEL_b])
                for h in range(H1):
                    lg = LGAM[h]
                    k.emit("dve", lambda e: e.memset(tmpc[:], -lg), writes=[tmpc_b])
                    k.emit("act", lambda e: e.activation(GNEG[h][:], IOTC[:], AF.Exp, scale=tmpc[:]),
                           reads=[IOTC_b], sreads=[tmpc_b], writes=[GNEG_b[h]])
                with contextlib.ExitStack() as ps:
                    P_gb = psg([128, 8], ps); P_gb_b = B(psum=True)
                    k.emit("pe", lambda e: e.matmul(P_gb[:], ones1_f[:], gbrow[:], start=True, stop=True),
                           reads=[ones1_f_b, gbrow_b], writes=[P_gb_b])
                    k.emit("act", lambda e: e.copy(GBS[:], P_gb[:]), reads=[P_gb_b], writes=[GBS_b])
                    k.retire(P_gb_b)
                k.retire(gbrow_b, IOTC_b, tIb, tTb, tmpc_b, SELF_b)

        def tap(name, src_list, src_bufs):
            if name not in tapd:
                return
            for c in range(NCH):
                k.dma("ct", "pool", tapd[name][c], src_list[c][:], reads=[src_bufs[c]])

        def emit_load_x(s):
            for c in range(NCH):
                k.dma("cx", "sp", X[c][:], xin[s][c * 128:(c + 1) * 128, :], writes=[Xb[c]])

        def emit_store_x(s):
            for c in range(NCH):
                k.dma("cs0" if c % 2 == 0 else "cs1", "sp",
                      out[s][c * 128:(c + 1) * 128, :], X[c][:], reads=[Xb[c]])

        def emit_layernorm():
            with contextlib.ExitStack() as fs:
                SQ = [sbg([128, S], st=fs) for _ in range(2)]
                SQb = [B(), B()]
                R_mu = sbg([1, S], st=fs); R_mu_b = B()
                R_m2 = sbg([1, S], st=fs); R_m2_b = B()
                R_rs = sbg([1, S], st=fs); R_rs_b = B()
                T1 = [sbg([128, S], st=fs) for _ in range(2)]
                T1b = [B(), B()]
                with contextlib.ExitStack() as ps:
                    P_s1 = psg([1, S], ps); P_s1_b = B(psum=True)
                    P_s2 = psg([1, S], ps); P_s2_b = B(psum=True)
                    for c in range(NCH):
                        q = SQ[c % 2]; qb = SQb[c % 2]
                        k.emit("act", lambda e: e.activation(q[:], X[c][:], AF.Square),
                               reads=[Xb[c]], writes=[qb])
                        for nb in range(NB):
                            k.emit("pe", lambda e: e.matmul(
                                P_s1[:, nb * 512:(nb + 1) * 512], ones_f[:],
                                X[c][:, nb * 512:(nb + 1) * 512],
                                start=(c == 0), stop=(c == NCH - 1)),
                                reads=[ones_f_b, Xb[c]], writes=[P_s1_b])
                            k.emit("pe", lambda e: e.matmul(
                                P_s2[:, nb * 512:(nb + 1) * 512], ones_f[:],
                                q[:, nb * 512:(nb + 1) * 512],
                                start=(c == 0), stop=(c == NCH - 1)),
                                reads=[ones_f_b, qb], writes=[P_s2_b])
                    k.emit("dve", lambda e: e.tensor_scalar_mul(R_mu[:], P_s1[:], 1.0 / D),
                           reads=[P_s1_b], writes=[R_mu_b])
                    k.emit("dve", lambda e: e.tensor_scalar_mul(R_m2[:], P_s2[:], 1.0 / D),
                           reads=[P_s2_b], writes=[R_m2_b])
                    k.emit("dve", lambda e: e.tensor_mul(R_rs[:], R_mu[:], R_mu[:]),
                           reads=[R_mu_b], writes=[R_rs_b])
                    k.emit("dve", lambda e: e.tensor_sub(R_m2[:], R_m2[:], R_rs[:]),
                           reads=[R_m2_b, R_rs_b], writes=[R_m2_b])
                    k.emit("act", lambda e: e.activation(R_m2[:], R_m2[:], AF.Sqrt, bias=ceps[0:1, :]),
                           reads=[R_m2_b], sreads=[ceps_b], writes=[R_m2_b])
                    k.emit("dve", lambda e: e.reciprocal(R_rs[:], R_m2[:]),
                           reads=[R_m2_b], writes=[R_rs_b])
                    k.retire(P_s1_b, P_s2_b)
                with contextlib.ExitStack() as ps:
                    P_bm = psg([128, S], ps); P_bm_b = B(psum=True)
                    P_br = psg([128, S], ps); P_br_b = B(psum=True)
                    for nb in range(NB):
                        k.emit("pe", lambda e: e.matmul(
                            P_bm[:, nb * 512:(nb + 1) * 512], ones1_f[:],
                            R_mu[0:1, nb * 512:(nb + 1) * 512], start=True, stop=True),
                            reads=[ones1_f_b, R_mu_b], writes=[P_bm_b])
                        k.emit("pe", lambda e: e.matmul(
                            P_br[:, nb * 512:(nb + 1) * 512], ones1_f[:],
                            R_rs[0:1, nb * 512:(nb + 1) * 512], start=True, stop=True),
                            reads=[ones1_f_b, R_rs_b], writes=[P_br_b])
                    for c in range(NCH):
                        t = T1[c % 2]; tb = T1b[c % 2]
                        k.emit("dve", lambda e: e.tensor_sub(t[:], X[c][:], P_bm[:]),
                               reads=[Xb[c], P_bm_b], writes=[tb])
                        k.emit("dve", lambda e: e.tensor_mul(XN[c][:], t[:], P_br[:]),
                               reads=[tb, P_br_b], writes=[XNb[c]])
                    k.retire(P_bm_b, P_br_b)
                k.retire(SQb[0], SQb[1], T1b[0], T1b[1], R_mu_b, R_m2_b, R_rs_b)

        def emit_l0_attn():
            with contextlib.ExitStack() as fs:
                WOD = [sbg([128, 1024], BF16, st=fs) for _ in range(2)]
                WODb = [B(), B()]
                O_all = [sbg([128, S], BF16, st=fs) for _ in range(NCH)]
                O_allb = [B() for _ in range(NCH)]
                WP = [[sbg([128, 1024], BF16, st=fs) for _ in range(3)] for _ in range(2)]
                WPb = [[B() for _ in range(3)] for _ in range(2)]
                QA = [sbg([65, S], BF16, st=fs) for _ in range(2)]
                QAb = [B(), B()]
                KA = [sbg([65, S], BF16, st=fs) for _ in range(2)]
                KAb = [B(), B()]
                VA = [sbg([128, 16 * 65], BF16, st=fs) for _ in range(2)]
                VAb = [B(), B()]
                SQH = sbg([128, S], BF16, st=fs); SQHb = B()
                KM = [sbg([1, 1], st=fs) for _ in range(2)]
                KMb = [B(small=True), B(small=True)]
                PTS = [sbg([128, 512], BF16, st=fs) for _ in range(2)]
                PTSb = [B(), B()]
                R_inv = sbg([1, S], BF16, st=fs); R_invb = B()
                BCS = sbg([64, S], BF16, st=fs); BCSb = B()

                def load_pair(hp):
                    i = hp % 2
                    ch = "cw0" if i == 0 else "cw1"
                    for w, nm in enumerate(["wqp", "wkp", "wvp"]):
                        k.dma(ch, "pool", WP[i][w][:], ins[nm][hp], writes=[WPb[i][w]])

                load_pair(0)
                for hp in range(H0 // 2):
                    i = hp % 2
                    if hp + 1 < H0 // 2:
                        load_pair(hp + 1)
                    with contextlib.ExitStack() as ps:
                        P_A = psg([128, S], ps); P_A_b = B(psum=True)
                        P_NR = psg([1, S], ps); P_NR_b = B(psum=True)
                        for kc in range(NCH):
                            for nb in range(NB):
                                k.emit("pe", lambda e: e.matmul(
                                    P_A[:, nb * 512:(nb + 1) * 512],
                                    WP[i][1][:, kc * 128:(kc + 1) * 128],
                                    XN[kc][:, nb * 512:(nb + 1) * 512],
                                    start=(kc == 0), stop=(kc == NCH - 1)),
                                    reads=[WPb[i][1], XNb[kc]], writes=[P_A_b])
                        k.emit("act", lambda e: e.copy(KA[0][0:64, :], P_A[0:64, :]),
                               reads=[P_A_b], writes=[KAb[0]])
                        k.emit("act", lambda e: e.copy(KA[1][0:64, :], P_A[64:128, :]),
                               reads=[P_A_b], writes=[KAb[1]])
                        k.emit("dve", lambda e: e.memset(KA[0][64:65, :], -1.0), writes=[KAb[0]])
                        k.emit("dve", lambda e: e.memset(KA[1][64:65, :], -1.0), writes=[KAb[1]])
                        k.emit("act", lambda e: e.activation(SQH[:], P_A[:], AF.Square),
                               reads=[P_A_b], writes=[SQHb])
                        for j in range(2):
                            oh, ohb = (onesH0, onesH0_b) if j == 0 else (onesH1, onesH1_b)
                            for nb in range(NB):
                                k.emit("pe", lambda e: e.matmul(
                                    P_NR[:, nb * 512:(nb + 1) * 512], oh[:],
                                    SQH[:, nb * 512:(nb + 1) * 512], start=True, stop=True),
                                    reads=[ohb, SQHb], writes=[P_NR_b])
                            k.emit("dve", lambda e: e.reduce_max(KM[j][:], P_NR[:], axis=AX.X),
                                   reads=[P_NR_b], writes=[KMb[j]])
                        for kc in range(NCH):
                            for nb in range(NB):
                                k.emit("pe", lambda e: e.matmul(
                                    P_A[:, nb * 512:(nb + 1) * 512],
                                    WP[i][0][:, kc * 128:(kc + 1) * 128],
                                    XN[kc][:, nb * 512:(nb + 1) * 512],
                                    start=(kc == 0), stop=(kc == NCH - 1)),
                                    reads=[WPb[i][0], XNb[kc]], writes=[P_A_b])
                        k.emit("act", lambda e: e.activation(QA[0][0:64, :], P_A[0:64, :],
                                                             AF.Identity, scale=c125[0:64, :]),
                               reads=[P_A_b], sreads=[c125_b], writes=[QAb[0]])
                        k.emit("act", lambda e: e.activation(QA[1][0:64, :], P_A[64:128, :],
                                                             AF.Identity, scale=c125[64:128, :]),
                               reads=[P_A_b], sreads=[c125_b], writes=[QAb[1]])
                        k.emit("act", lambda e: e.activation(SQH[:], P_A[:], AF.Square, scale=c125[:]),
                               reads=[P_A_b], sreads=[c125_b], writes=[SQHb])
                        for j in range(2):
                            oh, ohb = (onesH0, onesH0_b) if j == 0 else (onesH1, onesH1_b)
                            for nb in range(NB):
                                k.emit("pe", lambda e: e.matmul(
                                    P_NR[:, nb * 512:(nb + 1) * 512], oh[:],
                                    SQH[:, nb * 512:(nb + 1) * 512], start=True, stop=True),
                                    reads=[ohb, SQHb], writes=[P_NR_b])
                            k.emit("act", lambda e: e.activation(QA[j][64:65, :], P_NR[:],
                                                                 AF.Sqrt, scale=KM[j][0:1, :]),
                                   reads=[P_NR_b], sreads=[KMb[j]], writes=[QAb[j]])
                        k.retire(P_A_b, P_NR_b)
                    with contextlib.ExitStack() as ps:
                        P_V = psg([128, 128], ps); P_V_b = B(psum=True)
                        k.emit("dve", lambda e: e.memset(VA[0][:, 64:16 * 65:65], 1.0), writes=[VAb[0]])
                        k.emit("dve", lambda e: e.memset(VA[1][:, 64:16 * 65:65], 1.0), writes=[VAb[1]])
                        for mc in range(16):
                            for kc in range(NCH):
                                k.emit("pe", lambda e: e.matmul(
                                    P_V[:], XN[kc][:, mc * 128:(mc + 1) * 128],
                                    WP[i][2][:, kc * 128:(kc + 1) * 128],
                                    start=(kc == 0), stop=(kc == NCH - 1)),
                                    reads=[XNb[kc], WPb[i][2]], writes=[P_V_b])
                            k.emit("act", lambda e: e.copy(VA[0][:, mc * 65:mc * 65 + 64], P_V[:, 0:64]),
                                   reads=[P_V_b], writes=[VAb[0]])
                            k.emit("act", lambda e: e.copy(VA[1][:, mc * 65:mc * 65 + 64], P_V[:, 64:128]),
                                   reads=[P_V_b], writes=[VAb[1]])
                        k.retire(P_V_b)
                    for j in range(2):
                        with contextlib.ExitStack() as hs:
                            P_O = psg([65, S], hs); P_O_b = B(psum=True)
                            with contextlib.ExitStack() as ts:
                                P_T = psg([128, S], ts); P_T_b = B(psum=True)
                                for mc in range(16):
                                    for nb in range(NB):
                                        k.emit("pe", lambda e: e.matmul(
                                            P_T[:, nb * 512:(nb + 1) * 512],
                                            KA[j][:, mc * 128:(mc + 1) * 128],
                                            QA[j][:, nb * 512:(nb + 1) * 512],
                                            start=True, stop=True),
                                            reads=[KAb[j], QAb[j]], writes=[P_T_b])
                                    for nb in range(NB):
                                        pt = PTS[nb % 2]; ptb = PTSb[nb % 2]
                                        k.emit("act", lambda e: e.activation(
                                            pt[:], P_T[:, nb * 512:(nb + 1) * 512], AF.Exp),
                                            reads=[P_T_b], writes=[ptb])
                                        k.emit("pe", lambda e: e.matmul(
                                            P_O[:, nb * 512:(nb + 1) * 512],
                                            VA[j][:, mc * 65:(mc + 1) * 65], pt[:],
                                            start=(mc == 0), stop=(mc == 15)),
                                            reads=[VAb[j], ptb], writes=[P_O_b])
                                k.retire(P_T_b)
                            k.emit("dve", lambda e: e.reciprocal(R_inv[:], P_O[64:65, :]),
                                   reads=[P_O_b], writes=[R_invb], low_prec=True)
                            with contextlib.ExitStack() as ts:
                                P_Bc = psg([64, S], ts); P_B_b = B(psum=True)
                                for nb in range(NB):
                                    k.emit("pe", lambda e: e.matmul(
                                        P_Bc[:, nb * 512:(nb + 1) * 512], ones1_b16[0:1, 0:64],
                                        R_inv[0:1, nb * 512:(nb + 1) * 512], start=True, stop=True),
                                        reads=[ones1_b16_b, R_invb], writes=[P_B_b])
                                k.emit("act", lambda e: e.copy(BCS[:], P_Bc[:]),
                                       reads=[P_B_b], writes=[BCSb])
                                ro = j * 64
                                k.emit("dve", lambda e: e.tensor_mul(
                                    O_all[hp][ro:ro + 64, :], P_O[0:64, :], BCS[:]),
                                    reads=[P_O_b, BCSb], writes=[O_allb[hp]])
                                k.retire(P_B_b)
                            k.retire(P_O_b)
                with contextlib.ExitStack() as ps:
                    P_P = [psg([128, S], ps) for _ in range(2)]
                    P_P_b = [B(psum=True), B(psum=True)]
                    k.dma("cw2", "pool", WOD[0][:], ins["wo"][0], writes=[WODb[0]])
                    for dc in range(NCH):
                        i2 = dc % 2
                        if dc + 1 < NCH:
                            k.dma("cw2" if (dc + 1) % 2 == 0 else "cw3", "pool",
                                  WOD[(dc + 1) % 2][:], ins["wo"][dc + 1],
                                  writes=[WODb[(dc + 1) % 2]])
                        pp = P_P[dc % 2]; ppb = P_P_b[dc % 2]
                        for oc in range(NCH):
                            for nb in range(NB):
                                k.emit("pe", lambda e: e.matmul(
                                    pp[:, nb * 512:(nb + 1) * 512],
                                    WOD[i2][:, oc * 128:(oc + 1) * 128],
                                    O_all[oc][:, nb * 512:(nb + 1) * 512],
                                    start=(oc == 0), stop=(oc == NCH - 1)),
                                    reads=[WODb[i2], O_allb[oc]], writes=[ppb])
                        k.emit("dve", lambda e: e.tensor_add(X[dc][:], X[dc][:], pp[:]),
                               reads=[Xb[dc], ppb], writes=[Xb[dc]])
                    k.retire(P_P_b[0], P_P_b[1])
                k.retire(*WODb, *O_allb, *QAb, *KAb, *VAb, SQHb, *KMb,
                         *PTSb, R_invb, BCSb, *WPb[0], *WPb[1])

        def emit_l0_ffn(h_only=False, quantum=True, nfb=NFB, nth=2):
            with contextlib.ExitStack() as fs:
                Hsb = [sbg([128, 1024], BF16, st=fs) for _ in range(NFB)]
                Hb = [B() for _ in range(NFB)]
                WF = [sbg([128, 1024], BF16, st=fs) for _ in range(2)]
                WFb = [B(), B()]
                W2D = [sbg([128, 4096], BF16, st=fs)]
                W2Db = [B()]
                SC = sbg([128, 1024], BF16, st=fs); SCb = B()
                SS = sbg([128, 1024], BF16, st=fs); SSb = B()
                UU = sbg([128, 1024], st=fs); UUb = B()
                with contextlib.ExitStack() as ps:
                    P_H = [psg([128, 1024], ps) for _ in range(2)]
                    P_H_b = [B(psum=True), B(psum=True)]
                    P_Y = [psg([128, 1024], ps) for _ in range(2)]
                    P_Y_b = [B(psum=True), B(psum=True)]
                    for th in range(nth):
                        t0 = th * 1024
                        k.dma("cw0", "pool", WF[0][:], ins["w1f"][0], writes=[WFb[0]])
                        for fb in range(nfb):
                            i = fb % 2
                            if fb + 1 < nfb:
                                k.dma("cw0" if (fb + 1) % 2 == 0 else "cw1", "pool",
                                      WF[(fb + 1) % 2][:], ins["w1f"][fb + 1],
                                      writes=[WFb[(fb + 1) % 2]])
                            ph = P_H[i]; phb = P_H_b[i]
                            for kc in range(NCH):
                                for nb in range(2):
                                    k.emit("pe", lambda e: e.matmul(
                                        ph[:, nb * 512:(nb + 1) * 512],
                                        WF[i][:, kc * 128:(kc + 1) * 128],
                                        XN[kc][:, t0 + nb * 512:t0 + (nb + 1) * 512],
                                        start=(kc == 0), stop=(kc == NCH - 1)),
                                        reads=[WFb[i], XNb[kc]], writes=[phb])
                            if quantum:
                                k.emit("act", lambda e: e.activation(
                                    SC[:], ph[:], AF.Sin, bias=BCOSc[fb][:], scale=c01[:]),
                                    reads=[phb], sreads=[BCOSc_b[fb], c01_b], writes=[SCb])
                                k.emit("act", lambda e: e.activation(
                                    SS[:], ph[:], AF.Sin, bias=BSINc[fb][:], scale=c01[:]),
                                    reads=[phb], sreads=[BSINc_b[fb], c01_b], writes=[SSb])
                                k.emit("dve", lambda e: e.scalar_tensor_tensor(
                                    UU[:], SC[:], Ac[fb][:], ph[:], ALU.mult, ALU.add),
                                    reads=[SCb, phb], sreads=[Ac_b[fb]], writes=[UUb])
                                k.emit("dve", lambda e: e.scalar_tensor_tensor(
                                    UU[:], SS[:], Bc[fb][:], UU[:], ALU.mult, ALU.add),
                                    reads=[SSb, UUb], sreads=[Bc_b[fb]], writes=[UUb])
                                k.emit("act", lambda e: e.activation(
                                    Hsb[fb][:], UU[:], AF.Relu, bias=B1Cc[fb][:]),
                                    reads=[UUb], sreads=[B1Cc_b[fb]], writes=[Hb[fb]])
                            else:
                                k.emit("act", lambda e: e.activation(
                                    Hsb[fb][:], ph[:], AF.Relu, bias=B1Cc[fb][:]),
                                    reads=[phb], sreads=[B1Cc_b[fb]], writes=[Hb[fb]])
                        if h_only:
                            continue
                        if th == 0 and "hsb" in tapd:
                            for c8 in range(NCH):
                                k.dma("ct", "pool", tapd["hsb"][c8][:, 0:1024], Hsb[c8][:],
                                      reads=[Hb[c8]])
                        for dc in range(NCH):
                            i = dc % 2
                            k.dma("cw2", "pool", W2D[0][:], ins["w2f"][dc], writes=[W2Db[0]])
                            py = P_Y[i]; pyb = P_Y_b[i]
                            for fb in range(NFB):
                                for nb in range(2):
                                    k.emit("pe", lambda e: e.matmul(
                                        py[:, nb * 512:(nb + 1) * 512],
                                        W2D[0][:, fb * 128:(fb + 1) * 128],
                                        Hsb[fb][:, nb * 512:(nb + 1) * 512],
                                        start=(fb == 0), stop=(fb == NFB - 1)),
                                        reads=[W2Db[0], Hb[fb]], writes=[pyb])
                            k.emit("dve", lambda e: e.scalar_tensor_tensor(
                                X[dc][:, t0:t0 + 1024], py[:], B2Cc[dc][:],
                                X[dc][:, t0:t0 + 1024], ALU.add, ALU.add),
                                reads=[pyb, Xb[dc]], sreads=[B2Cc_b[dc]], writes=[Xb[dc]])
                    k.retire(P_H_b[0], P_H_b[1], P_Y_b[0], P_Y_b[1])
                k.retire(*Hb, *WFb, *W2Db, SCb, SSb, UUb)

        def emit_gate():
            with contextlib.ExitStack() as fs:
                GWS = [sbg([128, 8], BF16, st=fs) for _ in range(NCH)]
                GWSb = [B() for _ in range(NCH)]
                for kc in range(NCH):
                    k.dma("cw4", "pool", GWS[kc][:], ins["gw"][kc], writes=[GWSb[kc]])
                GL = sbg([128, 8], st=fs); GLb = B(small=True)
                GL2 = sbg([128, 8], st=fs); GL2b = B(small=True)
                M1 = sbg([128, 8], st=fs); M1b = B(small=True)
                M2 = sbg([128, 8], st=fs); M2b = B(small=True)
                V1 = sbg([128, 1], st=fs); V1b = B(small=True)
                V2 = sbg([128, 1], st=fs); V2b = B(small=True)
                W1g = sbg([128, 1], st=fs); W1gb = B(small=True)
                W2g = sbg([128, 1], st=fs); W2gb = B(small=True)
                CMB = sbg([128, 8], st=fs); CMBb = B(small=True)
                with contextlib.ExitStack() as ps:
                    P_G = psg([128, 8], ps); P_G_b = B(psum=True)
                    P_TC = psg([8, 128], ps); P_TC_b = B(psum=True)
                    for tc in range(16):
                        for kc in range(NCH):
                            k.emit("pe", lambda e: e.matmul(
                                P_G[:], XN[kc][:, tc * 128:(tc + 1) * 128], GWS[kc][:],
                                start=(kc == 0), stop=(kc == NCH - 1)),
                                reads=[XNb[kc], GWSb[kc]], writes=[P_G_b])
                        k.emit("dve", lambda e: e.tensor_add(GL[:], P_G[:], GBS[:]),
                               reads=[P_G_b, GBS_b], writes=[GLb])
                        k.emit("dve", lambda e: e.reduce_max(V1[:], GL[:], axis=AX.X),
                               reads=[GLb], writes=[V1b])
                        k.emit("dve", lambda e: e.tensor_scalar(M1[:], GL[:], V1[:], None, ALU.is_ge),
                               reads=[GLb], sreads=[V1b], writes=[M1b])
                        k.emit("dve", lambda e: e.scalar_tensor_tensor(
                            GL2[:], M1[:], -1e30, GL[:], ALU.mult, ALU.add),
                            reads=[M1b, GLb], writes=[GL2b])
                        k.emit("dve", lambda e: e.reduce_max(V2[:], GL2[:], axis=AX.X),
                               reads=[GL2b], writes=[V2b])
                        k.emit("dve", lambda e: e.tensor_scalar(M2[:], GL2[:], V2[:], None, ALU.is_ge),
                               reads=[GL2b], sreads=[V2b], writes=[M2b])
                        k.emit("dve", lambda e: e.tensor_sub(V1[:], V1[:], V2[:]),
                               reads=[V1b, V2b], writes=[V1b])
                        k.emit("act", lambda e: e.activation(W1g[:], V1[:], AF.Sigmoid),
                               reads=[V1b], writes=[W1gb])
                        k.emit("dve", lambda e: e.tensor_scalar(W2g[:], W1g[:], -1.0, 1.0,
                                                                ALU.mult, ALU.add),
                               reads=[W1gb], writes=[W2gb])
                        k.emit("dve", lambda e: e.tensor_scalar(CMB[:], M1[:], W1g[:], None, ALU.mult),
                               reads=[M1b], sreads=[W1gb], writes=[CMBb])
                        k.emit("dve", lambda e: e.scalar_tensor_tensor(
                            CMB[:], M2[:], W2g[:], CMB[:], ALU.mult, ALU.add),
                            reads=[M2b, CMBb], sreads=[W2gb], writes=[CMBb])
                        k.emit("pe", lambda e: e.transpose(P_TC[:], CMB[:], IDF[:]),
                               reads=[CMBb, IDF_b], writes=[P_TC_b])
                        k.emit("dve", lambda e: e.tensor_copy(combT[:, tc * 128:(tc + 1) * 128], P_TC[:]),
                               reads=[P_TC_b], writes=[combTb])
                    k.retire(P_G_b, P_TC_b)
                k.retire(*GWSb, GLb, GL2b, M1b, M2b, V1b, V2b, W1gb, W2gb, CMBb)

        def emit_l1_attn():
            with contextlib.ExitStack() as fs:
                RETN = [sbg([128, S], BF16, st=fs) for _ in range(NCH)]
                RETNb = [B() for _ in range(NCH)]
                with contextlib.ExitStack() as fa:
                    WQ1 = [sbg([128, 128], BF16, st=fa) for _ in range(2)]
                    WQ1b = [B(), B()]
                    WK1 = [sbg([128, 128], BF16, st=fa) for _ in range(2)]
                    WK1b = [B(), B()]
                    WV1 = [sbg([128, 128], BF16, st=fa) for _ in range(2)]
                    WV1b = [B(), B()]
                    QH = sbg([128, S], BF16, st=fa); QHb = B()
                    KH = sbg([128, S], BF16, st=fa); KHb = B()
                    VH = sbg([128, S], BF16, st=fa); VHb = B()
                    SDT = sbg([128, S], BF16, st=fa); SDTb = B()
                    GBR = sbg([128, S], BF16, st=fa); GBRb = B()
                    NROW = sbg([128, S], st=fa); NROWb = B()
                    GAMC = sbg([128, 1], st=fa); GAMCb = B(small=True)
                    RS = sbg([128, 1024], st=fa); RSb = B()
                    RS2 = sbg([128, 1024], st=fa); RS2b = B()
                    R_m = sbg([1, 1024], st=fa); R_mb = B()
                    R_rs = sbg([1, 1024], st=fa); R_rsb = B()
                    R_nm = sbg([1, 1024], st=fa); R_nmb = B()
                    k.emit("pool", lambda e: e.iota(NROW[:], pattern=[[1, S]], base=0,
                                                    channel_multiplier=0,
                                                    allow_small_or_imprecise_dtypes=True),
                           writes=[NROWb])

                    def load_h(h):
                        i = h % 2
                        ch = "cw0" if i == 0 else "cw1"
                        k.dma(ch, "pool", WQ1[i][:], ins["wq1"][h], writes=[WQ1b[i]])
                        k.dma(ch, "pool", WK1[i][:], ins["wk1"][h], writes=[WK1b[i]])
                        k.dma(ch, "pool", WV1[i][:], ins["wv1"][h], writes=[WV1b[i]])

                    load_h(0)
                    for h in range(H1):
                        i = h % 2
                        if h + 1 < H1:
                            load_h(h + 1)
                        lg = LGAM[h]
                        k.emit("dve", lambda e: e.memset(GAMC[:], lg), writes=[GAMCb])
                        k.emit("act", lambda e: e.activation(GBR[:], NROW[:], AF.Exp, scale=GAMC[:]),
                               reads=[NROWb], sreads=[GAMCb], writes=[GBRb])
                        with contextlib.ExitStack() as qs:
                            P_A = psg([128, S], qs); P_A_b = B(psum=True)
                            for nb in range(NB):
                                k.emit("pe", lambda e: e.matmul(
                                    P_A[:, nb * 512:(nb + 1) * 512], WQ1[i][:],
                                    XN[h][:, nb * 512:(nb + 1) * 512], start=True, stop=True),
                                    reads=[WQ1b[i], XNb[h]], writes=[P_A_b])
                            k.emit("dve", lambda e: e.tensor_mul(QH[:], P_A[:], GBR[:]),
                                   reads=[P_A_b, GBRb], writes=[QHb])
                            for nb in range(NB):
                                k.emit("pe", lambda e: e.matmul(
                                    P_A[:, nb * 512:(nb + 1) * 512], WK1[i][:],
                                    XN[h][:, nb * 512:(nb + 1) * 512], start=True, stop=True),
                                    reads=[WK1b[i], XNb[h]], writes=[P_A_b])
                            k.emit("act", lambda e: e.copy(KH[:], P_A[:]), reads=[P_A_b], writes=[KHb])
                            k.retire(P_A_b)
                        with contextlib.ExitStack() as vs:
                            P_V = psg([128, 128], vs); P_V_b = B(psum=True)
                            for mc in range(16):
                                k.emit("pe", lambda e: e.matmul(
                                    P_V[:], XN[h][:, mc * 128:(mc + 1) * 128], WV1[i][:],
                                    start=True, stop=True),
                                    reads=[XNb[h], WV1b[i]], writes=[P_V_b])
                                gm0 = float(np.exp(-lg * (mc * 128)))
                                k.emit("dve", lambda e: e.tensor_scalar(
                                    VH[:, mc * 128:(mc + 1) * 128], P_V[:], GNEG[h][:], gm0,
                                    ALU.mult, ALU.mult),
                                    reads=[P_V_b, GNEG_b[h]], writes=[VHb])
                            k.retire(P_V_b)
                        with contextlib.ExitStack() as rs_:
                            P_R = psg([128, S], rs_); P_R_b = B(psum=True)
                            with contextlib.ExitStack() as ts:
                                P_T = [psg([128, 512], ts) for _ in range(2)]
                                P_T_b = [B(psum=True), B(psum=True)]
                                for mc in range(16):
                                    m0 = mc * 128
                                    blocks = []
                                    n0 = m0
                                    while n0 < S:
                                        n1 = min(S, (n0 // 512 + 1) * 512)
                                        blocks.append((n0, n1))
                                        n0 = n1
                                    for bi, (n0, n1) in enumerate(blocks):
                                        pt = P_T[bi % 2]; ptb = P_T_b[bi % 2]
                                        k.emit("pe", lambda e: e.matmul(
                                            pt[:, 0:n1 - n0], KH[:, mc * 128:(mc + 1) * 128],
                                            QH[:, n0:n1], start=True, stop=True),
                                            reads=[KHb, QHb], writes=[ptb])
                                        k.emit("act", lambda e: e.copy(SDT[:, n0:n1], pt[:, 0:n1 - n0]),
                                               reads=[ptb], writes=[SDTb])
                                        if bi == 0:
                                            k.emit("dve", lambda e: e.tensor_mul(
                                                SDT[:, m0:m0 + 128], SDT[:, m0:m0 + 128], TRIB[:]),
                                                reads=[SDTb, TRIB_b], writes=[SDTb])
                                        for nq in range(n0, n1, 512):
                                            nq1 = min(n1, nq + 512)
                                            k.emit("pe", lambda e: e.matmul(
                                                P_R[:, nq:nq1], VH[:, mc * 128:(mc + 1) * 128],
                                                SDT[:, nq:nq1], start=(mc == 0),
                                                stop=(mc == 15)),
                                                reads=[VHb, SDTb], writes=[P_R_b])
                                k.retire(P_T_b[0], P_T_b[1])
                            for nh in range(2):
                                o0 = nh * 1024
                                k.emit("act", lambda e: e.copy(RS[:], P_R[:, o0:o0 + 1024]),
                                       reads=[P_R_b], writes=[RSb])
                                k.emit("act", lambda e: e.activation(RS2[:], P_R[:, o0:o0 + 1024],
                                                                     AF.Square),
                                       reads=[P_R_b], writes=[RS2b])
                                with contextlib.ExitStack() as gs:
                                    P_r1 = psg([1, 1024], gs); P_r1_b = B(psum=True)
                                    P_r2 = psg([1, 1024], gs); P_r2_b = B(psum=True)
                                    for nb in range(2):
                                        k.emit("pe", lambda e: e.matmul(
                                            P_r1[:, nb * 512:(nb + 1) * 512], ones_f[:],
                                            RS[:, nb * 512:(nb + 1) * 512], start=True, stop=True),
                                            reads=[ones_f_b, RSb], writes=[P_r1_b])
                                        k.emit("pe", lambda e: e.matmul(
                                            P_r2[:, nb * 512:(nb + 1) * 512], ones_f[:],
                                            RS2[:, nb * 512:(nb + 1) * 512], start=True, stop=True),
                                            reads=[ones_f_b, RS2b], writes=[P_r2_b])
                                    k.emit("dve", lambda e: e.tensor_scalar_mul(R_m[:], P_r1[:],
                                                                                1.0 / 128.0),
                                           reads=[P_r1_b], writes=[R_mb])
                                    k.emit("dve", lambda e: e.tensor_scalar_mul(R_rs[:], P_r2[:],
                                                                                1.0 / 128.0),
                                           reads=[P_r2_b], writes=[R_rsb])
                                    k.emit("dve", lambda e: e.tensor_mul(R_nm[:], R_m[:], R_m[:]),
                                           reads=[R_mb], writes=[R_nmb])
                                    k.emit("dve", lambda e: e.tensor_sub(R_rs[:], R_rs[:], R_nm[:]),
                                           reads=[R_rsb, R_nmb], writes=[R_rsb])
                                    k.emit("act", lambda e: e.activation(R_rs[:], R_rs[:], AF.Sqrt,
                                                                         bias=ceps[0:1, :]),
                                           reads=[R_rsb], sreads=[ceps_b], writes=[R_rsb])
                                    k.emit("dve", lambda e: e.reciprocal(R_rs[:], R_rs[:]),
                                           reads=[R_rsb], writes=[R_rsb])
                                    k.emit("dve", lambda e: e.tensor_mul(R_nm[:], R_m[:], R_rs[:]),
                                           reads=[R_mb, R_rsb], writes=[R_nmb])
                                    k.emit("dve", lambda e: e.tensor_scalar_mul(R_nm[:], R_nm[:], -1.0),
                                           reads=[R_nmb], writes=[R_nmb])
                                    k.retire(P_r1_b, P_r2_b)
                                with contextlib.ExitStack() as gs:
                                    P_b1 = psg([128, 1024], gs); P_b1_b = B(psum=True)
                                    P_b2g = psg([128, 1024], gs); P_b2_b = B(psum=True)
                                    for nb in range(2):
                                        k.emit("pe", lambda e: e.matmul(
                                            P_b1[:, nb * 512:(nb + 1) * 512], ones1_f[:],
                                            R_rs[0:1, nb * 512:(nb + 1) * 512], start=True, stop=True),
                                            reads=[ones1_f_b, R_rsb], writes=[P_b1_b])
                                        k.emit("pe", lambda e: e.matmul(
                                            P_b2g[:, nb * 512:(nb + 1) * 512], ones1_f[:],
                                            R_nm[0:1, nb * 512:(nb + 1) * 512], start=True, stop=True),
                                            reads=[ones1_f_b, R_nmb], writes=[P_b2_b])
                                    k.emit("dve", lambda e: e.tensor_mul(RS[:], RS[:], P_b1[:]),
                                           reads=[RSb, P_b1_b], writes=[RSb])
                                    k.emit("dve", lambda e: e.tensor_add(RS[:], RS[:], P_b2g[:]),
                                           reads=[RSb, P_b2_b], writes=[RSb])
                                    k.emit("dve", lambda e: e.tensor_scalar(
                                        RETN[h][:, o0:o0 + 1024], RS[:], GNWc[h][:],
                                        GNBc[h][:], ALU.mult, ALU.add),
                                        reads=[RSb], sreads=[GNWc_b[h], GNBc_b[h]], writes=[RETNb[h]])
                                    k.retire(P_b1_b, P_b2_b)
                            k.retire(P_R_b)
                    k.retire(*WQ1b, *WK1b, *WV1b, QHb, KHb, VHb, SDTb, GBRb, NROWb, GAMCb,
                             RSb, RS2b, R_mb, R_rsb, R_nmb)
                with contextlib.ExitStack() as fb_:
                    W11S = [sbg([128, 1024], BF16, st=fb_) for _ in range(NCH)]
                    W11b = [B() for _ in range(NCH)]
                    W21S = [sbg([128, 1024], BF16, st=fb_) for _ in range(NCH)]
                    W21b = [B() for _ in range(NCH)]
                    GS = sbg([128, S], BF16, st=fb_); GSb = B()
                    for kc in range(NCH):
                        k.dma("cw2", "pool", W11S[kc][:], ins["w11"][kc], writes=[W11b[kc]])
                        k.dma("cw3", "pool", W21S[kc][:], ins["w21"][kc], writes=[W21b[kc]])
                    with contextlib.ExitStack() as ps:
                        P_G = [psg([128, S], ps) for _ in range(2)]
                        P_G_b = [B(psum=True), B(psum=True)]
                        for dc in range(NCH):
                            pg = P_G[dc % 2]; pgb = P_G_b[dc % 2]
                            for kc in range(NCH):
                                for nb in range(NB):
                                    k.emit("pe", lambda e: e.matmul(
                                        pg[:, nb * 512:(nb + 1) * 512],
                                        W11S[kc][:, dc * 128:(dc + 1) * 128],
                                        XN[kc][:, nb * 512:(nb + 1) * 512],
                                        start=(kc == 0), stop=(kc == NCH - 1)),
                                        reads=[W11b[kc], XNb[kc]], writes=[pgb])
                            k.emit("act", lambda e: e.activation(GS[:], pg[:], AF.Silu),
                                   reads=[pgb], writes=[GSb])
                            k.emit("dve", lambda e: e.tensor_add(RETN[dc][:], GS[:], RETN[dc][:]),
                                   reads=[GSb, RETNb[dc]], writes=[RETNb[dc]])
                        k.retire(P_G_b[0], P_G_b[1])
                    with contextlib.ExitStack() as ps:
                        P_Y2 = [psg([128, S], ps) for _ in range(2)]
                        P_Y2_b = [B(psum=True), B(psum=True)]
                        for dc in range(NCH):
                            py = P_Y2[dc % 2]; pyb = P_Y2_b[dc % 2]
                            for oc in range(NCH):
                                for nb in range(NB):
                                    k.emit("pe", lambda e: e.matmul(
                                        py[:, nb * 512:(nb + 1) * 512],
                                        W21S[oc][:, dc * 128:(dc + 1) * 128],
                                        RETN[oc][:, nb * 512:(nb + 1) * 512],
                                        start=(oc == 0), stop=(oc == NCH - 1)),
                                        reads=[W21b[oc], RETNb[oc]], writes=[pyb])
                            k.emit("dve", lambda e: e.tensor_add(X[dc][:], X[dc][:], py[:]),
                                   reads=[Xb[dc], pyb], writes=[Xb[dc]])
                        k.retire(P_Y2_b[0], P_Y2_b[1])
                    k.retire(*W11b, *W21b, GSb)
                k.retire(*RETNb)

        def emit_moe():
            with contextlib.ExitStack() as fs:
                MB1c = []; MB1c_b = []
                MB2c = []; MB2c_b = []
                for e_ in range(NE):
                    t_, b_ = explode(mb1_s[e_], mb1_b[e_], 32, st=fs)
                    MB1c.append(t_); MB1c_b.append(b_)
                    t_, b_ = explode(mb2_s[e_], mb2_b[e_], 8, st=fs)
                    MB2c.append(t_); MB2c_b.append(b_)
                Hsb = [sbg([128, 1024], BF16, st=fs) for _ in range(NFB)]
                Hb = [B() for _ in range(NFB)]
                WF1 = [sbg([128, 1024], BF16, st=fs) for _ in range(2)]
                WF1b = [B(), B()]
                MW2 = [sbg([128, 4096], BF16, st=fs)]
                MW2b = [B()]
                CBS = [sbg([128, 1024], BF16, st=fs) for _ in range(2)]
                CBSb = [B(), B()]
                T1 = sbg([128, 1024], st=fs); T1b = B()
                for e_ in range(NE):
                    with contextlib.ExitStack() as cs:
                        P_CB = psg([128, 1024], cs); P_CB_b = B(psum=True)
                        for th in range(2):
                            for nb in range(2):
                                k.emit("pe", lambda e: e.matmul(
                                    P_CB[:, nb * 512:(nb + 1) * 512],
                                    SEL[:, e_ * 128:(e_ + 1) * 128],
                                    combT[:, th * 1024 + nb * 512:th * 1024 + (nb + 1) * 512],
                                    start=True, stop=True),
                                    reads=[SEL_b, combTb], writes=[P_CB_b])
                            k.emit("act", lambda e: e.copy(CBS[th][:], P_CB[:]),
                                   reads=[P_CB_b], writes=[CBSb[th]])
                        k.retire(P_CB_b)
                    with contextlib.ExitStack() as ps:
                        P_H = [psg([128, 1024], ps) for _ in range(2)]
                        P_H_b = [B(psum=True), B(psum=True)]
                        P_Y = [psg([128, 1024], ps) for _ in range(2)]
                        P_Y_b = [B(psum=True), B(psum=True)]
                        for th in range(2):
                            t0 = th * 1024
                            k.dma("cw4", "pool", WF1[0][:], ins["mw1"][e_, 0], writes=[WF1b[0]])
                            for fb in range(NFB):
                                i = fb % 2
                                if fb + 1 < NFB:
                                    k.dma("cw4" if (fb + 1) % 2 == 0 else "cw5", "pool",
                                          WF1[(fb + 1) % 2][:], ins["mw1"][e_, fb + 1],
                                          writes=[WF1b[(fb + 1) % 2]])
                                ph = P_H[i]; phb = P_H_b[i]
                                for kc in range(NCH):
                                    for nb in range(2):
                                        k.emit("pe", lambda e: e.matmul(
                                            ph[:, nb * 512:(nb + 1) * 512],
                                            WF1[i][:, kc * 128:(kc + 1) * 128],
                                            XN[kc][:, t0 + nb * 512:t0 + (nb + 1) * 512],
                                            start=(kc == 0), stop=(kc == NCH - 1)),
                                            reads=[WF1b[i], XNb[kc]], writes=[phb])
                                k.emit("act", lambda e: e.activation(
                                    Hsb[fb][:], ph[:], AF.Gelu, bias=MB1c[e_][fb][:]),
                                    reads=[phb], sreads=[MB1c_b[e_][fb]], writes=[Hb[fb]])
                            for dc in range(NCH):
                                k.dma("cw2", "pool", MW2[0][:], ins["mw2"][e_, dc], writes=[MW2b[0]])
                                py = P_Y[dc % 2]; pyb = P_Y_b[dc % 2]
                                for fb in range(NFB):
                                    for nb in range(2):
                                        k.emit("pe", lambda e: e.matmul(
                                            py[:, nb * 512:(nb + 1) * 512],
                                            MW2[0][:, fb * 128:(fb + 1) * 128],
                                            Hsb[fb][:, nb * 512:(nb + 1) * 512],
                                            start=(fb == 0), stop=(fb == NFB - 1)),
                                            reads=[MW2b[0], Hb[fb]], writes=[pyb])
                                k.emit("dve", lambda e: e.scalar_tensor_tensor(
                                    T1[:], py[:], MB2c[e_][dc][:], CBS[th][:],
                                    ALU.add, ALU.mult),
                                    reads=[pyb, CBSb[th]], sreads=[MB2c_b[e_][dc]], writes=[T1b])
                                k.emit("dve", lambda e: e.tensor_add(
                                    X[dc][:, t0:t0 + 1024], X[dc][:, t0:t0 + 1024], T1[:]),
                                    reads=[T1b, Xb[dc]], writes=[Xb[dc]])
                        k.retire(P_H_b[0], P_H_b[1], P_Y_b[0], P_Y_b[1])
                k.retire(*Hb, *WF1b, *MW2b, *CBSb, T1b)

        def emit_body(s):
            want_taps = (not loop) and s == 0
            emit_load_x(s)
            if 0 in layers:
                if "ln" in phases:
                    emit_layernorm()
                    if want_taps:
                        tap("xn0", XN, XNb)
                if "attn" in phases:
                    emit_l0_attn()
                    if want_taps:
                        tap("att0", X, Xb)
                if "ffnH" in phases:
                    emit_l0_ffn(h_only=True)
                if "ffnH4" in phases:
                    emit_l0_ffn(h_only=True, nfb=4, nth=1)
                if "ffnH4p" in phases:
                    emit_l0_ffn(h_only=True, quantum=False, nfb=4, nth=1)
                if "ffnH1p" in phases:
                    emit_l0_ffn(h_only=True, quantum=False, nfb=1, nth=1)
                if "ffnP" in phases:
                    emit_l0_ffn(quantum=False)
                if "ffnH1q" in phases:
                    emit_l0_ffn(h_only=True, quantum=True, nfb=1, nth=1)
                if "ffnH2q" in phases:
                    emit_l0_ffn(h_only=True, quantum=True, nfb=2, nth=1)
                if "ffn" in phases:
                    emit_l0_ffn()
                    if want_taps:
                        tap("x1", X, Xb)
            if 1 in layers:
                if "ln" in phases:
                    emit_layernorm()
                    if want_taps:
                        tap("xn1", XN, XNb)
                if "gate" in phases:
                    emit_gate()
                if "l1attn" in phases:
                    emit_l1_attn()
                    if want_taps:
                        tap("att1", X, Xb)
                if "moe" in phases:
                    emit_moe()
            emit_store_x(s)

        stats_out = None
        if not loop:
            for s in range(nseq):
                emit_body(s)
            for ch in ("cs0", "cs1", "cx", "cv", "ct"):
                if k.cnt[ch]:
                    nc.sync.wait_ge(k.h[ch], k.cnt[ch])
        else:
            k.begin_loop(loop_stats)
            if loop_stats is None:
                emit_body(0)
                stats_out = k.collect_stats(_buf_reg)
            else:
                with nc.Fori(0, nseq) as i_sym:
                    emit_body(i_sym)
                    k.end_body()
                stats_out = k.collect_stats(_buf_reg)
                for ch in DMACH:
                    d = k.delta.get(ch, 0)
                    final = k.cnt0[ch] + d * (nseq + 1)
                    if final:
                        nc.sync.wait_ge(k.h[ch], final)
    return (nc, stats_out) if loop else nc


def _prep_inputs(inp):
    f32 = np.float32
    A = lambda x: np.ascontiguousarray(np.asarray(x, f32))
    d = {}
    d["x"] = np.ascontiguousarray(A(inp["L0_attn_x"]).transpose(0, 2, 1))
    pr = lambda w: np.ascontiguousarray(
        w.reshape(8, 128, 8, 128).transpose(2, 1, 0, 3)).reshape(8, 128, 1024)
    d["wqp"] = pr(A(inp["L0_attn_w_q"]).T)
    d["wkp"] = pr(A(inp["L0_attn_w_k"]).T)
    d["wvp"] = pr(A(inp["L0_attn_w_v"]).T)
    d["wo"] = pr(A(inp["L0_attn_w_o"]).T)
    W1 = A(inp["L0_ffn_W1"]); W2 = A(inp["L0_ffn_W2"])
    d["w1f"] = np.ascontiguousarray(
        W1.reshape(8, 128, 32, 128).transpose(2, 1, 0, 3)).reshape(32, 128, 1024)
    d["w2f"] = np.ascontiguousarray(
        W2.reshape(32, 128, 8, 128).transpose(2, 1, 0, 3)).reshape(8, 128, 4096)
    d["b1c"] = np.ascontiguousarray(A(inp["L0_ffn_b1"]).reshape(32, 128).T)
    d["b2c"] = np.ascontiguousarray(A(inp["L0_ffn_b2"]).reshape(8, 128).T)
    d["thc"] = np.ascontiguousarray(A(inp["L0_ffn_quantum_phases"]).reshape(32, 128).T)
    d["wrc"] = np.ascontiguousarray(A(inp["L0_ffn_quantum_weights_real"]).reshape(32, 128).T)
    d["wic"] = np.ascontiguousarray(A(inp["L0_ffn_quantum_weights_imag"]).reshape(32, 128).T)
    d["wq1"] = A(inp["L1_attn_Wq"]); d["wk1"] = A(inp["L1_attn_Wk"]); d["wv1"] = A(inp["L1_attn_Wv"])
    d["w11"] = np.ascontiguousarray(A(inp["L1_attn_W1"]).reshape(8, 128, 1024))
    d["w21"] = np.ascontiguousarray(A(inp["L1_attn_W2"]).reshape(8, 128, 1024))
    d["gnw"] = np.ascontiguousarray(A(inp["L1_attn_gn_w"]).reshape(8, 128).T)
    d["gnb"] = np.ascontiguousarray(A(inp["L1_attn_gn_b"]).reshape(8, 128).T)
    d["gw"] = np.ascontiguousarray(A(inp["L1_ffn_gate_w"]).reshape(8, 128, 8))
    d["gb"] = np.ascontiguousarray(A(inp["L1_ffn_gate_b"]).reshape(1, 8))
    d["mw1"] = np.ascontiguousarray(
        A(inp["L1_ffn_w1"]).reshape(8, 8, 128, 32, 128).transpose(0, 3, 2, 1, 4)).reshape(8, 32, 128, 1024)
    d["mw2"] = np.ascontiguousarray(
        A(inp["L1_ffn_w2"]).reshape(8, 32, 128, 8, 128).transpose(0, 3, 2, 1, 4)).reshape(8, 8, 128, 4096)
    d["mb1"] = np.ascontiguousarray(A(inp["L1_ffn_b1"]).reshape(8, 32, 128).transpose(0, 2, 1))
    d["mb2"] = np.ascontiguousarray(A(inp["L1_ffn_b2"]).reshape(8, 8, 128).transpose(0, 2, 1))
    return d


def kernel(**inputs):
    d = _prep_inputs(inputs)
    nc1, st1 = build_program(loop=True, loop_stats=None)
    nc, st2 = build_program(loop=True, loop_stats=st1)
    if st2["wait_pairs"] - st1["wait_pairs"]:
        nc, _ = build_program(loop=True, loop_stats=st2)
    wkeys = [k_ for k_ in d if k_ != "x"]
    in_maps = []
    for c in range(8):
        m = {"x": np.ascontiguousarray(d["x"][c * NSEQ:(c + 1) * NSEQ])}
        for k_ in wkeys:
            m[k_] = d[k_]
        in_maps.append(m)
    res = run_bass_kernel_spmd(nc, in_maps, core_ids=list(range(8)), trace_cores=[0])
    outs = [np.asarray(r["out"]) for r in res.results]
    full = np.concatenate(outs, axis=0)
    return np.ascontiguousarray(full.transpose(0, 2, 1))


# version 06
# speedup vs baseline: 1.1354x; 1.1314x over previous
import contextlib
import numpy as np
import concourse.bass as bass
import concourse.mybir as mybir
from concourse.bass_utils import run_bass_kernel_spmd

F32 = mybir.dt.float32
BF16 = mybir.dt.bfloat16
I32 = mybir.dt.int32
AF = mybir.ActivationFunctionType
ALU = mybir.AluOpType
AX = mybir.AxisListType
PI = float(np.pi)

S = 2048
D = 1024
NCH = 8
FF = 4096
NFB = 32
NSEQ = 4
NB = S // 512
EPS = 1e-5
H0 = 16
H1 = 8
NE = 8

_lin = np.linspace(np.log(1.0 / 512.0), np.log(1.0 / 32.0), H1)
GAMMAS = (1.0 - np.exp(_lin)).astype(np.float64)
LGAM = [float(np.log(g)) for g in GAMMAS]

COMPUTE = ("pe", "act", "dve", "pool")
DMACH = ["cx", "cs0", "cs1", "cw0", "cw1", "cw2", "cw3", "cw4", "cw5", "cv", "ct"]
_uid = [0]


_buf_seq = [0]
_buf_reg = []


class Buf:
    __slots__ = ("w", "r", "fresh", "psum", "small", "bid")

    def __init__(self, psum=False, small=False):
        self.w = None
        self.r = {}
        self.fresh = True
        self.psum = psum
        self.small = small
        self.bid = _buf_seq[0]
        _buf_seq[0] += 1
        _buf_reg.append(self)


class K:
    ENAME2ENG = {"pe": "tensor", "act": "scalar", "dve": "vector",
                 "pool": "gpsimd", "sp": "sync"}

    def __init__(self, nc, es):
        self.nc = nc
        self.eng = {"pe": nc.tensor, "act": nc.scalar, "dve": nc.vector,
                    "pool": nc.gpsimd, "sp": nc.sync}
        self.h = {}
        self.cnt = {}
        for name in list(COMPUTE) + DMACH:
            self.h[name] = es.enter_context(nc.semaphore(name))
            self.cnt[name] = 0
        self.waited = {}
        self.legacy = {}
        self.epoch = None
        self.cnt0 = None
        self.delta = None
        self.lwaited = {}
        self.bregs = {}
        self.tmpreg = {}
        self.wait_pairs = set()
        self.chan_buf = {n: Buf() for n in DMACH}

    def begin_loop(self, stats=None):
        self.cnt0 = dict(self.cnt)
        self.epoch = 0
        self.lwaited = {}
        if stats is not None:
            self.delta = stats["delta"]
            for sem in self.cnt:
                if self.cnt0[sem] > 0 and self.delta.get(sem, 0) > 0:
                    self.eng["sp"].wait_ge(self.h[sem], self.cnt0[sem])
            for sem in COMPUTE:
                if self.cnt0[sem] > 0:
                    self.eng["sp"].wait_ge(self.h[sem], self.cnt0[sem])
            for sem, d in self.delta.items():
                if d > 0:
                    self.eng["sp"].sem_inc(self.h[sem], d)
            for (ename, sem) in sorted(stats["wait_pairs"]):
                if self.delta.get(sem, 0) == 0:
                    continue
                r = self.eng[ename].alloc_register(f"B_{ename}_{sem}")
                self.eng[ename].reg_mov(r, self.cnt0[sem] + self.delta[sem])
                self.bregs[(ename, sem)] = r
            for ename in ("pe", "act", "dve", "pool", "sp"):
                self.tmpreg[ename] = self.eng[ename].alloc_register(f"T_{ename}")
            by_id = stats["bufs_by_id"]
            for b in _buf_reg:
                st = by_id.get(b.bid)
                if st is None:
                    continue
                w, r_ = st
                if w is not None:
                    b.w = w
                if r_:
                    b.r = dict(r_)

    def end_body(self):
        for (ename, sem), r in self.bregs.items():
            self.eng[ename].reg_alu(r, r, self.delta[sem], mybir.AluOpType.add)

    def collect_stats(self, all_bufs):
        delta = {sem: self.cnt[sem] - self.cnt0[sem] for sem in self.cnt}
        bufs_by_id = {}
        def shift(tok):
            sem, cnt, ep = tok
            if ep is None:
                return tok
            return (sem, cnt, ep - 1)
        for b in all_bufs:
            w = shift(b.w) if b.w is not None else None
            r_ = {sem: (cnt, (ep - 1) if ep is not None else None)
                  for sem, (cnt, ep) in b.r.items()}
            bufs_by_id[b.bid] = (w, r_)
        return {"delta": delta, "wait_pairs": set(self.wait_pairs),
                "bufs_by_id": bufs_by_id}

    def _wait_tok(self, ename, tok, force=False):
        sem, cnt, ep = tok
        if cnt <= 0:
            return
        if sem == ename and sem in COMPUTE and not force:
            return
        key = (ename, sem)
        if self.epoch is None or ep is None:
            if self.waited.get(key, 0) >= cnt:
                return
            self.eng[ename].wait_ge(self.h[sem], cnt)
            self.waited[key] = cnt
            return
        self.wait_pairs.add(key)
        x = (cnt - self.cnt0[sem]) + self.delta_of(sem) * ep
        if self.lwaited.get(key, -10**9) >= x:
            return
        self.lwaited[key] = x
        if self.delta is None:
            return
        breg = self.bregs.get(key)
        if breg is None:
            return
        t = self.tmpreg[ename]
        self.eng[ename].reg_alu(t, breg, x, mybir.AluOpType.add)
        self.eng[ename].wait_ge(self.h[sem], t)

    def delta_of(self, sem):
        if self.delta is not None:
            return self.delta.get(sem, 0)
        return 0

    def _wait(self, ename, sem, cnt, force=False):
        self._wait_tok(ename, (sem, cnt, None), force)

    def _pre(self, ename, reads, writes, sreads=()):
        for b in sreads:
            if b.w is not None:
                self._wait_tok(ename, b.w, force=True)
        for b in reads:
            if b.w is not None:
                self._wait_tok(ename, b.w, force=b.small)
            if b.psum:
                for sem, (cnt, ep) in b.r.items():
                    if sem != ename:
                        self._wait_tok(ename, (sem, cnt, ep))
        for b in writes:
            if b.fresh:
                for sem, (cnt, ep) in self.legacy.items():
                    self._wait_tok(ename, (sem, cnt, ep))
                b.fresh = False
            if b.w is not None:
                self._wait_tok(ename, b.w, force=b.small)
            for sem, (cnt, ep) in b.r.items():
                self._wait_tok(ename, (sem, cnt, ep), force=b.small)

    def _post(self, tok, reads, writes):
        sem, cnt, ep = tok
        for b in reads:
            cur = b.r.get(sem)
            if cur is None or (cur[1] or 0, cur[0]) < ((ep or 0), cnt):
                b.r[sem] = (cnt, ep)
        for b in writes:
            b.w = tok
            b.r = {}

    def emit(self, ename, fn, reads=(), writes=(), sreads=(), low_prec=False):
        self._pre(ename, reads, writes, sreads)
        if low_prec:
            with self.nc.allow_low_precision(reason="rounded normalization factor"):
                instr = fn(self.eng[ename])
        else:
            instr = fn(self.eng[ename])
        self.cnt[ename] += 1
        instr.then_inc(self.h[ename], 1)
        tok = (ename, self.cnt[ename], self.epoch)
        self._post(tok, list(reads) + list(sreads), writes)
        return tok

    def dma(self, chan, ename, out, in_, reads=(), writes=()):
        cb = self.chan_buf[chan]
        self._pre(ename, reads, list(writes) + [cb])
        instr = self.eng[ename].dma_start(out=out, in_=in_)
        self.cnt[chan] += 16
        instr.then_inc(self.h[chan], 16)
        tok = (chan, self.cnt[chan], self.epoch)
        self._post(tok, reads, list(writes) + [cb])
        return tok

    def retire(self, *bufs):
        def fold(sem, cnt, ep):
            cur = self.legacy.get(sem)
            if cur is None or ((ep or 0), cnt) > ((cur[1] or 0), cur[0]):
                self.legacy[sem] = (cnt, ep)
        for b in bufs:
            if b.w is not None:
                fold(*b.w)
            for sem, (cnt, ep) in b.r.items():
                fold(sem, cnt, ep)


def build_program(layers=(0, 1), nseq=NSEQ, taps=(),
                  phases=("ln", "attn", "ffn", "gate", "l1attn", "moe"),
                  loop=False, loop_stats=None):
    _buf_seq[0] = 0
    _buf_reg.clear()
    _uid[0] = 0
    nc = bass.Bass("TRN2", target_bir_lowering=False)
    dt = nc.dram_tensor
    xin = dt("x", [nseq, D, S], F32, kind="ExternalInput").ap()
    out = dt("out", [nseq, D, S], F32, kind="ExternalOutput").ap()
    ins = {}

    def din(name, shape):
        ins[name] = dt(name, shape, F32, kind="ExternalInput").ap()
        return ins[name]

    if 0 in layers:
        din("wqp", [8, 128, 1024]); din("wkp", [8, 128, 1024]); din("wvp", [8, 128, 1024])
        din("wo", [8, 128, 1024])
        din("w1f", [32, 128, 1024]); din("w2f", [8, 128, 4096])
        din("b1c", [128, 32]); din("b2c", [128, 8])
        din("thc", [128, 32]); din("wrc", [128, 32]); din("wic", [128, 32])
    if 1 in layers:
        din("wq1", [8, 128, 128]); din("wk1", [8, 128, 128]); din("wv1", [8, 128, 128])
        din("w11", [8, 128, 1024]); din("w21", [8, 128, 1024])
        din("gnw", [128, 8]); din("gnb", [128, 8])
        din("gw", [8, 128, 8]); din("gb", [1, 8])
        din("mw1", [8, 32, 128, 1024]); din("mw2", [8, 8, 128, 4096])
        din("mb1", [8, 128, 32]); din("mb2", [8, 128, 8])
    tapd = {}
    for tname in taps:
        tapd[tname] = dt("tap_" + tname, [NCH, 128, S], F32, kind="ExternalOutput").ap()

    es = contextlib.ExitStack()
    with es:
        k = K(nc, es)
        es.enter_context(nc.Block())

        def sbg(shape, dtype=F32, st=es):
            _uid[0] += 1
            return st.enter_context(nc.sbuf_tensor(f"t{_uid[0]}", shape, dtype))

        def psg(shape, st, dtype=F32):
            _uid[0] += 1
            return st.enter_context(nc.psum_tensor(f"p{_uid[0]}", shape, dtype))

        B = Buf

        X = [sbg([128, S]) for _ in range(NCH)]
        Xb = [B() for _ in range(NCH)]
        XN = [sbg([128, S], BF16) for _ in range(NCH)]
        XNb = [B() for _ in range(NCH)]
        ones_f = sbg([128, 1]); ones_f_b = B()
        ones1_f = sbg([1, 128]); ones1_f_b = B()
        onesH0 = sbg([128, 1], BF16); onesH0_b = B()
        onesH1 = sbg([128, 1], BF16); onesH1_b = B()
        ones1_b16 = sbg([1, 128], BF16); ones1_b16_b = B()
        c01 = sbg([128, 1]); c01_b = B()
        c125 = sbg([128, 1]); c125_b = B()
        ceps = sbg([128, 1]); ceps_b = B()

        k.emit("dve", lambda e: e.memset(ones_f[:], 1.0), writes=[ones_f_b])
        k.emit("dve", lambda e: e.memset(ones1_f[:], 1.0), writes=[ones1_f_b])
        k.emit("dve", lambda e: e.memset(onesH0[0:64, :], 1.0), writes=[onesH0_b])
        k.emit("dve", lambda e: e.memset(onesH0[64:128, :], 0.0), writes=[onesH0_b])
        k.emit("dve", lambda e: e.memset(onesH1[0:64, :], 0.0), writes=[onesH1_b])
        k.emit("dve", lambda e: e.memset(onesH1[64:128, :], 1.0), writes=[onesH1_b])
        k.emit("dve", lambda e: e.memset(ones1_b16[:], 1.0), writes=[ones1_b16_b])
        k.emit("dve", lambda e: e.memset(c01[:], 0.1), writes=[c01_b])
        k.emit("dve", lambda e: e.memset(c125[:], 0.125), writes=[c125_b])
        k.emit("dve", lambda e: e.memset(ceps[:], EPS), writes=[ceps_b])

        def explode(srct, srcb, K, st=es):
            tiles = [sbg([128, 1], st=st) for _ in range(K)]
            bufs = [B(small=True) for _ in range(K)]
            for j in range(K):
                k.emit("dve", lambda e, j=j: e.tensor_copy(tiles[j][:], srct[:, j:j + 1]),
                       reads=[srcb], writes=[bufs[j]])
            return tiles, bufs

        if 0 in layers:
            b1c_s = sbg([128, 32]); b1c_b = B()
            b2c_s = sbg([128, 8]); b2c_b = B()
            Acol = sbg([128, 32]); Acol_b = B(small=True)
            Bcol = sbg([128, 32]); Bcol_b = B(small=True)
            BSIN = sbg([128, 32]); BSIN_b = B(small=True)
            BCOS = sbg([128, 32]); BCOS_b = B(small=True)
            with contextlib.ExitStack() as qs:
                thc_s = sbg([128, 32], st=qs); thc_b = B()
                wrc_s = sbg([128, 32], st=qs); wrc_b = B()
                wic_s = sbg([128, 32], st=qs); wic_b = B()
                t0 = sbg([128, 32], st=qs); t0b = B(small=True)
                t1 = sbg([128, 32], st=qs); t1b = B(small=True)
                t2 = sbg([128, 32], st=qs); t2b = B(small=True)
                ti = sbg([128, 32], I32, st=qs); tib = B(small=True)
                sinth = sbg([128, 32], st=qs); sinth_b = B(small=True)
                costh = sbg([128, 32], st=qs); costh_b = B(small=True)
                cpi2 = sbg([128, 1], st=qs); cpi2_b = B(small=True)
                cm1 = sbg([128, 1], st=qs); cm1_b = B(small=True)
                k.dma("cv", "sp", b1c_s[:], ins["b1c"], writes=[b1c_b])
                k.dma("cv", "sp", thc_s[:], ins["thc"], writes=[thc_b])
                k.dma("cv", "sp", wrc_s[:], ins["wrc"], writes=[wrc_b])
                k.dma("cv", "sp", wic_s[:], ins["wic"], writes=[wic_b])
                k.dma("cv", "sp", b2c_s[:], ins["b2c"], writes=[b2c_b])
                k.emit("dve", lambda e: e.tensor_scalar_mul(t0[:], thc_s[:], float(1.0 / (2 * PI))),
                       reads=[thc_b], writes=[t0b])
                k.emit("dve", lambda e: e.tensor_copy(ti[:], t0[:]), reads=[t0b], writes=[tib])
                k.emit("dve", lambda e: e.tensor_copy(t0[:], ti[:]), reads=[tib], writes=[t0b])
                k.emit("dve", lambda e: e.scalar_tensor_tensor(
                    t1[:], t0[:], float(-2 * PI), thc_s[:], ALU.mult, ALU.add),
                    reads=[t0b, thc_b], writes=[t1b])
                k.emit("dve", lambda e: e.tensor_scalar(t0[:], t1[:], PI, None, ALU.is_gt),
                       reads=[t1b], writes=[t0b])
                k.emit("dve", lambda e: e.scalar_tensor_tensor(
                    t1[:], t0[:], float(-2 * PI), t1[:], ALU.mult, ALU.add),
                    reads=[t0b, t1b], writes=[t1b])
                k.emit("dve", lambda e: e.tensor_scalar(t0[:], t1[:], -PI, None, ALU.is_lt),
                       reads=[t1b], writes=[t0b])
                k.emit("dve", lambda e: e.scalar_tensor_tensor(
                    t1[:], t0[:], float(2 * PI), t1[:], ALU.mult, ALU.add),
                    reads=[t0b, t1b], writes=[t1b])
                k.emit("act", lambda e: e.activation(sinth[:], t1[:], AF.Sin),
                       reads=[t1b], writes=[sinth_b])
                k.emit("act", lambda e: e.activation(t2[:], t1[:], AF.Abs),
                       reads=[t1b], writes=[t2b])
                k.emit("dve", lambda e: e.memset(cpi2[:], PI / 2), writes=[cpi2_b])
                k.emit("dve", lambda e: e.memset(cm1[:], -1.0), writes=[cm1_b])
                k.emit("act", lambda e: e.activation(costh[:], t2[:], AF.Sin, bias=cpi2[:], scale=cm1[:]),
                       reads=[t2b], sreads=[cpi2_b, cm1_b], writes=[costh_b])
                k.emit("dve", lambda e: e.tensor_mul(t0[:], costh[:], wrc_s[:]),
                       reads=[costh_b, wrc_b], writes=[t0b])
                k.emit("dve", lambda e: e.tensor_mul(t2[:], sinth[:], wic_s[:]),
                       reads=[sinth_b, wic_b], writes=[t2b])
                k.emit("dve", lambda e: e.tensor_add(t0[:], t0[:], t2[:]),
                       reads=[t0b, t2b], writes=[t0b])
                k.emit("dve", lambda e: e.tensor_mul(t0[:], t0[:], wic_s[:]),
                       reads=[t0b, wic_b], writes=[t0b])
                k.emit("dve", lambda e: e.tensor_scalar_mul(t0[:], t0[:], 0.1),
                       reads=[t0b], writes=[t0b])
                k.emit("dve", lambda e: e.tensor_mul(Acol[:], t0[:], sinth[:]),
                       reads=[t0b, sinth_b], writes=[Acol_b])
                k.emit("dve", lambda e: e.tensor_mul(Bcol[:], t0[:], costh[:]),
                       reads=[t0b, costh_b], writes=[Bcol_b])
                k.emit("dve", lambda e: e.tensor_scalar_mul(BSIN[:], b1c_s[:], 0.1),
                       reads=[b1c_b], writes=[BSIN_b])
                k.emit("dve", lambda e: e.tensor_scalar(BCOS[:], b1c_s[:], 0.1, PI / 2, ALU.mult, ALU.add),
                       reads=[b1c_b], writes=[BCOS_b])
                k.retire(thc_b, wrc_b, wic_b, t0b, t1b, t2b, tib, sinth_b, costh_b, cpi2_b, cm1_b)
            B1Cc, B1Cc_b = explode(b1c_s, b1c_b, 32)
            if "qc2" in tapd:
                k.dma("ct", "pool", tapd["qc2"][0][:, 0:32], Acol[:], reads=[Acol_b])
            B2Cc, B2Cc_b = explode(b2c_s, b2c_b, 8)
            Ac, Ac_b = explode(Acol, Acol_b, 32)
            Bc, Bc_b = explode(Bcol, Bcol_b, 32)
            BSINc, BSINc_b = explode(BSIN, BSIN_b, 32)
            BCOSc, BCOSc_b = explode(BCOS, BCOS_b, 32)
            if "qc2" in tapd:
                with nc.allow_non_contiguous_dma(reason="debug tap"):
                    for j in range(8):
                        k.dma("ct", "pool", tapd["qc2"][1][:, j:j+1], Ac[j][:], reads=[Ac_b[j]])
                        k.dma("ct", "pool", tapd["qc2"][2][:, j:j+1], BSINc[j][:], reads=[BSINc_b[j]])

        if 1 in layers:
            gnw_s = sbg([128, 8]); gnw_b = B()
            gnb_s = sbg([128, 8]); gnb_b = B()
            mb1_s = [sbg([128, 32]) for _ in range(NE)]
            mb1_b = [B() for _ in range(NE)]
            mb2_s = [sbg([128, 8]) for _ in range(NE)]
            mb2_b = [B() for _ in range(NE)]
            IDF = sbg([128, 128]); IDF_b = B()
            TRIB = sbg([128, 128], BF16); TRIB_b = B()
            SEL = sbg([8, 1024], BF16); SEL_b = B()
            GNEG = [sbg([128, 1]) for _ in range(H1)]
            GNEG_b = [B(small=True) for _ in range(H1)]
            GBS = sbg([128, 8]); GBS_b = B()
            combT = sbg([8, S], BF16); combTb = B()
            k.dma("cv", "sp", gnw_s[:], ins["gnw"], writes=[gnw_b])
            k.dma("cv", "sp", gnb_s[:], ins["gnb"], writes=[gnb_b])
            for e_ in range(NE):
                k.dma("cv", "sp", mb1_s[e_][:], ins["mb1"][e_], writes=[mb1_b[e_]])
                k.dma("cv", "sp", mb2_s[e_][:], ins["mb2"][e_], writes=[mb2_b[e_]])
            GNWc, GNWc_b = explode(gnw_s, gnw_b, 8)
            GNBc, GNBc_b = explode(gnb_s, gnb_b, 8)
            with contextlib.ExitStack() as qs:
                gbrow = sbg([1, 8], st=qs); gbrow_b = B()
                IOTC = sbg([128, 1], st=qs); IOTC_b = B()
                tI = sbg([128, 128], st=qs); tIb = B()
                tT = sbg([128, 128], st=qs); tTb = B()
                tmpc = sbg([128, 1], st=qs); tmpc_b = B(small=True)
                k.dma("cv", "sp", gbrow[:], ins["gb"], writes=[gbrow_b])
                k.emit("pool", lambda e: e.iota(tI[:], pattern=[[1, 128]], base=0,
                                                channel_multiplier=-1,
                                                allow_small_or_imprecise_dtypes=True), writes=[tIb])
                k.emit("dve", lambda e: e.tensor_scalar(IDF[:], tI[:], 0.0, None, ALU.is_equal),
                       reads=[tIb], writes=[IDF_b])
                k.emit("dve", lambda e: e.tensor_scalar(tT[:], tI[:], 0.0, None, ALU.is_ge),
                       reads=[tIb], writes=[tTb])
                k.emit("dve", lambda e: e.tensor_copy(TRIB[:], tT[:]), reads=[tTb], writes=[TRIB_b])
                k.emit("pool", lambda e: e.iota(IOTC[:], pattern=[[1, 1]], base=0,
                                                channel_multiplier=1,
                                                allow_small_or_imprecise_dtypes=True), writes=[IOTC_b])
                SELF = sbg([8, 1024], st=qs); SELF_b = B()
                selap = SELF[:].rearrange("k (e p) -> k e p", p=128)
                k.emit("pool", lambda e: e.iota(selap, pattern=[[1, 8], [0, 128]], base=0,
                                                channel_multiplier=-1,
                                                allow_small_or_imprecise_dtypes=True), writes=[SELF_b])
                k.emit("dve", lambda e: e.tensor_scalar(SELF[:], SELF[:], 0.0, None, ALU.is_equal),
                       reads=[SELF_b], writes=[SELF_b])
                k.emit("dve", lambda e: e.tensor_copy(SEL[:], SELF[:]),
                       reads=[SELF_b], writes=[SEL_b])
                for h in range(H1):
                    lg = LGAM[h]
                    k.emit("dve", lambda e: e.memset(tmpc[:], -lg), writes=[tmpc_b])
                    k.emit("act", lambda e: e.activation(GNEG[h][:], IOTC[:], AF.Exp, scale=tmpc[:]),
                           reads=[IOTC_b], sreads=[tmpc_b], writes=[GNEG_b[h]])
                with contextlib.ExitStack() as ps:
                    P_gb = psg([128, 8], ps); P_gb_b = B(psum=True)
                    k.emit("pe", lambda e: e.matmul(P_gb[:], ones1_f[:], gbrow[:], start=True, stop=True),
                           reads=[ones1_f_b, gbrow_b], writes=[P_gb_b])
                    k.emit("act", lambda e: e.copy(GBS[:], P_gb[:]), reads=[P_gb_b], writes=[GBS_b])
                    k.retire(P_gb_b)
                k.retire(gbrow_b, IOTC_b, tIb, tTb, tmpc_b, SELF_b)

        def tap(name, src_list, src_bufs):
            if name not in tapd:
                return
            for c in range(NCH):
                k.dma("ct", "pool", tapd[name][c], src_list[c][:], reads=[src_bufs[c]])

        def emit_load_x(s):
            for c in range(NCH):
                k.dma("cx", "sp", X[c][:], xin[s][c * 128:(c + 1) * 128, :], writes=[Xb[c]])

        def emit_store_x(s):
            for c in range(NCH):
                k.dma("cs0" if c % 2 == 0 else "cs1", "sp",
                      out[s][c * 128:(c + 1) * 128, :], X[c][:], reads=[Xb[c]])

        def emit_layernorm():
            with contextlib.ExitStack() as fs:
                SQ = [sbg([128, S], st=fs) for _ in range(2)]
                SQb = [B(), B()]
                R_mu = sbg([1, S], st=fs); R_mu_b = B()
                R_m2 = sbg([1, S], st=fs); R_m2_b = B()
                R_rs = sbg([1, S], st=fs); R_rs_b = B()
                T1 = [sbg([128, S], st=fs) for _ in range(2)]
                T1b = [B(), B()]
                with contextlib.ExitStack() as ps:
                    P_s1 = psg([1, S], ps); P_s1_b = B(psum=True)
                    P_s2 = psg([1, S], ps); P_s2_b = B(psum=True)
                    for c in range(NCH):
                        q = SQ[c % 2]; qb = SQb[c % 2]
                        k.emit("act", lambda e: e.activation(q[:], X[c][:], AF.Square),
                               reads=[Xb[c]], writes=[qb])
                        for nb in range(NB):
                            k.emit("pe", lambda e: e.matmul(
                                P_s1[:, nb * 512:(nb + 1) * 512], ones_f[:],
                                X[c][:, nb * 512:(nb + 1) * 512],
                                start=(c == 0), stop=(c == NCH - 1)),
                                reads=[ones_f_b, Xb[c]], writes=[P_s1_b])
                            k.emit("pe", lambda e: e.matmul(
                                P_s2[:, nb * 512:(nb + 1) * 512], ones_f[:],
                                q[:, nb * 512:(nb + 1) * 512],
                                start=(c == 0), stop=(c == NCH - 1)),
                                reads=[ones_f_b, qb], writes=[P_s2_b])
                    k.emit("dve", lambda e: e.tensor_scalar_mul(R_mu[:], P_s1[:], 1.0 / D),
                           reads=[P_s1_b], writes=[R_mu_b])
                    k.emit("dve", lambda e: e.tensor_scalar_mul(R_m2[:], P_s2[:], 1.0 / D),
                           reads=[P_s2_b], writes=[R_m2_b])
                    k.emit("dve", lambda e: e.tensor_mul(R_rs[:], R_mu[:], R_mu[:]),
                           reads=[R_mu_b], writes=[R_rs_b])
                    k.emit("dve", lambda e: e.tensor_sub(R_m2[:], R_m2[:], R_rs[:]),
                           reads=[R_m2_b, R_rs_b], writes=[R_m2_b])
                    k.emit("act", lambda e: e.activation(R_m2[:], R_m2[:], AF.Sqrt, bias=ceps[0:1, :]),
                           reads=[R_m2_b], sreads=[ceps_b], writes=[R_m2_b])
                    k.emit("dve", lambda e: e.reciprocal(R_rs[:], R_m2[:]),
                           reads=[R_m2_b], writes=[R_rs_b])
                    k.retire(P_s1_b, P_s2_b)
                with contextlib.ExitStack() as ps:
                    P_bm = psg([128, S], ps); P_bm_b = B(psum=True)
                    P_br = psg([128, S], ps); P_br_b = B(psum=True)
                    for nb in range(NB):
                        k.emit("pe", lambda e: e.matmul(
                            P_bm[:, nb * 512:(nb + 1) * 512], ones1_f[:],
                            R_mu[0:1, nb * 512:(nb + 1) * 512], start=True, stop=True),
                            reads=[ones1_f_b, R_mu_b], writes=[P_bm_b])
                        k.emit("pe", lambda e: e.matmul(
                            P_br[:, nb * 512:(nb + 1) * 512], ones1_f[:],
                            R_rs[0:1, nb * 512:(nb + 1) * 512], start=True, stop=True),
                            reads=[ones1_f_b, R_rs_b], writes=[P_br_b])
                    for c in range(NCH):
                        t = T1[c % 2]; tb = T1b[c % 2]
                        k.emit("dve", lambda e: e.tensor_sub(t[:], X[c][:], P_bm[:]),
                               reads=[Xb[c], P_bm_b], writes=[tb])
                        k.emit("dve", lambda e: e.tensor_mul(XN[c][:], t[:], P_br[:]),
                               reads=[tb, P_br_b], writes=[XNb[c]])
                    k.retire(P_bm_b, P_br_b)
                k.retire(SQb[0], SQb[1], T1b[0], T1b[1], R_mu_b, R_m2_b, R_rs_b)

        def emit_l0_attn():
            with contextlib.ExitStack() as fs:
                WOD = [sbg([128, 1024], BF16, st=fs) for _ in range(2)]
                WODb = [B(), B()]
                O_all = [sbg([128, S], BF16, st=fs) for _ in range(NCH)]
                O_allb = [B() for _ in range(NCH)]
                WP = [[sbg([128, 1024], BF16, st=fs) for _ in range(3)] for _ in range(2)]
                WPb = [[B() for _ in range(3)] for _ in range(2)]
                QA = [sbg([65, S], BF16, st=fs) for _ in range(2)]
                QAb = [B(), B()]
                KA = [sbg([65, S], BF16, st=fs) for _ in range(2)]
                KAb = [B(), B()]
                VA = [sbg([128, 16 * 65], BF16, st=fs) for _ in range(2)]
                VAb = [B(), B()]
                SQH = sbg([128, S], BF16, st=fs); SQHb = B()
                KM = [sbg([1, 1], st=fs) for _ in range(2)]
                KMb = [B(small=True), B(small=True)]
                PTS = [sbg([128, 512], BF16, st=fs) for _ in range(2)]
                PTSb = [B(), B()]
                R_inv = sbg([1, S], BF16, st=fs); R_invb = B()
                BCS = sbg([64, S], BF16, st=fs); BCSb = B()

                def load_pair(hp):
                    i = hp % 2
                    ch = "cw0" if i == 0 else "cw1"
                    for w, nm in enumerate(["wqp", "wkp", "wvp"]):
                        k.dma(ch, "pool", WP[i][w][:], ins[nm][hp], writes=[WPb[i][w]])

                load_pair(0)
                for hp in range(H0 // 2):
                    i = hp % 2
                    if hp + 1 < H0 // 2:
                        load_pair(hp + 1)
                    with contextlib.ExitStack() as ps:
                        P_A = psg([128, S], ps); P_A_b = B(psum=True)
                        P_NR = psg([1, S], ps); P_NR_b = B(psum=True)
                        for kc in range(NCH):
                            for nb in range(NB):
                                k.emit("pe", lambda e: e.matmul(
                                    P_A[:, nb * 512:(nb + 1) * 512],
                                    WP[i][1][:, kc * 128:(kc + 1) * 128],
                                    XN[kc][:, nb * 512:(nb + 1) * 512],
                                    start=(kc == 0), stop=(kc == NCH - 1)),
                                    reads=[WPb[i][1], XNb[kc]], writes=[P_A_b])
                        k.emit("act", lambda e: e.copy(KA[0][0:64, :], P_A[0:64, :]),
                               reads=[P_A_b], writes=[KAb[0]])
                        k.emit("act", lambda e: e.copy(KA[1][0:64, :], P_A[64:128, :]),
                               reads=[P_A_b], writes=[KAb[1]])
                        k.emit("dve", lambda e: e.memset(KA[0][64:65, :], -1.0), writes=[KAb[0]])
                        k.emit("dve", lambda e: e.memset(KA[1][64:65, :], -1.0), writes=[KAb[1]])
                        k.emit("act", lambda e: e.activation(SQH[:], P_A[:], AF.Square),
                               reads=[P_A_b], writes=[SQHb])
                        for j in range(2):
                            oh, ohb = (onesH0, onesH0_b) if j == 0 else (onesH1, onesH1_b)
                            for nb in range(NB):
                                k.emit("pe", lambda e: e.matmul(
                                    P_NR[:, nb * 512:(nb + 1) * 512], oh[:],
                                    SQH[:, nb * 512:(nb + 1) * 512], start=True, stop=True),
                                    reads=[ohb, SQHb], writes=[P_NR_b])
                            k.emit("dve", lambda e: e.reduce_max(KM[j][:], P_NR[:], axis=AX.X),
                                   reads=[P_NR_b], writes=[KMb[j]])
                        for kc in range(NCH):
                            for nb in range(NB):
                                k.emit("pe", lambda e: e.matmul(
                                    P_A[:, nb * 512:(nb + 1) * 512],
                                    WP[i][0][:, kc * 128:(kc + 1) * 128],
                                    XN[kc][:, nb * 512:(nb + 1) * 512],
                                    start=(kc == 0), stop=(kc == NCH - 1)),
                                    reads=[WPb[i][0], XNb[kc]], writes=[P_A_b])
                        k.emit("act", lambda e: e.activation(QA[0][0:64, :], P_A[0:64, :],
                                                             AF.Identity, scale=c125[0:64, :]),
                               reads=[P_A_b], sreads=[c125_b], writes=[QAb[0]])
                        k.emit("act", lambda e: e.activation(QA[1][0:64, :], P_A[64:128, :],
                                                             AF.Identity, scale=c125[64:128, :]),
                               reads=[P_A_b], sreads=[c125_b], writes=[QAb[1]])
                        k.emit("act", lambda e: e.activation(SQH[:], P_A[:], AF.Square, scale=c125[:]),
                               reads=[P_A_b], sreads=[c125_b], writes=[SQHb])
                        for j in range(2):
                            oh, ohb = (onesH0, onesH0_b) if j == 0 else (onesH1, onesH1_b)
                            for nb in range(NB):
                                k.emit("pe", lambda e: e.matmul(
                                    P_NR[:, nb * 512:(nb + 1) * 512], oh[:],
                                    SQH[:, nb * 512:(nb + 1) * 512], start=True, stop=True),
                                    reads=[ohb, SQHb], writes=[P_NR_b])
                            k.emit("act", lambda e: e.activation(QA[j][64:65, :], P_NR[:],
                                                                 AF.Sqrt, scale=KM[j][0:1, :]),
                                   reads=[P_NR_b], sreads=[KMb[j]], writes=[QAb[j]])
                        k.retire(P_A_b, P_NR_b)
                    with contextlib.ExitStack() as ps:
                        P_V = psg([128, 128], ps); P_V_b = B(psum=True)
                        k.emit("dve", lambda e: e.memset(VA[0][:, 64:16 * 65:65], 1.0), writes=[VAb[0]])
                        k.emit("dve", lambda e: e.memset(VA[1][:, 64:16 * 65:65], 1.0), writes=[VAb[1]])
                        for mc in range(16):
                            for kc in range(NCH):
                                k.emit("pe", lambda e: e.matmul(
                                    P_V[:], XN[kc][:, mc * 128:(mc + 1) * 128],
                                    WP[i][2][:, kc * 128:(kc + 1) * 128],
                                    start=(kc == 0), stop=(kc == NCH - 1)),
                                    reads=[XNb[kc], WPb[i][2]], writes=[P_V_b])
                            k.emit("act", lambda e: e.copy(VA[0][:, mc * 65:mc * 65 + 64], P_V[:, 0:64]),
                                   reads=[P_V_b], writes=[VAb[0]])
                            k.emit("act", lambda e: e.copy(VA[1][:, mc * 65:mc * 65 + 64], P_V[:, 64:128]),
                                   reads=[P_V_b], writes=[VAb[1]])
                        k.retire(P_V_b)
                    for j in range(2):
                        with contextlib.ExitStack() as hs:
                            P_O = psg([65, S], hs); P_O_b = B(psum=True)
                            with contextlib.ExitStack() as ts:
                                P_T = psg([128, S], ts); P_T_b = B(psum=True)
                                for mc in range(16):
                                    for nb in range(NB):
                                        k.emit("pe", lambda e: e.matmul(
                                            P_T[:, nb * 512:(nb + 1) * 512],
                                            KA[j][:, mc * 128:(mc + 1) * 128],
                                            QA[j][:, nb * 512:(nb + 1) * 512],
                                            start=True, stop=True),
                                            reads=[KAb[j], QAb[j]], writes=[P_T_b])
                                    for nb in range(NB):
                                        pt = PTS[nb % 2]; ptb = PTSb[nb % 2]
                                        k.emit("act", lambda e: e.activation(
                                            pt[:], P_T[:, nb * 512:(nb + 1) * 512], AF.Exp),
                                            reads=[P_T_b], writes=[ptb])
                                        k.emit("pe", lambda e: e.matmul(
                                            P_O[:, nb * 512:(nb + 1) * 512],
                                            VA[j][:, mc * 65:(mc + 1) * 65], pt[:],
                                            start=(mc == 0), stop=(mc == 15)),
                                            reads=[VAb[j], ptb], writes=[P_O_b])
                                k.retire(P_T_b)
                            k.emit("dve", lambda e: e.reciprocal(R_inv[:], P_O[64:65, :]),
                                   reads=[P_O_b], writes=[R_invb], low_prec=True)
                            with contextlib.ExitStack() as ts:
                                P_Bc = psg([64, S], ts); P_B_b = B(psum=True)
                                for nb in range(NB):
                                    k.emit("pe", lambda e: e.matmul(
                                        P_Bc[:, nb * 512:(nb + 1) * 512], ones1_b16[0:1, 0:64],
                                        R_inv[0:1, nb * 512:(nb + 1) * 512], start=True, stop=True),
                                        reads=[ones1_b16_b, R_invb], writes=[P_B_b])
                                k.emit("act", lambda e: e.copy(BCS[:], P_Bc[:]),
                                       reads=[P_B_b], writes=[BCSb])
                                ro = j * 64
                                k.emit("dve", lambda e: e.tensor_mul(
                                    O_all[hp][ro:ro + 64, :], P_O[0:64, :], BCS[:]),
                                    reads=[P_O_b, BCSb], writes=[O_allb[hp]])
                                k.retire(P_B_b)
                            k.retire(P_O_b)
                with contextlib.ExitStack() as ps:
                    P_P = [psg([128, S], ps) for _ in range(2)]
                    P_P_b = [B(psum=True), B(psum=True)]
                    k.dma("cw2", "pool", WOD[0][:], ins["wo"][0], writes=[WODb[0]])
                    for dc in range(NCH):
                        i2 = dc % 2
                        if dc + 1 < NCH:
                            k.dma("cw2" if (dc + 1) % 2 == 0 else "cw3", "pool",
                                  WOD[(dc + 1) % 2][:], ins["wo"][dc + 1],
                                  writes=[WODb[(dc + 1) % 2]])
                        pp = P_P[dc % 2]; ppb = P_P_b[dc % 2]
                        for oc in range(NCH):
                            for nb in range(NB):
                                k.emit("pe", lambda e: e.matmul(
                                    pp[:, nb * 512:(nb + 1) * 512],
                                    WOD[i2][:, oc * 128:(oc + 1) * 128],
                                    O_all[oc][:, nb * 512:(nb + 1) * 512],
                                    start=(oc == 0), stop=(oc == NCH - 1)),
                                    reads=[WODb[i2], O_allb[oc]], writes=[ppb])
                        k.emit("dve", lambda e: e.tensor_add(X[dc][:], X[dc][:], pp[:]),
                               reads=[Xb[dc], ppb], writes=[Xb[dc]])
                    k.retire(P_P_b[0], P_P_b[1])
                k.retire(*WODb, *O_allb, *QAb, *KAb, *VAb, SQHb, *KMb,
                         *PTSb, R_invb, BCSb, *WPb[0], *WPb[1])

        def emit_l0_ffn(h_only=False, quantum=True, nfb=NFB, nth=2):
            with contextlib.ExitStack() as fs:
                Hsb = [sbg([128, 1024], BF16, st=fs) for _ in range(NFB)]
                Hb = [B() for _ in range(NFB)]
                WF = [sbg([128, 1024], BF16, st=fs) for _ in range(2)]
                WFb = [B(), B()]
                W2D = [sbg([128, 2048], BF16, st=fs) for _ in range(2)]
                W2Db = [B(), B()]
                SC = sbg([128, 1024], BF16, st=fs); SCb = B()
                SS = sbg([128, 1024], BF16, st=fs); SSb = B()
                UU = sbg([128, 1024], st=fs); UUb = B()
                with contextlib.ExitStack() as ps:
                    P_H = [psg([128, 1024], ps) for _ in range(2)]
                    P_H_b = [B(psum=True), B(psum=True)]
                    P_Y = [psg([128, 1024], ps) for _ in range(2)]
                    P_Y_b = [B(psum=True), B(psum=True)]
                    for th in range(nth):
                        t0 = th * 1024
                        k.dma("cw0", "pool", WF[0][:], ins["w1f"][0], writes=[WFb[0]])
                        for fb in range(nfb):
                            i = fb % 2
                            if fb + 1 < nfb:
                                k.dma("cw0" if (fb + 1) % 2 == 0 else "cw1", "pool",
                                      WF[(fb + 1) % 2][:], ins["w1f"][fb + 1],
                                      writes=[WFb[(fb + 1) % 2]])
                            ph = P_H[i]; phb = P_H_b[i]
                            for kc in range(NCH):
                                for nb in range(2):
                                    k.emit("pe", lambda e: e.matmul(
                                        ph[:, nb * 512:(nb + 1) * 512],
                                        WF[i][:, kc * 128:(kc + 1) * 128],
                                        XN[kc][:, t0 + nb * 512:t0 + (nb + 1) * 512],
                                        start=(kc == 0), stop=(kc == NCH - 1)),
                                        reads=[WFb[i], XNb[kc]], writes=[phb])
                            if quantum:
                                k.emit("act", lambda e: e.activation(
                                    SC[:], ph[:], AF.Sin, bias=BCOSc[fb][:], scale=c01[:]),
                                    reads=[phb], sreads=[BCOSc_b[fb], c01_b], writes=[SCb])
                                k.emit("act", lambda e: e.activation(
                                    SS[:], ph[:], AF.Sin, bias=BSINc[fb][:], scale=c01[:]),
                                    reads=[phb], sreads=[BSINc_b[fb], c01_b], writes=[SSb])
                                k.emit("dve", lambda e: e.scalar_tensor_tensor(
                                    UU[:], SC[:], Ac[fb][:], ph[:], ALU.mult, ALU.add),
                                    reads=[SCb, phb], sreads=[Ac_b[fb]], writes=[UUb])
                                k.emit("dve", lambda e: e.scalar_tensor_tensor(
                                    UU[:], SS[:], Bc[fb][:], UU[:], ALU.mult, ALU.add),
                                    reads=[SSb, UUb], sreads=[Bc_b[fb]], writes=[UUb])
                                k.emit("act", lambda e: e.activation(
                                    Hsb[fb][:], UU[:], AF.Relu, bias=B1Cc[fb][:]),
                                    reads=[UUb], sreads=[B1Cc_b[fb]], writes=[Hb[fb]])
                            else:
                                k.emit("act", lambda e: e.activation(
                                    Hsb[fb][:], ph[:], AF.Relu, bias=B1Cc[fb][:]),
                                    reads=[phb], sreads=[B1Cc_b[fb]], writes=[Hb[fb]])
                        if h_only:
                            continue
                        if th == 0 and "hsb" in tapd:
                            for c8 in range(NCH):
                                k.dma("ct", "pool", tapd["hsb"][c8][:, 0:1024], Hsb[c8][:],
                                      reads=[Hb[c8]])
                        for dc in range(NCH):
                            i = dc % 2
                            k.dma("cw2", "pool", W2D[0][:], ins["w2f"][dc][:, 0:2048],
                                  writes=[W2Db[0]])
                            k.dma("cw3", "pool", W2D[1][:], ins["w2f"][dc][:, 2048:4096],
                                  writes=[W2Db[1]])
                            py = P_Y[i]; pyb = P_Y_b[i]
                            for fb in range(NFB):
                                hw = W2D[fb // 16]; hwb = W2Db[fb // 16]
                                fo = fb % 16
                                for nb in range(2):
                                    k.emit("pe", lambda e: e.matmul(
                                        py[:, nb * 512:(nb + 1) * 512],
                                        hw[:, fo * 128:(fo + 1) * 128],
                                        Hsb[fb][:, nb * 512:(nb + 1) * 512],
                                        start=(fb == 0), stop=(fb == NFB - 1)),
                                        reads=[hwb, Hb[fb]], writes=[pyb])
                            k.emit("dve", lambda e: e.scalar_tensor_tensor(
                                X[dc][:, t0:t0 + 1024], py[:], B2Cc[dc][:],
                                X[dc][:, t0:t0 + 1024], ALU.add, ALU.add),
                                reads=[pyb, Xb[dc]], sreads=[B2Cc_b[dc]], writes=[Xb[dc]])
                    k.retire(P_H_b[0], P_H_b[1], P_Y_b[0], P_Y_b[1])
                k.retire(*Hb, *WFb, *W2Db, SCb, SSb, UUb)

        def emit_gate():
            with contextlib.ExitStack() as fs:
                GWS = [sbg([128, 8], BF16, st=fs) for _ in range(NCH)]
                GWSb = [B() for _ in range(NCH)]
                for kc in range(NCH):
                    k.dma("cw4", "pool", GWS[kc][:], ins["gw"][kc], writes=[GWSb[kc]])
                GL = sbg([128, 8], st=fs); GLb = B(small=True)
                GL2 = sbg([128, 8], st=fs); GL2b = B(small=True)
                M1 = sbg([128, 8], st=fs); M1b = B(small=True)
                M2 = sbg([128, 8], st=fs); M2b = B(small=True)
                V1 = sbg([128, 1], st=fs); V1b = B(small=True)
                V2 = sbg([128, 1], st=fs); V2b = B(small=True)
                W1g = sbg([128, 1], st=fs); W1gb = B(small=True)
                W2g = sbg([128, 1], st=fs); W2gb = B(small=True)
                CMB = sbg([128, 8], st=fs); CMBb = B(small=True)
                with contextlib.ExitStack() as ps:
                    P_G = psg([128, 8], ps); P_G_b = B(psum=True)
                    P_TC = psg([8, 128], ps); P_TC_b = B(psum=True)
                    for tc in range(16):
                        for kc in range(NCH):
                            k.emit("pe", lambda e: e.matmul(
                                P_G[:], XN[kc][:, tc * 128:(tc + 1) * 128], GWS[kc][:],
                                start=(kc == 0), stop=(kc == NCH - 1)),
                                reads=[XNb[kc], GWSb[kc]], writes=[P_G_b])
                        k.emit("dve", lambda e: e.tensor_add(GL[:], P_G[:], GBS[:]),
                               reads=[P_G_b, GBS_b], writes=[GLb])
                        k.emit("dve", lambda e: e.reduce_max(V1[:], GL[:], axis=AX.X),
                               reads=[GLb], writes=[V1b])
                        k.emit("dve", lambda e: e.tensor_scalar(M1[:], GL[:], V1[:], None, ALU.is_ge),
                               reads=[GLb], sreads=[V1b], writes=[M1b])
                        k.emit("dve", lambda e: e.scalar_tensor_tensor(
                            GL2[:], M1[:], -1e30, GL[:], ALU.mult, ALU.add),
                            reads=[M1b, GLb], writes=[GL2b])
                        k.emit("dve", lambda e: e.reduce_max(V2[:], GL2[:], axis=AX.X),
                               reads=[GL2b], writes=[V2b])
                        k.emit("dve", lambda e: e.tensor_scalar(M2[:], GL2[:], V2[:], None, ALU.is_ge),
                               reads=[GL2b], sreads=[V2b], writes=[M2b])
                        k.emit("dve", lambda e: e.tensor_sub(V1[:], V1[:], V2[:]),
                               reads=[V1b, V2b], writes=[V1b])
                        k.emit("act", lambda e: e.activation(W1g[:], V1[:], AF.Sigmoid),
                               reads=[V1b], writes=[W1gb])
                        k.emit("dve", lambda e: e.tensor_scalar(W2g[:], W1g[:], -1.0, 1.0,
                                                                ALU.mult, ALU.add),
                               reads=[W1gb], writes=[W2gb])
                        k.emit("dve", lambda e: e.tensor_scalar(CMB[:], M1[:], W1g[:], None, ALU.mult),
                               reads=[M1b], sreads=[W1gb], writes=[CMBb])
                        k.emit("dve", lambda e: e.scalar_tensor_tensor(
                            CMB[:], M2[:], W2g[:], CMB[:], ALU.mult, ALU.add),
                            reads=[M2b, CMBb], sreads=[W2gb], writes=[CMBb])
                        k.emit("pe", lambda e: e.transpose(P_TC[:], CMB[:], IDF[:]),
                               reads=[CMBb, IDF_b], writes=[P_TC_b])
                        k.emit("dve", lambda e: e.tensor_copy(combT[:, tc * 128:(tc + 1) * 128], P_TC[:]),
                               reads=[P_TC_b], writes=[combTb])
                    k.retire(P_G_b, P_TC_b)
                k.retire(*GWSb, GLb, GL2b, M1b, M2b, V1b, V2b, W1gb, W2gb, CMBb)

        def emit_l1_attn():
            with contextlib.ExitStack() as fs:
                RETN = [sbg([128, S], BF16, st=fs) for _ in range(NCH)]
                RETNb = [B() for _ in range(NCH)]
                with contextlib.ExitStack() as fa:
                    WQ1 = [sbg([128, 128], BF16, st=fa) for _ in range(2)]
                    WQ1b = [B(), B()]
                    WK1 = [sbg([128, 128], BF16, st=fa) for _ in range(2)]
                    WK1b = [B(), B()]
                    WV1 = [sbg([128, 128], BF16, st=fa) for _ in range(2)]
                    WV1b = [B(), B()]
                    QH = sbg([128, S], BF16, st=fa); QHb = B()
                    KH = sbg([128, S], BF16, st=fa); KHb = B()
                    VH = sbg([128, S], BF16, st=fa); VHb = B()
                    SDT = sbg([128, S], BF16, st=fa); SDTb = B()
                    GBR = sbg([128, S], BF16, st=fa); GBRb = B()
                    NROW = sbg([128, S], st=fa); NROWb = B()
                    GAMC = sbg([128, 1], st=fa); GAMCb = B(small=True)
                    RS = sbg([128, 1024], st=fa); RSb = B()
                    RS2 = sbg([128, 1024], st=fa); RS2b = B()
                    R_m = sbg([1, 1024], st=fa); R_mb = B()
                    R_rs = sbg([1, 1024], st=fa); R_rsb = B()
                    R_nm = sbg([1, 1024], st=fa); R_nmb = B()
                    k.emit("pool", lambda e: e.iota(NROW[:], pattern=[[1, S]], base=0,
                                                    channel_multiplier=0,
                                                    allow_small_or_imprecise_dtypes=True),
                           writes=[NROWb])

                    def load_h(h):
                        i = h % 2
                        ch = "cw0" if i == 0 else "cw1"
                        k.dma(ch, "pool", WQ1[i][:], ins["wq1"][h], writes=[WQ1b[i]])
                        k.dma(ch, "pool", WK1[i][:], ins["wk1"][h], writes=[WK1b[i]])
                        k.dma(ch, "pool", WV1[i][:], ins["wv1"][h], writes=[WV1b[i]])

                    load_h(0)
                    for h in range(H1):
                        i = h % 2
                        if h + 1 < H1:
                            load_h(h + 1)
                        lg = LGAM[h]
                        k.emit("dve", lambda e: e.memset(GAMC[:], lg), writes=[GAMCb])
                        k.emit("act", lambda e: e.activation(GBR[:], NROW[:], AF.Exp, scale=GAMC[:]),
                               reads=[NROWb], sreads=[GAMCb], writes=[GBRb])
                        with contextlib.ExitStack() as qs:
                            P_A = psg([128, S], qs); P_A_b = B(psum=True)
                            for nb in range(NB):
                                k.emit("pe", lambda e: e.matmul(
                                    P_A[:, nb * 512:(nb + 1) * 512], WQ1[i][:],
                                    XN[h][:, nb * 512:(nb + 1) * 512], start=True, stop=True),
                                    reads=[WQ1b[i], XNb[h]], writes=[P_A_b])
                            k.emit("dve", lambda e: e.tensor_mul(QH[:], P_A[:], GBR[:]),
                                   reads=[P_A_b, GBRb], writes=[QHb])
                            for nb in range(NB):
                                k.emit("pe", lambda e: e.matmul(
                                    P_A[:, nb * 512:(nb + 1) * 512], WK1[i][:],
                                    XN[h][:, nb * 512:(nb + 1) * 512], start=True, stop=True),
                                    reads=[WK1b[i], XNb[h]], writes=[P_A_b])
                            k.emit("act", lambda e: e.copy(KH[:], P_A[:]), reads=[P_A_b], writes=[KHb])
                            k.retire(P_A_b)
                        with contextlib.ExitStack() as vs:
                            P_V = psg([128, 128], vs); P_V_b = B(psum=True)
                            for mc in range(16):
                                k.emit("pe", lambda e: e.matmul(
                                    P_V[:], XN[h][:, mc * 128:(mc + 1) * 128], WV1[i][:],
                                    start=True, stop=True),
                                    reads=[XNb[h], WV1b[i]], writes=[P_V_b])
                                gm0 = float(np.exp(-lg * (mc * 128)))
                                k.emit("dve", lambda e: e.tensor_scalar(
                                    VH[:, mc * 128:(mc + 1) * 128], P_V[:], GNEG[h][:], gm0,
                                    ALU.mult, ALU.mult),
                                    reads=[P_V_b, GNEG_b[h]], writes=[VHb])
                            k.retire(P_V_b)
                        with contextlib.ExitStack() as rs_:
                            P_R = psg([128, S], rs_); P_R_b = B(psum=True)
                            with contextlib.ExitStack() as ts:
                                P_T = [psg([128, 512], ts) for _ in range(2)]
                                P_T_b = [B(psum=True), B(psum=True)]
                                for mc in range(16):
                                    m0 = mc * 128
                                    blocks = []
                                    n0 = m0
                                    while n0 < S:
                                        n1 = min(S, (n0 // 512 + 1) * 512)
                                        blocks.append((n0, n1))
                                        n0 = n1
                                    for bi, (n0, n1) in enumerate(blocks):
                                        pt = P_T[bi % 2]; ptb = P_T_b[bi % 2]
                                        k.emit("pe", lambda e: e.matmul(
                                            pt[:, 0:n1 - n0], KH[:, mc * 128:(mc + 1) * 128],
                                            QH[:, n0:n1], start=True, stop=True),
                                            reads=[KHb, QHb], writes=[ptb])
                                        k.emit("act", lambda e: e.copy(SDT[:, n0:n1], pt[:, 0:n1 - n0]),
                                               reads=[ptb], writes=[SDTb])
                                        if bi == 0:
                                            k.emit("dve", lambda e: e.tensor_mul(
                                                SDT[:, m0:m0 + 128], SDT[:, m0:m0 + 128], TRIB[:]),
                                                reads=[SDTb, TRIB_b], writes=[SDTb])
                                        for nq in range(n0, n1, 512):
                                            nq1 = min(n1, nq + 512)
                                            k.emit("pe", lambda e: e.matmul(
                                                P_R[:, nq:nq1], VH[:, mc * 128:(mc + 1) * 128],
                                                SDT[:, nq:nq1], start=(mc == 0),
                                                stop=(mc == 15)),
                                                reads=[VHb, SDTb], writes=[P_R_b])
                                k.retire(P_T_b[0], P_T_b[1])
                            for nh in range(2):
                                o0 = nh * 1024
                                k.emit("act", lambda e: e.copy(RS[:], P_R[:, o0:o0 + 1024]),
                                       reads=[P_R_b], writes=[RSb])
                                k.emit("act", lambda e: e.activation(RS2[:], P_R[:, o0:o0 + 1024],
                                                                     AF.Square),
                                       reads=[P_R_b], writes=[RS2b])
                                with contextlib.ExitStack() as gs:
                                    P_r1 = psg([1, 1024], gs); P_r1_b = B(psum=True)
                                    P_r2 = psg([1, 1024], gs); P_r2_b = B(psum=True)
                                    for nb in range(2):
                                        k.emit("pe", lambda e: e.matmul(
                                            P_r1[:, nb * 512:(nb + 1) * 512], ones_f[:],
                                            RS[:, nb * 512:(nb + 1) * 512], start=True, stop=True),
                                            reads=[ones_f_b, RSb], writes=[P_r1_b])
                                        k.emit("pe", lambda e: e.matmul(
                                            P_r2[:, nb * 512:(nb + 1) * 512], ones_f[:],
                                            RS2[:, nb * 512:(nb + 1) * 512], start=True, stop=True),
                                            reads=[ones_f_b, RS2b], writes=[P_r2_b])
                                    k.emit("dve", lambda e: e.tensor_scalar_mul(R_m[:], P_r1[:],
                                                                                1.0 / 128.0),
                                           reads=[P_r1_b], writes=[R_mb])
                                    k.emit("dve", lambda e: e.tensor_scalar_mul(R_rs[:], P_r2[:],
                                                                                1.0 / 128.0),
                                           reads=[P_r2_b], writes=[R_rsb])
                                    k.emit("dve", lambda e: e.tensor_mul(R_nm[:], R_m[:], R_m[:]),
                                           reads=[R_mb], writes=[R_nmb])
                                    k.emit("dve", lambda e: e.tensor_sub(R_rs[:], R_rs[:], R_nm[:]),
                                           reads=[R_rsb, R_nmb], writes=[R_rsb])
                                    k.emit("act", lambda e: e.activation(R_rs[:], R_rs[:], AF.Sqrt,
                                                                         bias=ceps[0:1, :]),
                                           reads=[R_rsb], sreads=[ceps_b], writes=[R_rsb])
                                    k.emit("dve", lambda e: e.reciprocal(R_rs[:], R_rs[:]),
                                           reads=[R_rsb], writes=[R_rsb])
                                    k.emit("dve", lambda e: e.tensor_mul(R_nm[:], R_m[:], R_rs[:]),
                                           reads=[R_mb, R_rsb], writes=[R_nmb])
                                    k.emit("dve", lambda e: e.tensor_scalar_mul(R_nm[:], R_nm[:], -1.0),
                                           reads=[R_nmb], writes=[R_nmb])
                                    k.retire(P_r1_b, P_r2_b)
                                with contextlib.ExitStack() as gs:
                                    P_b1 = psg([128, 1024], gs); P_b1_b = B(psum=True)
                                    P_b2g = psg([128, 1024], gs); P_b2_b = B(psum=True)
                                    for nb in range(2):
                                        k.emit("pe", lambda e: e.matmul(
                                            P_b1[:, nb * 512:(nb + 1) * 512], ones1_f[:],
                                            R_rs[0:1, nb * 512:(nb + 1) * 512], start=True, stop=True),
                                            reads=[ones1_f_b, R_rsb], writes=[P_b1_b])
                                        k.emit("pe", lambda e: e.matmul(
                                            P_b2g[:, nb * 512:(nb + 1) * 512], ones1_f[:],
                                            R_nm[0:1, nb * 512:(nb + 1) * 512], start=True, stop=True),
                                            reads=[ones1_f_b, R_nmb], writes=[P_b2_b])
                                    k.emit("dve", lambda e: e.tensor_mul(RS[:], RS[:], P_b1[:]),
                                           reads=[RSb, P_b1_b], writes=[RSb])
                                    k.emit("dve", lambda e: e.tensor_add(RS[:], RS[:], P_b2g[:]),
                                           reads=[RSb, P_b2_b], writes=[RSb])
                                    k.emit("dve", lambda e: e.tensor_scalar(
                                        RETN[h][:, o0:o0 + 1024], RS[:], GNWc[h][:],
                                        GNBc[h][:], ALU.mult, ALU.add),
                                        reads=[RSb], sreads=[GNWc_b[h], GNBc_b[h]], writes=[RETNb[h]])
                                    k.retire(P_b1_b, P_b2_b)
                            k.retire(P_R_b)
                    k.retire(*WQ1b, *WK1b, *WV1b, QHb, KHb, VHb, SDTb, GBRb, NROWb, GAMCb,
                             RSb, RS2b, R_mb, R_rsb, R_nmb)
                with contextlib.ExitStack() as fb_:
                    W11S = [sbg([128, 1024], BF16, st=fb_) for _ in range(NCH)]
                    W11b = [B() for _ in range(NCH)]
                    W21S = [sbg([128, 1024], BF16, st=fb_) for _ in range(NCH)]
                    W21b = [B() for _ in range(NCH)]
                    GS = sbg([128, S], BF16, st=fb_); GSb = B()
                    for kc in range(NCH):
                        k.dma("cw2", "pool", W11S[kc][:], ins["w11"][kc], writes=[W11b[kc]])
                        k.dma("cw3", "pool", W21S[kc][:], ins["w21"][kc], writes=[W21b[kc]])
                    with contextlib.ExitStack() as ps:
                        P_G = [psg([128, S], ps) for _ in range(2)]
                        P_G_b = [B(psum=True), B(psum=True)]
                        for dc in range(NCH):
                            pg = P_G[dc % 2]; pgb = P_G_b[dc % 2]
                            for kc in range(NCH):
                                for nb in range(NB):
                                    k.emit("pe", lambda e: e.matmul(
                                        pg[:, nb * 512:(nb + 1) * 512],
                                        W11S[kc][:, dc * 128:(dc + 1) * 128],
                                        XN[kc][:, nb * 512:(nb + 1) * 512],
                                        start=(kc == 0), stop=(kc == NCH - 1)),
                                        reads=[W11b[kc], XNb[kc]], writes=[pgb])
                            k.emit("act", lambda e: e.activation(GS[:], pg[:], AF.Silu),
                                   reads=[pgb], writes=[GSb])
                            k.emit("dve", lambda e: e.tensor_add(RETN[dc][:], GS[:], RETN[dc][:]),
                                   reads=[GSb, RETNb[dc]], writes=[RETNb[dc]])
                        k.retire(P_G_b[0], P_G_b[1])
                    with contextlib.ExitStack() as ps:
                        P_Y2 = [psg([128, S], ps) for _ in range(2)]
                        P_Y2_b = [B(psum=True), B(psum=True)]
                        for dc in range(NCH):
                            py = P_Y2[dc % 2]; pyb = P_Y2_b[dc % 2]
                            for oc in range(NCH):
                                for nb in range(NB):
                                    k.emit("pe", lambda e: e.matmul(
                                        py[:, nb * 512:(nb + 1) * 512],
                                        W21S[oc][:, dc * 128:(dc + 1) * 128],
                                        RETN[oc][:, nb * 512:(nb + 1) * 512],
                                        start=(oc == 0), stop=(oc == NCH - 1)),
                                        reads=[W21b[oc], RETNb[oc]], writes=[pyb])
                            k.emit("dve", lambda e: e.tensor_add(X[dc][:], X[dc][:], py[:]),
                                   reads=[Xb[dc], pyb], writes=[Xb[dc]])
                        k.retire(P_Y2_b[0], P_Y2_b[1])
                    k.retire(*W11b, *W21b, GSb)
                k.retire(*RETNb)

        def emit_moe():
            with contextlib.ExitStack() as fs:
                MB1c = []; MB1c_b = []
                MB2c = []; MB2c_b = []
                for e_ in range(NE):
                    t_, b_ = explode(mb1_s[e_], mb1_b[e_], 32, st=fs)
                    MB1c.append(t_); MB1c_b.append(b_)
                    t_, b_ = explode(mb2_s[e_], mb2_b[e_], 8, st=fs)
                    MB2c.append(t_); MB2c_b.append(b_)
                Hsb = [sbg([128, 1024], BF16, st=fs) for _ in range(NFB)]
                Hb = [B() for _ in range(NFB)]
                WF1 = [sbg([128, 1024], BF16, st=fs) for _ in range(2)]
                WF1b = [B(), B()]
                MW2 = [sbg([128, 2048], BF16, st=fs) for _ in range(2)]
                MW2b = [B(psum=False), B()]
                CBS = [sbg([128, 1024], BF16, st=fs) for _ in range(2)]
                CBSb = [B(), B()]
                T1 = sbg([128, 1024], st=fs); T1b = B()
                for e_ in range(NE):
                    with contextlib.ExitStack() as cs:
                        P_CB = psg([128, 1024], cs); P_CB_b = B(psum=True)
                        for th in range(2):
                            for nb in range(2):
                                k.emit("pe", lambda e: e.matmul(
                                    P_CB[:, nb * 512:(nb + 1) * 512],
                                    SEL[:, e_ * 128:(e_ + 1) * 128],
                                    combT[:, th * 1024 + nb * 512:th * 1024 + (nb + 1) * 512],
                                    start=True, stop=True),
                                    reads=[SEL_b, combTb], writes=[P_CB_b])
                            k.emit("act", lambda e: e.copy(CBS[th][:], P_CB[:]),
                                   reads=[P_CB_b], writes=[CBSb[th]])
                        k.retire(P_CB_b)
                    with contextlib.ExitStack() as ps:
                        P_H = [psg([128, 1024], ps) for _ in range(2)]
                        P_H_b = [B(psum=True), B(psum=True)]
                        P_Y = [psg([128, 1024], ps) for _ in range(2)]
                        P_Y_b = [B(psum=True), B(psum=True)]
                        for th in range(2):
                            t0 = th * 1024
                            k.dma("cw4", "pool", WF1[0][:], ins["mw1"][e_, 0], writes=[WF1b[0]])
                            for fb in range(NFB):
                                i = fb % 2
                                if fb + 1 < NFB:
                                    k.dma("cw4" if (fb + 1) % 2 == 0 else "cw5", "pool",
                                          WF1[(fb + 1) % 2][:], ins["mw1"][e_, fb + 1],
                                          writes=[WF1b[(fb + 1) % 2]])
                                ph = P_H[i]; phb = P_H_b[i]
                                for kc in range(NCH):
                                    for nb in range(2):
                                        k.emit("pe", lambda e: e.matmul(
                                            ph[:, nb * 512:(nb + 1) * 512],
                                            WF1[i][:, kc * 128:(kc + 1) * 128],
                                            XN[kc][:, t0 + nb * 512:t0 + (nb + 1) * 512],
                                            start=(kc == 0), stop=(kc == NCH - 1)),
                                            reads=[WF1b[i], XNb[kc]], writes=[phb])
                                k.emit("act", lambda e: e.activation(
                                    Hsb[fb][:], ph[:], AF.Gelu, bias=MB1c[e_][fb][:]),
                                    reads=[phb], sreads=[MB1c_b[e_][fb]], writes=[Hb[fb]])
                            for dc in range(NCH):
                                k.dma("cw2", "pool", MW2[0][:], ins["mw2"][e_, dc][:, 0:2048],
                                      writes=[MW2b[0]])
                                k.dma("cw3", "pool", MW2[1][:], ins["mw2"][e_, dc][:, 2048:4096],
                                      writes=[MW2b[1]])
                                py = P_Y[dc % 2]; pyb = P_Y_b[dc % 2]
                                for fb in range(NFB):
                                    hw = MW2[fb // 16]; hwb = MW2b[fb // 16]
                                    fo = fb % 16
                                    for nb in range(2):
                                        k.emit("pe", lambda e: e.matmul(
                                            py[:, nb * 512:(nb + 1) * 512],
                                            hw[:, fo * 128:(fo + 1) * 128],
                                            Hsb[fb][:, nb * 512:(nb + 1) * 512],
                                            start=(fb == 0), stop=(fb == NFB - 1)),
                                            reads=[hwb, Hb[fb]], writes=[pyb])
                                k.emit("dve", lambda e: e.scalar_tensor_tensor(
                                    T1[:], py[:], MB2c[e_][dc][:], CBS[th][:],
                                    ALU.add, ALU.mult),
                                    reads=[pyb, CBSb[th]], sreads=[MB2c_b[e_][dc]], writes=[T1b])
                                k.emit("dve", lambda e: e.tensor_add(
                                    X[dc][:, t0:t0 + 1024], X[dc][:, t0:t0 + 1024], T1[:]),
                                    reads=[T1b, Xb[dc]], writes=[Xb[dc]])
                        k.retire(P_H_b[0], P_H_b[1], P_Y_b[0], P_Y_b[1])
                k.retire(*Hb, *WF1b, *MW2b, *CBSb, T1b)

        def emit_body(s):
            want_taps = (not loop) and s == 0
            emit_load_x(s)
            if 0 in layers:
                if "ln" in phases:
                    emit_layernorm()
                    if want_taps:
                        tap("xn0", XN, XNb)
                if "attn" in phases:
                    emit_l0_attn()
                    if want_taps:
                        tap("att0", X, Xb)
                if "ffnH" in phases:
                    emit_l0_ffn(h_only=True)
                if "ffnH4" in phases:
                    emit_l0_ffn(h_only=True, nfb=4, nth=1)
                if "ffnH4p" in phases:
                    emit_l0_ffn(h_only=True, quantum=False, nfb=4, nth=1)
                if "ffnH1p" in phases:
                    emit_l0_ffn(h_only=True, quantum=False, nfb=1, nth=1)
                if "ffnP" in phases:
                    emit_l0_ffn(quantum=False)
                if "ffnH1q" in phases:
                    emit_l0_ffn(h_only=True, quantum=True, nfb=1, nth=1)
                if "ffnH2q" in phases:
                    emit_l0_ffn(h_only=True, quantum=True, nfb=2, nth=1)
                if "ffn" in phases:
                    emit_l0_ffn()
                    if want_taps:
                        tap("x1", X, Xb)
            if 1 in layers:
                if "ln" in phases:
                    emit_layernorm()
                    if want_taps:
                        tap("xn1", XN, XNb)
                if "gate" in phases:
                    emit_gate()
                if "l1attn" in phases:
                    emit_l1_attn()
                    if want_taps:
                        tap("att1", X, Xb)
                if "moe" in phases:
                    emit_moe()
            emit_store_x(s)

        stats_out = None
        if not loop:
            for s in range(nseq):
                emit_body(s)
            for ch in ("cs0", "cs1", "cx", "cv", "ct"):
                if k.cnt[ch]:
                    nc.sync.wait_ge(k.h[ch], k.cnt[ch])
        else:
            k.begin_loop(loop_stats)
            if loop_stats is None:
                emit_body(0)
                stats_out = k.collect_stats(_buf_reg)
            else:
                with nc.Fori(0, nseq) as i_sym:
                    emit_body(i_sym)
                    k.end_body()
                stats_out = k.collect_stats(_buf_reg)
                for ch in DMACH:
                    d = k.delta.get(ch, 0)
                    final = k.cnt0[ch] + d * (nseq + 1)
                    if final:
                        nc.sync.wait_ge(k.h[ch], final)
    return (nc, stats_out) if loop else nc


def _prep_inputs(inp):
    f32 = np.float32
    A = lambda x: np.ascontiguousarray(np.asarray(x, f32))
    d = {}
    d["x"] = np.ascontiguousarray(A(inp["L0_attn_x"]).transpose(0, 2, 1))
    pr = lambda w: np.ascontiguousarray(
        w.reshape(8, 128, 8, 128).transpose(2, 1, 0, 3)).reshape(8, 128, 1024)
    d["wqp"] = pr(A(inp["L0_attn_w_q"]).T)
    d["wkp"] = pr(A(inp["L0_attn_w_k"]).T)
    d["wvp"] = pr(A(inp["L0_attn_w_v"]).T)
    d["wo"] = pr(A(inp["L0_attn_w_o"]).T)
    W1 = A(inp["L0_ffn_W1"]); W2 = A(inp["L0_ffn_W2"])
    d["w1f"] = np.ascontiguousarray(
        W1.reshape(8, 128, 32, 128).transpose(2, 1, 0, 3)).reshape(32, 128, 1024)
    d["w2f"] = np.ascontiguousarray(
        W2.reshape(32, 128, 8, 128).transpose(2, 1, 0, 3)).reshape(8, 128, 4096)
    d["b1c"] = np.ascontiguousarray(A(inp["L0_ffn_b1"]).reshape(32, 128).T)
    d["b2c"] = np.ascontiguousarray(A(inp["L0_ffn_b2"]).reshape(8, 128).T)
    d["thc"] = np.ascontiguousarray(A(inp["L0_ffn_quantum_phases"]).reshape(32, 128).T)
    d["wrc"] = np.ascontiguousarray(A(inp["L0_ffn_quantum_weights_real"]).reshape(32, 128).T)
    d["wic"] = np.ascontiguousarray(A(inp["L0_ffn_quantum_weights_imag"]).reshape(32, 128).T)
    d["wq1"] = A(inp["L1_attn_Wq"]); d["wk1"] = A(inp["L1_attn_Wk"]); d["wv1"] = A(inp["L1_attn_Wv"])
    d["w11"] = np.ascontiguousarray(A(inp["L1_attn_W1"]).reshape(8, 128, 1024))
    d["w21"] = np.ascontiguousarray(A(inp["L1_attn_W2"]).reshape(8, 128, 1024))
    d["gnw"] = np.ascontiguousarray(A(inp["L1_attn_gn_w"]).reshape(8, 128).T)
    d["gnb"] = np.ascontiguousarray(A(inp["L1_attn_gn_b"]).reshape(8, 128).T)
    d["gw"] = np.ascontiguousarray(A(inp["L1_ffn_gate_w"]).reshape(8, 128, 8))
    d["gb"] = np.ascontiguousarray(A(inp["L1_ffn_gate_b"]).reshape(1, 8))
    d["mw1"] = np.ascontiguousarray(
        A(inp["L1_ffn_w1"]).reshape(8, 8, 128, 32, 128).transpose(0, 3, 2, 1, 4)).reshape(8, 32, 128, 1024)
    d["mw2"] = np.ascontiguousarray(
        A(inp["L1_ffn_w2"]).reshape(8, 32, 128, 8, 128).transpose(0, 3, 2, 1, 4)).reshape(8, 8, 128, 4096)
    d["mb1"] = np.ascontiguousarray(A(inp["L1_ffn_b1"]).reshape(8, 32, 128).transpose(0, 2, 1))
    d["mb2"] = np.ascontiguousarray(A(inp["L1_ffn_b2"]).reshape(8, 8, 128).transpose(0, 2, 1))
    return d


def kernel(**inputs):
    d = _prep_inputs(inputs)
    nc1, st1 = build_program(loop=True, loop_stats=None)
    nc, st2 = build_program(loop=True, loop_stats=st1)
    if st2["wait_pairs"] - st1["wait_pairs"]:
        nc, _ = build_program(loop=True, loop_stats=st2)
    wkeys = [k_ for k_ in d if k_ != "x"]
    in_maps = []
    for c in range(8):
        m = {"x": np.ascontiguousarray(d["x"][c * NSEQ:(c + 1) * NSEQ])}
        for k_ in wkeys:
            m[k_] = d[k_]
        in_maps.append(m)
    res = run_bass_kernel_spmd(nc, in_maps, core_ids=list(range(8)), trace_cores=[0])
    outs = [np.asarray(r["out"]) for r in res.results]
    full = np.concatenate(outs, axis=0)
    return np.ascontiguousarray(full.transpose(0, 2, 1))


# version 07
# speedup vs baseline: 1.1614x; 1.0229x over previous
import contextlib
import numpy as np
import concourse.bass as bass
import concourse.mybir as mybir
from concourse.bass_utils import run_bass_kernel_spmd

F32 = mybir.dt.float32
BF16 = mybir.dt.bfloat16
I32 = mybir.dt.int32
AF = mybir.ActivationFunctionType
ALU = mybir.AluOpType
AX = mybir.AxisListType
PI = float(np.pi)

S = 2048
D = 1024
NCH = 8
FF = 4096
NFB = 32
NSEQ = 4
NB = S // 512
EPS = 1e-5
H0 = 16
H1 = 8
NE = 8

_lin = np.linspace(np.log(1.0 / 512.0), np.log(1.0 / 32.0), H1)
GAMMAS = (1.0 - np.exp(_lin)).astype(np.float64)
LGAM = [float(np.log(g)) for g in GAMMAS]

COMPUTE = ("pe", "act", "dve", "pool")
DMACH = ["cx", "cs0", "cs1", "cw0", "cw1", "cw2", "cw3", "cw4", "cw5", "cv", "ct"]
_uid = [0]


_buf_seq = [0]
_buf_reg = []


class Buf:
    __slots__ = ("w", "r", "fresh", "psum", "small", "bid")

    def __init__(self, psum=False, small=False):
        self.w = None
        self.r = {}
        self.fresh = True
        self.psum = psum
        self.small = small
        self.bid = _buf_seq[0]
        _buf_seq[0] += 1
        _buf_reg.append(self)


class K:
    ENAME2ENG = {"pe": "tensor", "act": "scalar", "dve": "vector",
                 "pool": "gpsimd", "sp": "sync"}

    def __init__(self, nc, es):
        self.nc = nc
        self.eng = {"pe": nc.tensor, "act": nc.scalar, "dve": nc.vector,
                    "pool": nc.gpsimd, "sp": nc.sync}
        self.h = {}
        self.cnt = {}
        for name in list(COMPUTE) + DMACH:
            self.h[name] = es.enter_context(nc.semaphore(name))
            self.cnt[name] = 0
        self.waited = {}
        self.legacy = {}
        self.legacy_ps = {}
        self.epoch = None
        self.cnt0 = None
        self.delta = None
        self.lwaited = {}
        self.bregs = {}
        self.tmpreg = {}
        self.wait_pairs = set()
        self.chan_buf = {n: Buf() for n in DMACH}

    def begin_loop(self, stats=None):
        self.cnt0 = dict(self.cnt)
        self.epoch = 0
        self.lwaited = {}
        if stats is not None:
            self.delta = stats["delta"]
            for sem in self.cnt:
                if self.cnt0[sem] > 0 and self.delta.get(sem, 0) > 0:
                    self.eng["sp"].wait_ge(self.h[sem], self.cnt0[sem])
            for sem in COMPUTE:
                if self.cnt0[sem] > 0:
                    self.eng["sp"].wait_ge(self.h[sem], self.cnt0[sem])
            for sem, d in self.delta.items():
                if d > 0:
                    self.eng["sp"].sem_inc(self.h[sem], d)
            for (ename, sem) in sorted(stats["wait_pairs"]):
                if self.delta.get(sem, 0) == 0:
                    continue
                r = self.eng[ename].alloc_register(f"B_{ename}_{sem}")
                self.eng[ename].reg_mov(r, self.cnt0[sem] + self.delta[sem])
                self.bregs[(ename, sem)] = r
            for ename in ("pe", "act", "dve", "pool", "sp"):
                self.tmpreg[ename] = self.eng[ename].alloc_register(f"T_{ename}")
            by_id = stats["bufs_by_id"]
            for b in _buf_reg:
                st = by_id.get(b.bid)
                if st is None:
                    continue
                w, r_ = st
                if w is not None:
                    b.w = w
                if r_:
                    b.r = dict(r_)

    def end_body(self):
        for (ename, sem), r in self.bregs.items():
            self.eng[ename].reg_alu(r, r, self.delta[sem], mybir.AluOpType.add)

    def collect_stats(self, all_bufs):
        delta = {sem: self.cnt[sem] - self.cnt0[sem] for sem in self.cnt}
        bufs_by_id = {}
        def shift(tok):
            sem, cnt, ep = tok
            if ep is None:
                return tok
            return (sem, cnt, ep - 1)
        for b in all_bufs:
            w = shift(b.w) if b.w is not None else None
            r_ = {sem: (cnt, (ep - 1) if ep is not None else None)
                  for sem, (cnt, ep) in b.r.items()}
            bufs_by_id[b.bid] = (w, r_)
        return {"delta": delta, "wait_pairs": set(self.wait_pairs),
                "bufs_by_id": bufs_by_id}

    def _wait_tok(self, ename, tok, force=False):
        sem, cnt, ep = tok
        if cnt <= 0:
            return
        if sem == ename and sem in COMPUTE and not force:
            return
        key = (ename, sem)
        if self.epoch is None or ep is None:
            if self.waited.get(key, 0) >= cnt:
                return
            self.eng[ename].wait_ge(self.h[sem], cnt)
            self.waited[key] = cnt
            return
        self.wait_pairs.add(key)
        x = (cnt - self.cnt0[sem]) + self.delta_of(sem) * ep
        if self.lwaited.get(key, -10**9) >= x:
            return
        self.lwaited[key] = x
        if self.delta is None:
            return
        breg = self.bregs.get(key)
        if breg is None:
            return
        t = self.tmpreg[ename]
        self.eng[ename].reg_alu(t, breg, x, mybir.AluOpType.add)
        self.eng[ename].wait_ge(self.h[sem], t)

    def delta_of(self, sem):
        if self.delta is not None:
            return self.delta.get(sem, 0)
        return 0

    def _wait(self, ename, sem, cnt, force=False):
        self._wait_tok(ename, (sem, cnt, None), force)

    def _pre(self, ename, reads, writes, sreads=()):
        for b in sreads:
            if b.w is not None:
                self._wait_tok(ename, b.w, force=True)
        for b in reads:
            if b.w is not None:
                self._wait_tok(ename, b.w, force=b.small)
            if b.psum:
                for sem, (cnt, ep) in b.r.items():
                    if sem != ename:
                        self._wait_tok(ename, (sem, cnt, ep))
        for b in writes:
            if b.fresh:
                pool_ = self.legacy_ps if b.psum else self.legacy
                for sem, (cnt, ep) in pool_.items():
                    self._wait_tok(ename, (sem, cnt, ep))
                b.fresh = False
            if b.w is not None:
                self._wait_tok(ename, b.w, force=b.small)
            for sem, (cnt, ep) in b.r.items():
                self._wait_tok(ename, (sem, cnt, ep), force=b.small)

    def _post(self, tok, reads, writes):
        sem, cnt, ep = tok
        for b in reads:
            cur = b.r.get(sem)
            if cur is None or (cur[1] or 0, cur[0]) < ((ep or 0), cnt):
                b.r[sem] = (cnt, ep)
        for b in writes:
            b.w = tok
            b.r = {}

    def emit(self, ename, fn, reads=(), writes=(), sreads=(), low_prec=False):
        self._pre(ename, reads, writes, sreads)
        if low_prec:
            with self.nc.allow_low_precision(reason="rounded normalization factor"):
                instr = fn(self.eng[ename])
        else:
            instr = fn(self.eng[ename])
        self.cnt[ename] += 1
        instr.then_inc(self.h[ename], 1)
        tok = (ename, self.cnt[ename], self.epoch)
        self._post(tok, list(reads) + list(sreads), writes)
        return tok

    def dma(self, chan, ename, out, in_, reads=(), writes=()):
        cb = self.chan_buf[chan]
        self._pre(ename, reads, list(writes) + [cb])
        instr = self.eng[ename].dma_start(out=out, in_=in_)
        self.cnt[chan] += 16
        instr.then_inc(self.h[chan], 16)
        tok = (chan, self.cnt[chan], self.epoch)
        self._post(tok, reads, list(writes) + [cb])
        return tok

    def retire(self, *bufs):
        for b in bufs:
            pool_ = self.legacy_ps if b.psum else self.legacy
            def fold(sem, cnt, ep, pool_=pool_):
                cur = pool_.get(sem)
                if cur is None or ((ep or 0), cnt) > ((cur[1] or 0), cur[0]):
                    pool_[sem] = (cnt, ep)
            if b.w is not None:
                fold(*b.w)
            for sem, (cnt, ep) in b.r.items():
                fold(sem, cnt, ep)


def build_program(layers=(0, 1), nseq=NSEQ, taps=(),
                  phases=("ln", "attn", "ffn", "gate", "l1attn", "moe"),
                  loop=False, loop_stats=None):
    _buf_seq[0] = 0
    _buf_reg.clear()
    _uid[0] = 0
    nc = bass.Bass("TRN2", target_bir_lowering=False)
    dt = nc.dram_tensor
    xin = dt("x", [nseq, D, S], F32, kind="ExternalInput").ap()
    out = dt("out", [nseq, D, S], F32, kind="ExternalOutput").ap()
    ins = {}

    def din(name, shape):
        ins[name] = dt(name, shape, F32, kind="ExternalInput").ap()
        return ins[name]

    if 0 in layers:
        din("wqp", [8, 128, 1024]); din("wkp", [8, 128, 1024]); din("wvp", [8, 128, 1024])
        din("wo", [8, 128, 1024])
        din("w1f", [32, 128, 1024]); din("w2f", [8, 128, 4096])
        din("b1c", [128, 32]); din("b2c", [128, 8])
        din("thc", [128, 32]); din("wrc", [128, 32]); din("wic", [128, 32])
    if 1 in layers:
        din("wq1", [8, 128, 128]); din("wk1", [8, 128, 128]); din("wv1", [8, 128, 128])
        din("w11", [8, 128, 1024]); din("w21", [8, 128, 1024])
        din("gnw", [128, 8]); din("gnb", [128, 8])
        din("gw", [8, 128, 8]); din("gb", [1, 8])
        din("mw1", [8, 32, 128, 1024]); din("mw2", [8, 8, 128, 4096])
        din("mb1", [8, 128, 32]); din("mb2", [8, 128, 8])
    tapd = {}
    for tname in taps:
        tapd[tname] = dt("tap_" + tname, [NCH, 128, S], F32, kind="ExternalOutput").ap()

    es = contextlib.ExitStack()
    with es:
        k = K(nc, es)
        es.enter_context(nc.Block())

        def sbg(shape, dtype=F32, st=es):
            _uid[0] += 1
            return st.enter_context(nc.sbuf_tensor(f"t{_uid[0]}", shape, dtype))

        def psg(shape, st, dtype=F32):
            _uid[0] += 1
            return st.enter_context(nc.psum_tensor(f"p{_uid[0]}", shape, dtype))

        B = Buf

        X = [sbg([128, S]) for _ in range(NCH)]
        Xb = [B() for _ in range(NCH)]
        XN = [sbg([128, S], BF16) for _ in range(NCH)]
        XNb = [B() for _ in range(NCH)]
        ones_f = sbg([128, 1]); ones_f_b = B()
        ones1_f = sbg([1, 128]); ones1_f_b = B()
        onesH0 = sbg([128, 1], BF16); onesH0_b = B()
        onesH1 = sbg([128, 1], BF16); onesH1_b = B()
        ones1_b16 = sbg([1, 128], BF16); ones1_b16_b = B()
        c01 = sbg([128, 1]); c01_b = B()
        c125 = sbg([128, 1]); c125_b = B()
        ceps = sbg([128, 1]); ceps_b = B()

        k.emit("dve", lambda e: e.memset(ones_f[:], 1.0), writes=[ones_f_b])
        k.emit("dve", lambda e: e.memset(ones1_f[:], 1.0), writes=[ones1_f_b])
        k.emit("dve", lambda e: e.memset(onesH0[0:64, :], 1.0), writes=[onesH0_b])
        k.emit("dve", lambda e: e.memset(onesH0[64:128, :], 0.0), writes=[onesH0_b])
        k.emit("dve", lambda e: e.memset(onesH1[0:64, :], 0.0), writes=[onesH1_b])
        k.emit("dve", lambda e: e.memset(onesH1[64:128, :], 1.0), writes=[onesH1_b])
        k.emit("dve", lambda e: e.memset(ones1_b16[:], 1.0), writes=[ones1_b16_b])
        k.emit("dve", lambda e: e.memset(c01[:], 0.1), writes=[c01_b])
        k.emit("dve", lambda e: e.memset(c125[:], 0.125), writes=[c125_b])
        k.emit("dve", lambda e: e.memset(ceps[:], EPS), writes=[ceps_b])

        def explode(srct, srcb, K, st=es):
            tiles = [sbg([128, 1], st=st) for _ in range(K)]
            bufs = [B(small=True) for _ in range(K)]
            for j in range(K):
                k.emit("dve", lambda e, j=j: e.tensor_copy(tiles[j][:], srct[:, j:j + 1]),
                       reads=[srcb], writes=[bufs[j]])
            return tiles, bufs

        if 0 in layers:
            b1c_s = sbg([128, 32]); b1c_b = B()
            b2c_s = sbg([128, 8]); b2c_b = B()
            Acol = sbg([128, 32]); Acol_b = B(small=True)
            Bcol = sbg([128, 32]); Bcol_b = B(small=True)
            BSIN = sbg([128, 32]); BSIN_b = B(small=True)
            BCOS = sbg([128, 32]); BCOS_b = B(small=True)
            with contextlib.ExitStack() as qs:
                thc_s = sbg([128, 32], st=qs); thc_b = B()
                wrc_s = sbg([128, 32], st=qs); wrc_b = B()
                wic_s = sbg([128, 32], st=qs); wic_b = B()
                t0 = sbg([128, 32], st=qs); t0b = B(small=True)
                t1 = sbg([128, 32], st=qs); t1b = B(small=True)
                t2 = sbg([128, 32], st=qs); t2b = B(small=True)
                ti = sbg([128, 32], I32, st=qs); tib = B(small=True)
                sinth = sbg([128, 32], st=qs); sinth_b = B(small=True)
                costh = sbg([128, 32], st=qs); costh_b = B(small=True)
                cpi2 = sbg([128, 1], st=qs); cpi2_b = B(small=True)
                cm1 = sbg([128, 1], st=qs); cm1_b = B(small=True)
                k.dma("cv", "sp", b1c_s[:], ins["b1c"], writes=[b1c_b])
                k.dma("cv", "sp", thc_s[:], ins["thc"], writes=[thc_b])
                k.dma("cv", "sp", wrc_s[:], ins["wrc"], writes=[wrc_b])
                k.dma("cv", "sp", wic_s[:], ins["wic"], writes=[wic_b])
                k.dma("cv", "sp", b2c_s[:], ins["b2c"], writes=[b2c_b])
                k.emit("dve", lambda e: e.tensor_scalar_mul(t0[:], thc_s[:], float(1.0 / (2 * PI))),
                       reads=[thc_b], writes=[t0b])
                k.emit("dve", lambda e: e.tensor_copy(ti[:], t0[:]), reads=[t0b], writes=[tib])
                k.emit("dve", lambda e: e.tensor_copy(t0[:], ti[:]), reads=[tib], writes=[t0b])
                k.emit("dve", lambda e: e.scalar_tensor_tensor(
                    t1[:], t0[:], float(-2 * PI), thc_s[:], ALU.mult, ALU.add),
                    reads=[t0b, thc_b], writes=[t1b])
                k.emit("dve", lambda e: e.tensor_scalar(t0[:], t1[:], PI, None, ALU.is_gt),
                       reads=[t1b], writes=[t0b])
                k.emit("dve", lambda e: e.scalar_tensor_tensor(
                    t1[:], t0[:], float(-2 * PI), t1[:], ALU.mult, ALU.add),
                    reads=[t0b, t1b], writes=[t1b])
                k.emit("dve", lambda e: e.tensor_scalar(t0[:], t1[:], -PI, None, ALU.is_lt),
                       reads=[t1b], writes=[t0b])
                k.emit("dve", lambda e: e.scalar_tensor_tensor(
                    t1[:], t0[:], float(2 * PI), t1[:], ALU.mult, ALU.add),
                    reads=[t0b, t1b], writes=[t1b])
                k.emit("act", lambda e: e.activation(sinth[:], t1[:], AF.Sin),
                       reads=[t1b], writes=[sinth_b])
                k.emit("act", lambda e: e.activation(t2[:], t1[:], AF.Abs),
                       reads=[t1b], writes=[t2b])
                k.emit("dve", lambda e: e.memset(cpi2[:], PI / 2), writes=[cpi2_b])
                k.emit("dve", lambda e: e.memset(cm1[:], -1.0), writes=[cm1_b])
                k.emit("act", lambda e: e.activation(costh[:], t2[:], AF.Sin, bias=cpi2[:], scale=cm1[:]),
                       reads=[t2b], sreads=[cpi2_b, cm1_b], writes=[costh_b])
                k.emit("dve", lambda e: e.tensor_mul(t0[:], costh[:], wrc_s[:]),
                       reads=[costh_b, wrc_b], writes=[t0b])
                k.emit("dve", lambda e: e.tensor_mul(t2[:], sinth[:], wic_s[:]),
                       reads=[sinth_b, wic_b], writes=[t2b])
                k.emit("dve", lambda e: e.tensor_add(t0[:], t0[:], t2[:]),
                       reads=[t0b, t2b], writes=[t0b])
                k.emit("dve", lambda e: e.tensor_mul(t0[:], t0[:], wic_s[:]),
                       reads=[t0b, wic_b], writes=[t0b])
                k.emit("dve", lambda e: e.tensor_scalar_mul(t0[:], t0[:], 0.1),
                       reads=[t0b], writes=[t0b])
                k.emit("dve", lambda e: e.tensor_mul(Acol[:], t0[:], sinth[:]),
                       reads=[t0b, sinth_b], writes=[Acol_b])
                k.emit("dve", lambda e: e.tensor_mul(Bcol[:], t0[:], costh[:]),
                       reads=[t0b, costh_b], writes=[Bcol_b])
                k.emit("dve", lambda e: e.tensor_scalar_mul(BSIN[:], b1c_s[:], 0.1),
                       reads=[b1c_b], writes=[BSIN_b])
                k.emit("dve", lambda e: e.tensor_scalar(BCOS[:], b1c_s[:], 0.1, PI / 2, ALU.mult, ALU.add),
                       reads=[b1c_b], writes=[BCOS_b])
                k.retire(thc_b, wrc_b, wic_b, t0b, t1b, t2b, tib, sinth_b, costh_b, cpi2_b, cm1_b)
            B1Cc, B1Cc_b = explode(b1c_s, b1c_b, 32)
            if "qc2" in tapd:
                k.dma("ct", "pool", tapd["qc2"][0][:, 0:32], Acol[:], reads=[Acol_b])
            B2Cc, B2Cc_b = explode(b2c_s, b2c_b, 8)
            Ac, Ac_b = explode(Acol, Acol_b, 32)
            Bc, Bc_b = explode(Bcol, Bcol_b, 32)
            BSINc, BSINc_b = explode(BSIN, BSIN_b, 32)
            BCOSc, BCOSc_b = explode(BCOS, BCOS_b, 32)
            if "qc2" in tapd:
                with nc.allow_non_contiguous_dma(reason="debug tap"):
                    for j in range(8):
                        k.dma("ct", "pool", tapd["qc2"][1][:, j:j+1], Ac[j][:], reads=[Ac_b[j]])
                        k.dma("ct", "pool", tapd["qc2"][2][:, j:j+1], BSINc[j][:], reads=[BSINc_b[j]])

        if 1 in layers:
            gnw_s = sbg([128, 8]); gnw_b = B()
            gnb_s = sbg([128, 8]); gnb_b = B()
            mb1_s = [sbg([128, 32]) for _ in range(NE)]
            mb1_b = [B() for _ in range(NE)]
            mb2_s = [sbg([128, 8]) for _ in range(NE)]
            mb2_b = [B() for _ in range(NE)]
            IDF = sbg([128, 128]); IDF_b = B()
            TRIB = sbg([128, 128], BF16); TRIB_b = B()
            SEL = sbg([8, 1024], BF16); SEL_b = B()
            GNEG = [sbg([128, 1]) for _ in range(H1)]
            GNEG_b = [B(small=True) for _ in range(H1)]
            GBS = sbg([128, 8]); GBS_b = B()
            combT = sbg([8, S], BF16); combTb = B()
            CMBALL = sbg([128, 128]); CMBALLb = B()
            k.dma("cv", "sp", gnw_s[:], ins["gnw"], writes=[gnw_b])
            k.dma("cv", "sp", gnb_s[:], ins["gnb"], writes=[gnb_b])
            for e_ in range(NE):
                k.dma("cv", "sp", mb1_s[e_][:], ins["mb1"][e_], writes=[mb1_b[e_]])
                k.dma("cv", "sp", mb2_s[e_][:], ins["mb2"][e_], writes=[mb2_b[e_]])
            GNWc, GNWc_b = explode(gnw_s, gnw_b, 8)
            GNBc, GNBc_b = explode(gnb_s, gnb_b, 8)
            with contextlib.ExitStack() as qs:
                gbrow = sbg([1, 8], st=qs); gbrow_b = B()
                IOTC = sbg([128, 1], st=qs); IOTC_b = B()
                tI = sbg([128, 128], st=qs); tIb = B()
                tT = sbg([128, 128], st=qs); tTb = B()
                tmpc = sbg([128, 1], st=qs); tmpc_b = B(small=True)
                k.dma("cv", "sp", gbrow[:], ins["gb"], writes=[gbrow_b])
                k.emit("pool", lambda e: e.iota(tI[:], pattern=[[1, 128]], base=0,
                                                channel_multiplier=-1,
                                                allow_small_or_imprecise_dtypes=True), writes=[tIb])
                k.emit("dve", lambda e: e.tensor_scalar(IDF[:], tI[:], 0.0, None, ALU.is_equal),
                       reads=[tIb], writes=[IDF_b])
                k.emit("dve", lambda e: e.tensor_scalar(tT[:], tI[:], 0.0, None, ALU.is_ge),
                       reads=[tIb], writes=[tTb])
                k.emit("dve", lambda e: e.tensor_copy(TRIB[:], tT[:]), reads=[tTb], writes=[TRIB_b])
                k.emit("pool", lambda e: e.iota(IOTC[:], pattern=[[1, 1]], base=0,
                                                channel_multiplier=1,
                                                allow_small_or_imprecise_dtypes=True), writes=[IOTC_b])
                SELF = sbg([8, 1024], st=qs); SELF_b = B()
                selap = SELF[:].rearrange("k (e p) -> k e p", p=128)
                k.emit("pool", lambda e: e.iota(selap, pattern=[[1, 8], [0, 128]], base=0,
                                                channel_multiplier=-1,
                                                allow_small_or_imprecise_dtypes=True), writes=[SELF_b])
                k.emit("dve", lambda e: e.tensor_scalar(SELF[:], SELF[:], 0.0, None, ALU.is_equal),
                       reads=[SELF_b], writes=[SELF_b])
                k.emit("dve", lambda e: e.tensor_copy(SEL[:], SELF[:]),
                       reads=[SELF_b], writes=[SEL_b])
                for h in range(H1):
                    lg = LGAM[h]
                    k.emit("dve", lambda e: e.memset(tmpc[:], -lg), writes=[tmpc_b])
                    k.emit("act", lambda e: e.activation(GNEG[h][:], IOTC[:], AF.Exp, scale=tmpc[:]),
                           reads=[IOTC_b], sreads=[tmpc_b], writes=[GNEG_b[h]])
                with contextlib.ExitStack() as ps:
                    P_gb = psg([128, 8], ps); P_gb_b = B(psum=True)
                    k.emit("pe", lambda e: e.matmul(P_gb[:], ones1_f[:], gbrow[:], start=True, stop=True),
                           reads=[ones1_f_b, gbrow_b], writes=[P_gb_b])
                    k.emit("act", lambda e: e.copy(GBS[:], P_gb[:]), reads=[P_gb_b], writes=[GBS_b])
                    k.retire(P_gb_b)
                k.retire(gbrow_b, IOTC_b, tIb, tTb, tmpc_b, SELF_b)

        def tap(name, src_list, src_bufs):
            if name not in tapd:
                return
            for c in range(NCH):
                k.dma("ct", "pool", tapd[name][c], src_list[c][:], reads=[src_bufs[c]])

        def emit_load_x(s):
            for c in range(NCH):
                k.dma("cx", "sp", X[c][:], xin[s][c * 128:(c + 1) * 128, :], writes=[Xb[c]])

        def emit_store_x(s):
            for c in range(NCH):
                k.dma("cs0" if c % 2 == 0 else "cs1", "sp",
                      out[s][c * 128:(c + 1) * 128, :], X[c][:], reads=[Xb[c]])

        def emit_layernorm():
            with contextlib.ExitStack() as fs:
                SQ = [sbg([128, S], st=fs) for _ in range(2)]
                SQb = [B(), B()]
                R_mu = sbg([1, S], st=fs); R_mu_b = B()
                R_m2 = sbg([1, S], st=fs); R_m2_b = B()
                R_rs = sbg([1, S], st=fs); R_rs_b = B()
                T1 = [sbg([128, S], st=fs) for _ in range(2)]
                T1b = [B(), B()]
                with contextlib.ExitStack() as ps:
                    P_s1 = psg([1, S], ps); P_s1_b = B(psum=True)
                    P_s2 = psg([1, S], ps); P_s2_b = B(psum=True)
                    for c in range(NCH):
                        q = SQ[c % 2]; qb = SQb[c % 2]
                        k.emit("act", lambda e: e.activation(q[:], X[c][:], AF.Square),
                               reads=[Xb[c]], writes=[qb])
                        for nb in range(NB):
                            k.emit("pe", lambda e: e.matmul(
                                P_s1[:, nb * 512:(nb + 1) * 512], ones_f[:],
                                X[c][:, nb * 512:(nb + 1) * 512],
                                start=(c == 0), stop=(c == NCH - 1)),
                                reads=[ones_f_b, Xb[c]], writes=[P_s1_b])
                            k.emit("pe", lambda e: e.matmul(
                                P_s2[:, nb * 512:(nb + 1) * 512], ones_f[:],
                                q[:, nb * 512:(nb + 1) * 512],
                                start=(c == 0), stop=(c == NCH - 1)),
                                reads=[ones_f_b, qb], writes=[P_s2_b])
                    k.emit("dve", lambda e: e.tensor_scalar_mul(R_mu[:], P_s1[:], 1.0 / D),
                           reads=[P_s1_b], writes=[R_mu_b])
                    k.emit("dve", lambda e: e.tensor_scalar_mul(R_m2[:], P_s2[:], 1.0 / D),
                           reads=[P_s2_b], writes=[R_m2_b])
                    k.emit("dve", lambda e: e.tensor_mul(R_rs[:], R_mu[:], R_mu[:]),
                           reads=[R_mu_b], writes=[R_rs_b])
                    k.emit("dve", lambda e: e.tensor_sub(R_m2[:], R_m2[:], R_rs[:]),
                           reads=[R_m2_b, R_rs_b], writes=[R_m2_b])
                    k.emit("act", lambda e: e.activation(R_m2[:], R_m2[:], AF.Sqrt, bias=ceps[0:1, :]),
                           reads=[R_m2_b], sreads=[ceps_b], writes=[R_m2_b])
                    k.emit("dve", lambda e: e.reciprocal(R_rs[:], R_m2[:]),
                           reads=[R_m2_b], writes=[R_rs_b])
                    k.retire(P_s1_b, P_s2_b)
                with contextlib.ExitStack() as ps:
                    P_bm = psg([128, S], ps); P_bm_b = B(psum=True)
                    P_br = psg([128, S], ps); P_br_b = B(psum=True)
                    for nb in range(NB):
                        k.emit("pe", lambda e: e.matmul(
                            P_bm[:, nb * 512:(nb + 1) * 512], ones1_f[:],
                            R_mu[0:1, nb * 512:(nb + 1) * 512], start=True, stop=True),
                            reads=[ones1_f_b, R_mu_b], writes=[P_bm_b])
                        k.emit("pe", lambda e: e.matmul(
                            P_br[:, nb * 512:(nb + 1) * 512], ones1_f[:],
                            R_rs[0:1, nb * 512:(nb + 1) * 512], start=True, stop=True),
                            reads=[ones1_f_b, R_rs_b], writes=[P_br_b])
                    for c in range(NCH):
                        t = T1[c % 2]; tb = T1b[c % 2]
                        k.emit("dve", lambda e: e.tensor_sub(t[:], X[c][:], P_bm[:]),
                               reads=[Xb[c], P_bm_b], writes=[tb])
                        k.emit("dve", lambda e: e.tensor_mul(XN[c][:], t[:], P_br[:]),
                               reads=[tb, P_br_b], writes=[XNb[c]])
                    k.retire(P_bm_b, P_br_b)
                k.retire(SQb[0], SQb[1], T1b[0], T1b[1], R_mu_b, R_m2_b, R_rs_b)

        def emit_l0_attn():
            with contextlib.ExitStack() as fs:
                WOD = [sbg([128, 1024], BF16, st=fs) for _ in range(2)]
                WODb = [B(), B()]
                O_all = [sbg([128, S], BF16, st=fs) for _ in range(NCH)]
                O_allb = [B() for _ in range(NCH)]
                WP = [[sbg([128, 1024], BF16, st=fs) for _ in range(3)] for _ in range(2)]
                WPb = [[B() for _ in range(3)] for _ in range(2)]
                QA = [sbg([65, S], BF16, st=fs) for _ in range(2)]
                QAb = [B(), B()]
                KA = [sbg([65, S], BF16, st=fs) for _ in range(2)]
                KAb = [B(), B()]
                VA = [sbg([128, 16 * 65], BF16, st=fs) for _ in range(2)]
                VAb = [B(), B()]
                SQH = sbg([128, S], BF16, st=fs); SQHb = B()
                KM = [sbg([1, 1], st=fs) for _ in range(2)]
                KMb = [B(small=True), B(small=True)]
                PTS = [sbg([128, 512], BF16, st=fs) for _ in range(2)]
                PTSb = [B(), B()]
                R_inv = sbg([1, S], BF16, st=fs); R_invb = B()
                BCS = sbg([64, S], BF16, st=fs); BCSb = B()

                def load_pair(hp):
                    i = hp % 2
                    ch = "cw0" if i == 0 else "cw1"
                    for w, nm in enumerate(["wqp", "wkp", "wvp"]):
                        k.dma(ch, "pool", WP[i][w][:], ins[nm][hp], writes=[WPb[i][w]])

                load_pair(0)
                for hp in range(H0 // 2):
                    i = hp % 2
                    if hp + 1 < H0 // 2:
                        load_pair(hp + 1)
                    with contextlib.ExitStack() as ps:
                        P_A = psg([128, S], ps); P_A_b = B(psum=True)
                        P_NR = psg([1, S], ps); P_NR_b = B(psum=True)
                        for kc in range(NCH):
                            for nb in range(NB):
                                k.emit("pe", lambda e: e.matmul(
                                    P_A[:, nb * 512:(nb + 1) * 512],
                                    WP[i][1][:, kc * 128:(kc + 1) * 128],
                                    XN[kc][:, nb * 512:(nb + 1) * 512],
                                    start=(kc == 0), stop=(kc == NCH - 1)),
                                    reads=[WPb[i][1], XNb[kc]], writes=[P_A_b])
                        k.emit("act", lambda e: e.copy(KA[0][0:64, :], P_A[0:64, :]),
                               reads=[P_A_b], writes=[KAb[0]])
                        k.emit("act", lambda e: e.copy(KA[1][0:64, :], P_A[64:128, :]),
                               reads=[P_A_b], writes=[KAb[1]])
                        k.emit("dve", lambda e: e.memset(KA[0][64:65, :], -1.0), writes=[KAb[0]])
                        k.emit("dve", lambda e: e.memset(KA[1][64:65, :], -1.0), writes=[KAb[1]])
                        k.emit("act", lambda e: e.activation(SQH[:], P_A[:], AF.Square),
                               reads=[P_A_b], writes=[SQHb])
                        for j in range(2):
                            oh, ohb = (onesH0, onesH0_b) if j == 0 else (onesH1, onesH1_b)
                            for nb in range(NB):
                                k.emit("pe", lambda e: e.matmul(
                                    P_NR[:, nb * 512:(nb + 1) * 512], oh[:],
                                    SQH[:, nb * 512:(nb + 1) * 512], start=True, stop=True),
                                    reads=[ohb, SQHb], writes=[P_NR_b])
                            k.emit("dve", lambda e: e.reduce_max(KM[j][:], P_NR[:], axis=AX.X),
                                   reads=[P_NR_b], writes=[KMb[j]])
                        for kc in range(NCH):
                            for nb in range(NB):
                                k.emit("pe", lambda e: e.matmul(
                                    P_A[:, nb * 512:(nb + 1) * 512],
                                    WP[i][0][:, kc * 128:(kc + 1) * 128],
                                    XN[kc][:, nb * 512:(nb + 1) * 512],
                                    start=(kc == 0), stop=(kc == NCH - 1)),
                                    reads=[WPb[i][0], XNb[kc]], writes=[P_A_b])
                        k.emit("act", lambda e: e.activation(QA[0][0:64, :], P_A[0:64, :],
                                                             AF.Identity, scale=c125[0:64, :]),
                               reads=[P_A_b], sreads=[c125_b], writes=[QAb[0]])
                        k.emit("act", lambda e: e.activation(QA[1][0:64, :], P_A[64:128, :],
                                                             AF.Identity, scale=c125[64:128, :]),
                               reads=[P_A_b], sreads=[c125_b], writes=[QAb[1]])
                        k.emit("act", lambda e: e.activation(SQH[:], P_A[:], AF.Square, scale=c125[:]),
                               reads=[P_A_b], sreads=[c125_b], writes=[SQHb])
                        for j in range(2):
                            oh, ohb = (onesH0, onesH0_b) if j == 0 else (onesH1, onesH1_b)
                            for nb in range(NB):
                                k.emit("pe", lambda e: e.matmul(
                                    P_NR[:, nb * 512:(nb + 1) * 512], oh[:],
                                    SQH[:, nb * 512:(nb + 1) * 512], start=True, stop=True),
                                    reads=[ohb, SQHb], writes=[P_NR_b])
                            k.emit("act", lambda e: e.activation(QA[j][64:65, :], P_NR[:],
                                                                 AF.Sqrt, scale=KM[j][0:1, :]),
                                   reads=[P_NR_b], sreads=[KMb[j]], writes=[QAb[j]])
                        k.retire(P_A_b, P_NR_b)
                    with contextlib.ExitStack() as ps:
                        P_V = psg([128, 128], ps); P_V_b = B(psum=True)
                        k.emit("dve", lambda e: e.memset(VA[0][:, 64:16 * 65:65], 1.0), writes=[VAb[0]])
                        k.emit("dve", lambda e: e.memset(VA[1][:, 64:16 * 65:65], 1.0), writes=[VAb[1]])
                        for mc in range(16):
                            for kc in range(NCH):
                                k.emit("pe", lambda e: e.matmul(
                                    P_V[:], XN[kc][:, mc * 128:(mc + 1) * 128],
                                    WP[i][2][:, kc * 128:(kc + 1) * 128],
                                    start=(kc == 0), stop=(kc == NCH - 1)),
                                    reads=[XNb[kc], WPb[i][2]], writes=[P_V_b])
                            k.emit("act", lambda e: e.copy(VA[0][:, mc * 65:mc * 65 + 64], P_V[:, 0:64]),
                                   reads=[P_V_b], writes=[VAb[0]])
                            k.emit("act", lambda e: e.copy(VA[1][:, mc * 65:mc * 65 + 64], P_V[:, 64:128]),
                                   reads=[P_V_b], writes=[VAb[1]])
                        k.retire(P_V_b)
                    for j in range(2):
                        with contextlib.ExitStack() as hs:
                            P_O = psg([65, S], hs); P_O_b = B(psum=True)
                            with contextlib.ExitStack() as ts:
                                P_T = psg([128, S], ts); P_T_b = B(psum=True)
                                for mc in range(16):
                                    for nb in range(NB):
                                        k.emit("pe", lambda e: e.matmul(
                                            P_T[:, nb * 512:(nb + 1) * 512],
                                            KA[j][:, mc * 128:(mc + 1) * 128],
                                            QA[j][:, nb * 512:(nb + 1) * 512],
                                            start=True, stop=True),
                                            reads=[KAb[j], QAb[j]], writes=[P_T_b])
                                    for nb in range(NB):
                                        pt = PTS[nb % 2]; ptb = PTSb[nb % 2]
                                        k.emit("act", lambda e: e.activation(
                                            pt[:], P_T[:, nb * 512:(nb + 1) * 512], AF.Exp),
                                            reads=[P_T_b], writes=[ptb])
                                        k.emit("pe", lambda e: e.matmul(
                                            P_O[:, nb * 512:(nb + 1) * 512],
                                            VA[j][:, mc * 65:(mc + 1) * 65], pt[:],
                                            start=(mc == 0), stop=(mc == 15)),
                                            reads=[VAb[j], ptb], writes=[P_O_b])
                                k.retire(P_T_b)
                            k.emit("dve", lambda e: e.reciprocal(R_inv[:], P_O[64:65, :]),
                                   reads=[P_O_b], writes=[R_invb], low_prec=True)
                            with contextlib.ExitStack() as ts:
                                P_Bc = psg([64, S], ts); P_B_b = B(psum=True)
                                for nb in range(NB):
                                    k.emit("pe", lambda e: e.matmul(
                                        P_Bc[:, nb * 512:(nb + 1) * 512], ones1_b16[0:1, 0:64],
                                        R_inv[0:1, nb * 512:(nb + 1) * 512], start=True, stop=True),
                                        reads=[ones1_b16_b, R_invb], writes=[P_B_b])
                                k.emit("act", lambda e: e.copy(BCS[:], P_Bc[:]),
                                       reads=[P_B_b], writes=[BCSb])
                                ro = j * 64
                                k.emit("dve", lambda e: e.tensor_mul(
                                    O_all[hp][ro:ro + 64, :], P_O[0:64, :], BCS[:]),
                                    reads=[P_O_b, BCSb], writes=[O_allb[hp]])
                                k.retire(P_B_b)
                            k.retire(P_O_b)
                with contextlib.ExitStack() as ps:
                    P_P = [psg([128, S], ps) for _ in range(2)]
                    P_P_b = [B(psum=True), B(psum=True)]
                    k.dma("cw2", "pool", WOD[0][:], ins["wo"][0], writes=[WODb[0]])
                    for dc in range(NCH):
                        i2 = dc % 2
                        if dc + 1 < NCH:
                            k.dma("cw2" if (dc + 1) % 2 == 0 else "cw3", "pool",
                                  WOD[(dc + 1) % 2][:], ins["wo"][dc + 1],
                                  writes=[WODb[(dc + 1) % 2]])
                        pp = P_P[dc % 2]; ppb = P_P_b[dc % 2]
                        for oc in range(NCH):
                            for nb in range(NB):
                                k.emit("pe", lambda e: e.matmul(
                                    pp[:, nb * 512:(nb + 1) * 512],
                                    WOD[i2][:, oc * 128:(oc + 1) * 128],
                                    O_all[oc][:, nb * 512:(nb + 1) * 512],
                                    start=(oc == 0), stop=(oc == NCH - 1)),
                                    reads=[WODb[i2], O_allb[oc]], writes=[ppb])
                        k.emit("dve", lambda e: e.tensor_add(X[dc][:], X[dc][:], pp[:]),
                               reads=[Xb[dc], ppb], writes=[Xb[dc]])
                    k.retire(P_P_b[0], P_P_b[1])
                k.retire(*WODb, *O_allb, *QAb, *KAb, *VAb, SQHb, *KMb,
                         *PTSb, R_invb, BCSb, *WPb[0], *WPb[1])

        def emit_l0_ffn(h_only=False, quantum=True, nfb=NFB, nth=2):
            with contextlib.ExitStack() as fs:
                Hsb = [sbg([128, 1024], BF16, st=fs) for _ in range(NFB)]
                Hb = [B() for _ in range(NFB)]
                WF = [sbg([128, 1024], BF16, st=fs) for _ in range(2)]
                WFb = [B(), B()]
                W2D = [sbg([128, 2048], BF16, st=fs) for _ in range(2)]
                W2Db = [B(), B()]
                SC2 = [sbg([128, 1024], BF16, st=fs) for _ in range(2)]
                SC2b = [B(), B()]
                SS2 = [sbg([128, 1024], BF16, st=fs) for _ in range(2)]
                SS2b = [B(), B()]
                UU2 = [sbg([128, 1024], st=fs) for _ in range(2)]
                UU2b = [B(), B()]
                pending_relu = [None]
                with contextlib.ExitStack() as ps:
                    P_H = [psg([128, 1024], ps) for _ in range(2)]
                    P_H_b = [B(psum=True), B(psum=True)]
                    P_Y = [psg([128, 1024], ps) for _ in range(2)]
                    P_Y_b = [B(psum=True), B(psum=True)]
                    for th in range(nth):
                        t0 = th * 1024
                        k.dma("cw0", "pool", WF[0][:], ins["w1f"][0], writes=[WFb[0]])
                        for fb in range(nfb):
                            i = fb % 2
                            if fb + 1 < nfb:
                                k.dma("cw0" if (fb + 1) % 2 == 0 else "cw1", "pool",
                                      WF[(fb + 1) % 2][:], ins["w1f"][fb + 1],
                                      writes=[WFb[(fb + 1) % 2]])
                            ph = P_H[i]; phb = P_H_b[i]
                            for kc in range(NCH):
                                for nb in range(2):
                                    k.emit("pe", lambda e: e.matmul(
                                        ph[:, nb * 512:(nb + 1) * 512],
                                        WF[i][:, kc * 128:(kc + 1) * 128],
                                        XN[kc][:, t0 + nb * 512:t0 + (nb + 1) * 512],
                                        start=(kc == 0), stop=(kc == NCH - 1)),
                                        reads=[WFb[i], XNb[kc]], writes=[phb])
                            if quantum:
                                SC = SC2[fb % 2]; SCb = SC2b[fb % 2]
                                SS = SS2[fb % 2]; SSb = SS2b[fb % 2]
                                UU = UU2[fb % 2]; UUb = UU2b[fb % 2]
                                k.emit("act", lambda e: e.activation(
                                    SC[:], ph[:], AF.Sin, bias=BCOSc[fb][:], scale=c01[:]),
                                    reads=[phb], sreads=[BCOSc_b[fb], c01_b], writes=[SCb])
                                k.emit("act", lambda e: e.activation(
                                    SS[:], ph[:], AF.Sin, bias=BSINc[fb][:], scale=c01[:]),
                                    reads=[phb], sreads=[BSINc_b[fb], c01_b], writes=[SSb])
                                k.emit("dve", lambda e: e.scalar_tensor_tensor(
                                    UU[:], SC[:], Ac[fb][:], ph[:], ALU.mult, ALU.add),
                                    reads=[SCb, phb], sreads=[Ac_b[fb]], writes=[UUb])
                                k.emit("dve", lambda e: e.scalar_tensor_tensor(
                                    UU[:], SS[:], Bc[fb][:], UU[:], ALU.mult, ALU.add),
                                    reads=[SSb, UUb], sreads=[Bc_b[fb]], writes=[UUb])
                                if pending_relu[0] is not None:
                                    pending_relu[0]()
                                def _relu(fb=fb, UU=UU, UUb=UUb):
                                    k.emit("act", lambda e: e.activation(
                                        Hsb[fb][:], UU[:], AF.Relu, bias=B1Cc[fb][:]),
                                        reads=[UUb], sreads=[B1Cc_b[fb]], writes=[Hb[fb]])
                                pending_relu[0] = _relu
                            else:
                                k.emit("act", lambda e: e.activation(
                                    Hsb[fb][:], ph[:], AF.Relu, bias=B1Cc[fb][:]),
                                    reads=[phb], sreads=[B1Cc_b[fb]], writes=[Hb[fb]])
                        if pending_relu[0] is not None:
                            pending_relu[0]()
                            pending_relu[0] = None
                        if h_only:
                            continue
                        if th == 0 and "hsb" in tapd:
                            for c8 in range(NCH):
                                k.dma("ct", "pool", tapd["hsb"][c8][:, 0:1024], Hsb[c8][:],
                                      reads=[Hb[c8]])
                        for dc in range(NCH):
                            i = dc % 2
                            k.dma("cw2", "pool", W2D[0][:], ins["w2f"][dc][:, 0:2048],
                                  writes=[W2Db[0]])
                            k.dma("cw3", "pool", W2D[1][:], ins["w2f"][dc][:, 2048:4096],
                                  writes=[W2Db[1]])
                            py = P_Y[i]; pyb = P_Y_b[i]
                            for fb in range(NFB):
                                hw = W2D[fb // 16]; hwb = W2Db[fb // 16]
                                fo = fb % 16
                                for nb in range(2):
                                    k.emit("pe", lambda e: e.matmul(
                                        py[:, nb * 512:(nb + 1) * 512],
                                        hw[:, fo * 128:(fo + 1) * 128],
                                        Hsb[fb][:, nb * 512:(nb + 1) * 512],
                                        start=(fb == 0), stop=(fb == NFB - 1)),
                                        reads=[hwb, Hb[fb]], writes=[pyb])
                            k.emit("dve", lambda e: e.scalar_tensor_tensor(
                                X[dc][:, t0:t0 + 1024], py[:], B2Cc[dc][:],
                                X[dc][:, t0:t0 + 1024], ALU.add, ALU.add),
                                reads=[pyb, Xb[dc]], sreads=[B2Cc_b[dc]], writes=[Xb[dc]])
                    k.retire(P_H_b[0], P_H_b[1], P_Y_b[0], P_Y_b[1])
                k.retire(*Hb, *WFb, *W2Db, SCb, SSb, UUb)

        def emit_gate():
            with contextlib.ExitStack() as fs:
                GWS = [sbg([128, 8], BF16, st=fs) for _ in range(NCH)]
                GWSb = [B() for _ in range(NCH)]
                for kc in range(NCH):
                    k.dma("cw4", "pool", GWS[kc][:], ins["gw"][kc], writes=[GWSb[kc]])
                GLALL = sbg([128, 128], st=fs); GLALLb = B()
                GL2 = sbg([128, 8], st=fs); GL2b = B(small=True)
                M1 = sbg([128, 8], st=fs); M1b = B(small=True)
                M2 = sbg([128, 8], st=fs); M2b = B(small=True)
                V1 = sbg([128, 1], st=fs); V1b = B(small=True)
                V2 = sbg([128, 1], st=fs); V2b = B(small=True)
                W1g = sbg([128, 1], st=fs); W1gb = B(small=True)
                W2g = sbg([128, 1], st=fs); W2gb = B(small=True)
                with contextlib.ExitStack() as ps:
                    P_G = psg([128, 8], ps); P_G_b = B(psum=True)
                    for tc in range(16):
                        for kc in range(NCH):
                            k.emit("pe", lambda e: e.matmul(
                                P_G[:], XN[kc][:, tc * 128:(tc + 1) * 128], GWS[kc][:],
                                start=(kc == 0), stop=(kc == NCH - 1)),
                                reads=[XNb[kc], GWSb[kc]], writes=[P_G_b])
                        k.emit("dve", lambda e: e.tensor_add(
                            GLALL[:, tc * 8:(tc + 1) * 8], P_G[:], GBS[:]),
                            reads=[P_G_b, GBS_b], writes=[GLALLb])
                    k.retire(P_G_b)
                for tc in range(16):
                    gl = GLALL[:, tc * 8:(tc + 1) * 8]
                    cm = CMBALL[:, tc * 8:(tc + 1) * 8]
                    k.emit("dve", lambda e: e.reduce_max(V1[:], gl, axis=AX.X),
                           reads=[GLALLb], writes=[V1b])
                    k.emit("dve", lambda e: e.tensor_scalar(M1[:], gl, V1[:], None, ALU.is_ge),
                           reads=[GLALLb], sreads=[V1b], writes=[M1b])
                    k.emit("dve", lambda e: e.scalar_tensor_tensor(
                        GL2[:], M1[:], -1e30, gl, ALU.mult, ALU.add),
                        reads=[M1b, GLALLb], writes=[GL2b])
                    k.emit("dve", lambda e: e.reduce_max(V2[:], GL2[:], axis=AX.X),
                           reads=[GL2b], writes=[V2b])
                    k.emit("dve", lambda e: e.tensor_scalar(M2[:], GL2[:], V2[:], None, ALU.is_ge),
                           reads=[GL2b], sreads=[V2b], writes=[M2b])
                    k.emit("dve", lambda e: e.tensor_sub(V1[:], V1[:], V2[:]),
                           reads=[V1b, V2b], writes=[V1b])
                    k.emit("act", lambda e: e.activation(W1g[:], V1[:], AF.Sigmoid),
                           reads=[V1b], writes=[W1gb])
                    k.emit("dve", lambda e: e.tensor_scalar(W2g[:], W1g[:], -1.0, 1.0,
                                                            ALU.mult, ALU.add),
                           reads=[W1gb], writes=[W2gb])
                    k.emit("dve", lambda e: e.tensor_scalar(cm, M1[:], W1g[:], None, ALU.mult),
                           reads=[M1b], sreads=[W1gb], writes=[CMBALLb])
                    k.emit("dve", lambda e: e.scalar_tensor_tensor(
                        cm, M2[:], W2g[:], cm, ALU.mult, ALU.add),
                        reads=[M2b, CMBALLb], sreads=[W2gb], writes=[CMBALLb])
                k.retire(*GWSb, GLALLb, GL2b, M1b, M2b, V1b, V2b, W1gb, W2gb)

        def emit_comb_transposes():
            with contextlib.ExitStack() as ps:
                P_TC = psg([8, 128], ps); P_TC_b = B(psum=True)
                for tc in range(16):
                    k.emit("pe", lambda e: e.transpose(
                        P_TC[:], CMBALL[:, tc * 8:(tc + 1) * 8], IDF[:]),
                        reads=[CMBALLb, IDF_b], writes=[P_TC_b])
                    k.emit("dve", lambda e: e.tensor_copy(combT[:, tc * 128:(tc + 1) * 128], P_TC[:]),
                           reads=[P_TC_b], writes=[combTb])
                k.retire(P_TC_b)

        def emit_l1_attn():
            with contextlib.ExitStack() as fs:
                RETN = [sbg([128, S], BF16, st=fs) for _ in range(NCH)]
                RETNb = [B() for _ in range(NCH)]
                with contextlib.ExitStack() as fa:
                    WQ1 = [sbg([128, 128], BF16, st=fa) for _ in range(2)]
                    WQ1b = [B(), B()]
                    WK1 = [sbg([128, 128], BF16, st=fa) for _ in range(2)]
                    WK1b = [B(), B()]
                    WV1 = [sbg([128, 128], BF16, st=fa) for _ in range(2)]
                    WV1b = [B(), B()]
                    QH = sbg([128, S], BF16, st=fa); QHb = B()
                    KH = sbg([128, S], BF16, st=fa); KHb = B()
                    VH = sbg([128, S], BF16, st=fa); VHb = B()
                    SDT = sbg([128, S], BF16, st=fa); SDTb = B()
                    GBR = sbg([128, S], BF16, st=fa); GBRb = B()
                    NROW = sbg([128, S], st=fa); NROWb = B()
                    GAMC = sbg([128, 1], st=fa); GAMCb = B(small=True)
                    RS = sbg([128, 1024], st=fa); RSb = B()
                    RS2 = sbg([128, 1024], st=fa); RS2b = B()
                    R_m = sbg([1, 1024], st=fa); R_mb = B()
                    R_rs = sbg([1, 1024], st=fa); R_rsb = B()
                    R_nm = sbg([1, 1024], st=fa); R_nmb = B()
                    k.emit("pool", lambda e: e.iota(NROW[:], pattern=[[1, S]], base=0,
                                                    channel_multiplier=0,
                                                    allow_small_or_imprecise_dtypes=True),
                           writes=[NROWb])

                    def load_h(h):
                        i = h % 2
                        ch = "cw0" if i == 0 else "cw1"
                        k.dma(ch, "pool", WQ1[i][:], ins["wq1"][h], writes=[WQ1b[i]])
                        k.dma(ch, "pool", WK1[i][:], ins["wk1"][h], writes=[WK1b[i]])
                        k.dma(ch, "pool", WV1[i][:], ins["wv1"][h], writes=[WV1b[i]])

                    load_h(0)
                    for h in range(H1):
                        i = h % 2
                        if h + 1 < H1:
                            load_h(h + 1)
                        lg = LGAM[h]
                        k.emit("dve", lambda e: e.memset(GAMC[:], lg), writes=[GAMCb])
                        k.emit("act", lambda e: e.activation(GBR[:], NROW[:], AF.Exp, scale=GAMC[:]),
                               reads=[NROWb], sreads=[GAMCb], writes=[GBRb])
                        with contextlib.ExitStack() as qs:
                            P_A = psg([128, S], qs); P_A_b = B(psum=True)
                            for nb in range(NB):
                                k.emit("pe", lambda e: e.matmul(
                                    P_A[:, nb * 512:(nb + 1) * 512], WQ1[i][:],
                                    XN[h][:, nb * 512:(nb + 1) * 512], start=True, stop=True),
                                    reads=[WQ1b[i], XNb[h]], writes=[P_A_b])
                            k.emit("dve", lambda e: e.tensor_mul(QH[:], P_A[:], GBR[:]),
                                   reads=[P_A_b, GBRb], writes=[QHb])
                            for nb in range(NB):
                                k.emit("pe", lambda e: e.matmul(
                                    P_A[:, nb * 512:(nb + 1) * 512], WK1[i][:],
                                    XN[h][:, nb * 512:(nb + 1) * 512], start=True, stop=True),
                                    reads=[WK1b[i], XNb[h]], writes=[P_A_b])
                            k.emit("act", lambda e: e.copy(KH[:], P_A[:]), reads=[P_A_b], writes=[KHb])
                            k.retire(P_A_b)
                        with contextlib.ExitStack() as vs:
                            P_V = psg([128, 128], vs); P_V_b = B(psum=True)
                            for mc in range(16):
                                k.emit("pe", lambda e: e.matmul(
                                    P_V[:], XN[h][:, mc * 128:(mc + 1) * 128], WV1[i][:],
                                    start=True, stop=True),
                                    reads=[XNb[h], WV1b[i]], writes=[P_V_b])
                                gm0 = float(np.exp(-lg * (mc * 128)))
                                k.emit("dve", lambda e: e.tensor_scalar(
                                    VH[:, mc * 128:(mc + 1) * 128], P_V[:], GNEG[h][:], gm0,
                                    ALU.mult, ALU.mult),
                                    reads=[P_V_b, GNEG_b[h]], writes=[VHb])
                            k.retire(P_V_b)
                        with contextlib.ExitStack() as rs_:
                            P_R = psg([128, S], rs_); P_R_b = B(psum=True)
                            with contextlib.ExitStack() as ts:
                                P_T = [psg([128, 512], ts) for _ in range(2)]
                                P_T_b = [B(psum=True), B(psum=True)]
                                for mc in range(16):
                                    m0 = mc * 128
                                    blocks = []
                                    n0 = m0
                                    while n0 < S:
                                        n1 = min(S, (n0 // 512 + 1) * 512)
                                        blocks.append((n0, n1))
                                        n0 = n1
                                    for bi, (n0, n1) in enumerate(blocks):
                                        pt = P_T[bi % 2]; ptb = P_T_b[bi % 2]
                                        k.emit("pe", lambda e: e.matmul(
                                            pt[:, 0:n1 - n0], KH[:, mc * 128:(mc + 1) * 128],
                                            QH[:, n0:n1], start=True, stop=True),
                                            reads=[KHb, QHb], writes=[ptb])
                                        k.emit("act", lambda e: e.copy(SDT[:, n0:n1], pt[:, 0:n1 - n0]),
                                               reads=[ptb], writes=[SDTb])
                                        if bi == 0:
                                            k.emit("dve", lambda e: e.tensor_mul(
                                                SDT[:, m0:m0 + 128], SDT[:, m0:m0 + 128], TRIB[:]),
                                                reads=[SDTb, TRIB_b], writes=[SDTb])
                                        for nq in range(n0, n1, 512):
                                            nq1 = min(n1, nq + 512)
                                            k.emit("pe", lambda e: e.matmul(
                                                P_R[:, nq:nq1], VH[:, mc * 128:(mc + 1) * 128],
                                                SDT[:, nq:nq1], start=(mc == 0),
                                                stop=(mc == 15)),
                                                reads=[VHb, SDTb], writes=[P_R_b])
                                k.retire(P_T_b[0], P_T_b[1])
                            for nh in range(2):
                                o0 = nh * 1024
                                k.emit("act", lambda e: e.copy(RS[:], P_R[:, o0:o0 + 1024]),
                                       reads=[P_R_b], writes=[RSb])
                                k.emit("act", lambda e: e.activation(RS2[:], P_R[:, o0:o0 + 1024],
                                                                     AF.Square),
                                       reads=[P_R_b], writes=[RS2b])
                                with contextlib.ExitStack() as gs:
                                    P_r1 = psg([1, 1024], gs); P_r1_b = B(psum=True)
                                    P_r2 = psg([1, 1024], gs); P_r2_b = B(psum=True)
                                    for nb in range(2):
                                        k.emit("pe", lambda e: e.matmul(
                                            P_r1[:, nb * 512:(nb + 1) * 512], ones_f[:],
                                            RS[:, nb * 512:(nb + 1) * 512], start=True, stop=True),
                                            reads=[ones_f_b, RSb], writes=[P_r1_b])
                                        k.emit("pe", lambda e: e.matmul(
                                            P_r2[:, nb * 512:(nb + 1) * 512], ones_f[:],
                                            RS2[:, nb * 512:(nb + 1) * 512], start=True, stop=True),
                                            reads=[ones_f_b, RS2b], writes=[P_r2_b])
                                    k.emit("dve", lambda e: e.tensor_scalar_mul(R_m[:], P_r1[:],
                                                                                1.0 / 128.0),
                                           reads=[P_r1_b], writes=[R_mb])
                                    k.emit("dve", lambda e: e.tensor_scalar_mul(R_rs[:], P_r2[:],
                                                                                1.0 / 128.0),
                                           reads=[P_r2_b], writes=[R_rsb])
                                    k.emit("dve", lambda e: e.tensor_mul(R_nm[:], R_m[:], R_m[:]),
                                           reads=[R_mb], writes=[R_nmb])
                                    k.emit("dve", lambda e: e.tensor_sub(R_rs[:], R_rs[:], R_nm[:]),
                                           reads=[R_rsb, R_nmb], writes=[R_rsb])
                                    k.emit("act", lambda e: e.activation(R_rs[:], R_rs[:], AF.Sqrt,
                                                                         bias=ceps[0:1, :]),
                                           reads=[R_rsb], sreads=[ceps_b], writes=[R_rsb])
                                    k.emit("dve", lambda e: e.reciprocal(R_rs[:], R_rs[:]),
                                           reads=[R_rsb], writes=[R_rsb])
                                    k.emit("dve", lambda e: e.tensor_mul(R_nm[:], R_m[:], R_rs[:]),
                                           reads=[R_mb, R_rsb], writes=[R_nmb])
                                    k.emit("dve", lambda e: e.tensor_scalar_mul(R_nm[:], R_nm[:], -1.0),
                                           reads=[R_nmb], writes=[R_nmb])
                                    k.retire(P_r1_b, P_r2_b)
                                with contextlib.ExitStack() as gs:
                                    P_b1 = psg([128, 1024], gs); P_b1_b = B(psum=True)
                                    P_b2g = psg([128, 1024], gs); P_b2_b = B(psum=True)
                                    for nb in range(2):
                                        k.emit("pe", lambda e: e.matmul(
                                            P_b1[:, nb * 512:(nb + 1) * 512], ones1_f[:],
                                            R_rs[0:1, nb * 512:(nb + 1) * 512], start=True, stop=True),
                                            reads=[ones1_f_b, R_rsb], writes=[P_b1_b])
                                        k.emit("pe", lambda e: e.matmul(
                                            P_b2g[:, nb * 512:(nb + 1) * 512], ones1_f[:],
                                            R_nm[0:1, nb * 512:(nb + 1) * 512], start=True, stop=True),
                                            reads=[ones1_f_b, R_nmb], writes=[P_b2_b])
                                    k.emit("dve", lambda e: e.tensor_mul(RS[:], RS[:], P_b1[:]),
                                           reads=[RSb, P_b1_b], writes=[RSb])
                                    k.emit("dve", lambda e: e.tensor_add(RS[:], RS[:], P_b2g[:]),
                                           reads=[RSb, P_b2_b], writes=[RSb])
                                    k.emit("dve", lambda e: e.tensor_scalar(
                                        RETN[h][:, o0:o0 + 1024], RS[:], GNWc[h][:],
                                        GNBc[h][:], ALU.mult, ALU.add),
                                        reads=[RSb], sreads=[GNWc_b[h], GNBc_b[h]], writes=[RETNb[h]])
                                    k.retire(P_b1_b, P_b2_b)
                            k.retire(P_R_b)
                    k.retire(*WQ1b, *WK1b, *WV1b, QHb, KHb, VHb, SDTb, GBRb, NROWb, GAMCb,
                             RSb, RS2b, R_mb, R_rsb, R_nmb)
                with contextlib.ExitStack() as fb_:
                    W11S = [sbg([128, 1024], BF16, st=fb_) for _ in range(NCH)]
                    W11b = [B() for _ in range(NCH)]
                    W21S = [sbg([128, 1024], BF16, st=fb_) for _ in range(NCH)]
                    W21b = [B() for _ in range(NCH)]
                    GS = sbg([128, S], BF16, st=fb_); GSb = B()
                    for kc in range(NCH):
                        k.dma("cw2", "pool", W11S[kc][:], ins["w11"][kc], writes=[W11b[kc]])
                        k.dma("cw3", "pool", W21S[kc][:], ins["w21"][kc], writes=[W21b[kc]])
                    with contextlib.ExitStack() as ps:
                        P_G = [psg([128, S], ps) for _ in range(2)]
                        P_G_b = [B(psum=True), B(psum=True)]
                        for dc in range(NCH):
                            pg = P_G[dc % 2]; pgb = P_G_b[dc % 2]
                            for kc in range(NCH):
                                for nb in range(NB):
                                    k.emit("pe", lambda e: e.matmul(
                                        pg[:, nb * 512:(nb + 1) * 512],
                                        W11S[kc][:, dc * 128:(dc + 1) * 128],
                                        XN[kc][:, nb * 512:(nb + 1) * 512],
                                        start=(kc == 0), stop=(kc == NCH - 1)),
                                        reads=[W11b[kc], XNb[kc]], writes=[pgb])
                            k.emit("act", lambda e: e.activation(GS[:], pg[:], AF.Silu),
                                   reads=[pgb], writes=[GSb])
                            k.emit("dve", lambda e: e.tensor_add(RETN[dc][:], GS[:], RETN[dc][:]),
                                   reads=[GSb, RETNb[dc]], writes=[RETNb[dc]])
                        k.retire(P_G_b[0], P_G_b[1])
                    with contextlib.ExitStack() as ps:
                        P_Y2 = [psg([128, S], ps) for _ in range(2)]
                        P_Y2_b = [B(psum=True), B(psum=True)]
                        for dc in range(NCH):
                            py = P_Y2[dc % 2]; pyb = P_Y2_b[dc % 2]
                            for oc in range(NCH):
                                for nb in range(NB):
                                    k.emit("pe", lambda e: e.matmul(
                                        py[:, nb * 512:(nb + 1) * 512],
                                        W21S[oc][:, dc * 128:(dc + 1) * 128],
                                        RETN[oc][:, nb * 512:(nb + 1) * 512],
                                        start=(oc == 0), stop=(oc == NCH - 1)),
                                        reads=[W21b[oc], RETNb[oc]], writes=[pyb])
                            k.emit("dve", lambda e: e.tensor_add(X[dc][:], X[dc][:], py[:]),
                                   reads=[Xb[dc], pyb], writes=[Xb[dc]])
                        k.retire(P_Y2_b[0], P_Y2_b[1])
                    k.retire(*W11b, *W21b, GSb)
                k.retire(*RETNb)

        def emit_moe():
            emit_comb_transposes()
            with contextlib.ExitStack() as fs:
                MB1c = []; MB1c_b = []
                MB2c = []; MB2c_b = []
                for e_ in range(NE):
                    t_, b_ = explode(mb1_s[e_], mb1_b[e_], 32, st=fs)
                    MB1c.append(t_); MB1c_b.append(b_)
                    t_, b_ = explode(mb2_s[e_], mb2_b[e_], 8, st=fs)
                    MB2c.append(t_); MB2c_b.append(b_)
                Hsb = [sbg([128, 1024], BF16, st=fs) for _ in range(NFB)]
                Hb = [B() for _ in range(NFB)]
                WF1 = [sbg([128, 1024], BF16, st=fs) for _ in range(2)]
                WF1b = [B(), B()]
                MW2 = [sbg([128, 2048], BF16, st=fs) for _ in range(2)]
                MW2b = [B(psum=False), B()]
                CBS = [sbg([128, 1024], BF16, st=fs) for _ in range(2)]
                CBSb = [B(), B()]
                T1 = sbg([128, 1024], st=fs); T1b = B()
                for e_ in range(NE):
                    with contextlib.ExitStack() as cs:
                        P_CB = psg([128, 1024], cs); P_CB_b = B(psum=True)
                        for th in range(2):
                            for nb in range(2):
                                k.emit("pe", lambda e: e.matmul(
                                    P_CB[:, nb * 512:(nb + 1) * 512],
                                    SEL[:, e_ * 128:(e_ + 1) * 128],
                                    combT[:, th * 1024 + nb * 512:th * 1024 + (nb + 1) * 512],
                                    start=True, stop=True),
                                    reads=[SEL_b, combTb], writes=[P_CB_b])
                            k.emit("act", lambda e: e.copy(CBS[th][:], P_CB[:]),
                                   reads=[P_CB_b], writes=[CBSb[th]])
                        k.retire(P_CB_b)
                    with contextlib.ExitStack() as ps:
                        P_H = [psg([128, 1024], ps) for _ in range(2)]
                        P_H_b = [B(psum=True), B(psum=True)]
                        P_Y = [psg([128, 1024], ps) for _ in range(2)]
                        P_Y_b = [B(psum=True), B(psum=True)]
                        for th in range(2):
                            t0 = th * 1024
                            k.dma("cw4", "pool", WF1[0][:], ins["mw1"][e_, 0], writes=[WF1b[0]])
                            for fb in range(NFB):
                                i = fb % 2
                                if fb + 1 < NFB:
                                    k.dma("cw4" if (fb + 1) % 2 == 0 else "cw5", "pool",
                                          WF1[(fb + 1) % 2][:], ins["mw1"][e_, fb + 1],
                                          writes=[WF1b[(fb + 1) % 2]])
                                ph = P_H[i]; phb = P_H_b[i]
                                for kc in range(NCH):
                                    for nb in range(2):
                                        k.emit("pe", lambda e: e.matmul(
                                            ph[:, nb * 512:(nb + 1) * 512],
                                            WF1[i][:, kc * 128:(kc + 1) * 128],
                                            XN[kc][:, t0 + nb * 512:t0 + (nb + 1) * 512],
                                            start=(kc == 0), stop=(kc == NCH - 1)),
                                            reads=[WF1b[i], XNb[kc]], writes=[phb])
                                k.emit("act", lambda e: e.activation(
                                    Hsb[fb][:], ph[:], AF.Gelu, bias=MB1c[e_][fb][:]),
                                    reads=[phb], sreads=[MB1c_b[e_][fb]], writes=[Hb[fb]])
                            for dc in range(NCH):
                                k.dma("cw2", "pool", MW2[0][:], ins["mw2"][e_, dc][:, 0:2048],
                                      writes=[MW2b[0]])
                                k.dma("cw3", "pool", MW2[1][:], ins["mw2"][e_, dc][:, 2048:4096],
                                      writes=[MW2b[1]])
                                py = P_Y[dc % 2]; pyb = P_Y_b[dc % 2]
                                for fb in range(NFB):
                                    hw = MW2[fb // 16]; hwb = MW2b[fb // 16]
                                    fo = fb % 16
                                    for nb in range(2):
                                        k.emit("pe", lambda e: e.matmul(
                                            py[:, nb * 512:(nb + 1) * 512],
                                            hw[:, fo * 128:(fo + 1) * 128],
                                            Hsb[fb][:, nb * 512:(nb + 1) * 512],
                                            start=(fb == 0), stop=(fb == NFB - 1)),
                                            reads=[hwb, Hb[fb]], writes=[pyb])
                                k.emit("dve", lambda e: e.scalar_tensor_tensor(
                                    T1[:], py[:], MB2c[e_][dc][:], CBS[th][:],
                                    ALU.add, ALU.mult),
                                    reads=[pyb, CBSb[th]], sreads=[MB2c_b[e_][dc]], writes=[T1b])
                                k.emit("dve", lambda e: e.tensor_add(
                                    X[dc][:, t0:t0 + 1024], X[dc][:, t0:t0 + 1024], T1[:]),
                                    reads=[T1b, Xb[dc]], writes=[Xb[dc]])
                        k.retire(P_H_b[0], P_H_b[1], P_Y_b[0], P_Y_b[1])
                k.retire(*Hb, *WF1b, *MW2b, *CBSb, T1b)

        def emit_body(s):
            want_taps = (not loop) and s == 0
            emit_load_x(s)
            if 0 in layers:
                if "ln" in phases:
                    emit_layernorm()
                    if want_taps:
                        tap("xn0", XN, XNb)
                if "attn" in phases:
                    emit_l0_attn()
                    if want_taps:
                        tap("att0", X, Xb)
                if "ffnH" in phases:
                    emit_l0_ffn(h_only=True)
                if "ffnH4" in phases:
                    emit_l0_ffn(h_only=True, nfb=4, nth=1)
                if "ffnH4p" in phases:
                    emit_l0_ffn(h_only=True, quantum=False, nfb=4, nth=1)
                if "ffnH1p" in phases:
                    emit_l0_ffn(h_only=True, quantum=False, nfb=1, nth=1)
                if "ffnP" in phases:
                    emit_l0_ffn(quantum=False)
                if "ffnH1q" in phases:
                    emit_l0_ffn(h_only=True, quantum=True, nfb=1, nth=1)
                if "ffnH2q" in phases:
                    emit_l0_ffn(h_only=True, quantum=True, nfb=2, nth=1)
                if "ffn" in phases:
                    emit_l0_ffn()
                    if want_taps:
                        tap("x1", X, Xb)
            if 1 in layers:
                if "ln" in phases:
                    emit_layernorm()
                    if want_taps:
                        tap("xn1", XN, XNb)
                if "gate" in phases:
                    emit_gate()
                if "l1attn" in phases:
                    emit_l1_attn()
                    if want_taps:
                        tap("att1", X, Xb)
                if "moe" in phases:
                    emit_moe()
            emit_store_x(s)

        stats_out = None
        if not loop:
            for s in range(nseq):
                emit_body(s)
            for ch in ("cs0", "cs1", "cx", "cv", "ct"):
                if k.cnt[ch]:
                    nc.sync.wait_ge(k.h[ch], k.cnt[ch])
        else:
            k.begin_loop(loop_stats)
            if loop_stats is None:
                emit_body(0)
                stats_out = k.collect_stats(_buf_reg)
            else:
                with nc.Fori(0, nseq) as i_sym:
                    emit_body(i_sym)
                    k.end_body()
                stats_out = k.collect_stats(_buf_reg)
                for ch in DMACH:
                    d = k.delta.get(ch, 0)
                    final = k.cnt0[ch] + d * (nseq + 1)
                    if final:
                        nc.sync.wait_ge(k.h[ch], final)
    return (nc, stats_out) if loop else nc


def _prep_inputs(inp):
    f32 = np.float32
    A = lambda x: np.ascontiguousarray(np.asarray(x, f32))
    d = {}
    d["x"] = np.ascontiguousarray(A(inp["L0_attn_x"]).transpose(0, 2, 1))
    pr = lambda w: np.ascontiguousarray(
        w.reshape(8, 128, 8, 128).transpose(2, 1, 0, 3)).reshape(8, 128, 1024)
    d["wqp"] = pr(A(inp["L0_attn_w_q"]).T)
    d["wkp"] = pr(A(inp["L0_attn_w_k"]).T)
    d["wvp"] = pr(A(inp["L0_attn_w_v"]).T)
    d["wo"] = pr(A(inp["L0_attn_w_o"]).T)
    W1 = A(inp["L0_ffn_W1"]); W2 = A(inp["L0_ffn_W2"])
    d["w1f"] = np.ascontiguousarray(
        W1.reshape(8, 128, 32, 128).transpose(2, 1, 0, 3)).reshape(32, 128, 1024)
    d["w2f"] = np.ascontiguousarray(
        W2.reshape(32, 128, 8, 128).transpose(2, 1, 0, 3)).reshape(8, 128, 4096)
    d["b1c"] = np.ascontiguousarray(A(inp["L0_ffn_b1"]).reshape(32, 128).T)
    d["b2c"] = np.ascontiguousarray(A(inp["L0_ffn_b2"]).reshape(8, 128).T)
    d["thc"] = np.ascontiguousarray(A(inp["L0_ffn_quantum_phases"]).reshape(32, 128).T)
    d["wrc"] = np.ascontiguousarray(A(inp["L0_ffn_quantum_weights_real"]).reshape(32, 128).T)
    d["wic"] = np.ascontiguousarray(A(inp["L0_ffn_quantum_weights_imag"]).reshape(32, 128).T)
    d["wq1"] = A(inp["L1_attn_Wq"]); d["wk1"] = A(inp["L1_attn_Wk"]); d["wv1"] = A(inp["L1_attn_Wv"])
    d["w11"] = np.ascontiguousarray(A(inp["L1_attn_W1"]).reshape(8, 128, 1024))
    d["w21"] = np.ascontiguousarray(A(inp["L1_attn_W2"]).reshape(8, 128, 1024))
    d["gnw"] = np.ascontiguousarray(A(inp["L1_attn_gn_w"]).reshape(8, 128).T)
    d["gnb"] = np.ascontiguousarray(A(inp["L1_attn_gn_b"]).reshape(8, 128).T)
    d["gw"] = np.ascontiguousarray(A(inp["L1_ffn_gate_w"]).reshape(8, 128, 8))
    d["gb"] = np.ascontiguousarray(A(inp["L1_ffn_gate_b"]).reshape(1, 8))
    d["mw1"] = np.ascontiguousarray(
        A(inp["L1_ffn_w1"]).reshape(8, 8, 128, 32, 128).transpose(0, 3, 2, 1, 4)).reshape(8, 32, 128, 1024)
    d["mw2"] = np.ascontiguousarray(
        A(inp["L1_ffn_w2"]).reshape(8, 32, 128, 8, 128).transpose(0, 3, 2, 1, 4)).reshape(8, 8, 128, 4096)
    d["mb1"] = np.ascontiguousarray(A(inp["L1_ffn_b1"]).reshape(8, 32, 128).transpose(0, 2, 1))
    d["mb2"] = np.ascontiguousarray(A(inp["L1_ffn_b2"]).reshape(8, 8, 128).transpose(0, 2, 1))
    return d


def kernel(**inputs):
    d = _prep_inputs(inputs)
    nc1, st1 = build_program(loop=True, loop_stats=None)
    nc, st2 = build_program(loop=True, loop_stats=st1)
    if st2["wait_pairs"] - st1["wait_pairs"]:
        nc, _ = build_program(loop=True, loop_stats=st2)
    wkeys = [k_ for k_ in d if k_ != "x"]
    in_maps = []
    for c in range(8):
        m = {"x": np.ascontiguousarray(d["x"][c * NSEQ:(c + 1) * NSEQ])}
        for k_ in wkeys:
            m[k_] = d[k_]
        in_maps.append(m)
    res = run_bass_kernel_spmd(nc, in_maps, core_ids=list(range(8)), trace_cores=[0])
    outs = [np.asarray(r["out"]) for r in res.results]
    full = np.concatenate(outs, axis=0)
    return np.ascontiguousarray(full.transpose(0, 2, 1))
